# version 06 (final)
# speedup vs baseline: 1.1891x; 1.0536x over previous
import math
from contextlib import ExitStack

import numpy as np
import concourse.bass as bass
import concourse.mybir as mybir
from concourse.bass_utils import run_bass_kernel_spmd

F32 = mybir.dt.float32
BF16 = mybir.dt.bfloat16
U8 = mybir.dt.uint8
I32 = mybir.dt.int32
AF = mybir.ActivationFunctionType
ALU = mybir.AluOpType
AX = mybir.AxisListType

ENGINES = ("tensor", "vector", "scalar", "gpsimd", "sync")

B_ = 4
S_ = 4096
TL = 2048
DM = 1024
ND = 8


class _Node:
    __slots__ = ("eng", "idx", "fn", "deps", "kind", "dma_idx", "marked",
                 "cum", "tag")

    def __init__(self, eng, idx, fn, kind, dma_idx=-1, tag=""):
        self.eng = eng
        self.idx = idx
        self.fn = fn
        self.kind = kind
        self.dma_idx = dma_idx
        self.deps = []
        self.marked = False
        self.cum = 0
        self.tag = tag


class Tracker:
    def __init__(self):
        self.nodes = {e: [] for e in ENGINES}
        self.dma_count = {e: 0 for e in ENGINES}
        self.res = {}

    STRICT_SAME_ENGINE = True

    def _dep_ok(self, src, dst):
        if self.STRICT_SAME_ENGINE and src.kind == "op" and dst.kind == "op" \
                and src.eng == dst.eng and src.eng in ("vector", "scalar", "gpsimd"):
            return True
        return not (src.kind == "op" and dst.kind in ("op", "dma")
                    and src.eng == dst.eng)

    def _record(self, node, reads, writes):
        deps = set()
        for r in reads:
            e = self.res.get(r)
            if e and e[0] is not None and self._dep_ok(e[0], node):
                deps.add(e[0])
        for w in writes:
            e = self.res.get(w)
            if e:
                if e[0] is not None and self._dep_ok(e[0], node):
                    deps.add(e[0])
                for rd in e[1]:
                    if self._dep_ok(rd, node):
                        deps.add(rd)
        node.deps = list(deps)
        for r in reads:
            self.res.setdefault(r, [None, []])[1].append(node)
        for w in writes:
            self.res[w] = [node, []]
        self.nodes[node.eng].append(node)

    def op(self, eng, fn, reads=(), writes=(), tag=""):
        node = _Node(eng, len(self.nodes[eng]), fn, "op", tag=tag)
        self._record(node, reads, writes)
        return node

    def fence(self, fn, tag="fence"):
        node = _Node("vector", len(self.nodes["vector"]), fn, "op", tag=tag)
        deps = set()
        for e in self.res.values():
            if e[0] is not None and self._dep_ok(e[0], node):
                deps.add(e[0])
            for rd in e[1]:
                if self._dep_ok(rd, node):
                    deps.add(rd)
        node.deps = list(deps)
        for k in self.res:
            self.res[k] = [node, []]
        self.nodes["vector"].append(node)
        return node

    def dma(self, eng, fn, reads=(), writes=(), tag=""):
        node = _Node(eng, len(self.nodes[eng]), fn, "dma",
                     dma_idx=self.dma_count[eng], tag=tag)
        self.dma_count[eng] += 1
        self._record(node, reads, writes)
        return node

    def finalize(self, nc, block):
        for e in ENGINES:
            for n in self.nodes[e]:
                for d in n.deps:
                    d.marked = True
        NLANES = {"sync": 8, "gpsimd": 8, "scalar": 2, "vector": 2, "tensor": 2}
        sems = {}
        dsems = {}
        es = ExitStack()
        for e in ENGINES:
            if any(n.kind == "op" and n.marked for n in self.nodes[e]):
                sems[e] = es.enter_context(nc.semaphore(f"s_{e}"))
            if self.dma_count[e]:
                k = min(NLANES[e], self.dma_count[e])
                dsems[e] = [es.enter_context(nc.semaphore(f"d_{e}{j}"))
                            for j in range(k)]
        for e in ENGINES:
            c = 0
            for n in self.nodes[e]:
                if n.kind == "op" and n.marked:
                    c += 1
                    n.cum = c

        def dma_wait_of(d):
            k = len(dsems[d.eng])
            lane, cyc = d.dma_idx % k, d.dma_idx // k
            return dsems[d.eng][lane], 16 * (cyc + 1)

        def emit(eng_obj, ename):
            hw = {}

            def wait_for_sem(sem, val):
                if hw.get(sem.name, -1) < val:
                    eng_obj.wait_ge(sem, val)
                    hw[sem.name] = val

            def wait_for(d):
                if d.kind == "dma":
                    sem, val = dma_wait_of(d)
                else:
                    sem, val = sems[d.eng], d.cum
                wait_for_sem(sem, val)

            for n in self.nodes[ename]:
                for d in sorted(n.deps, key=lambda z: (z.eng, z.cum, z.dma_idx)):
                    wait_for(d)
                if n.kind == "dma":
                    k = len(dsems[ename])
                    lane, cyc = n.dma_idx % k, n.dma_idx // k
                    if cyc > 0:
                        wait_for_sem(dsems[ename][lane], 16 * cyc)
                inst = n.fn(eng_obj)
                if n.kind == "dma":
                    inst.then_inc(dsems[ename][lane], 16)
                elif n.marked:
                    inst.then_inc(sems[ename], 1)

        def final_waits(sync_obj):
            for e in ENGINES:
                if self.dma_count[e]:
                    k = len(dsems[e])
                    for j in range(k):
                        tot = (self.dma_count[e] - j + k - 1) // k
                        if tot:
                            sync_obj.wait_ge(dsems[e][j], 16 * tot)
            for e in ENGINES:
                ns = [n for n in self.nodes[e] if n.kind == "op" and n.marked]
                if ns and e != "sync":
                    sync_obj.wait_ge(sems[e], ns[-1].cum)

        @block.tensor
        def _(t):
            emit(t, "tensor")

        @block.vector
        def _(v):
            emit(v, "vector")

        @block.scalar
        def _(s):
            emit(s, "scalar")

        @block.gpsimd
        def _(g):
            emit(g, "gpsimd")

        @block.sync
        def _(sy):
            emit(sy, "sync")
            final_waits(sy)

        return es


class Builder:
    def __init__(self, name, race_detect=True):
        self.nc = bass.Bass("TRN2", target_bir_lowering=False,
                            detect_race_conditions=race_detect)
        self.T = Tracker()
        self.es = ExitStack()
        self.name = name
        self.sbuf_bytes = 0
        self._uid = 0
        self._const_cols = {}
        self._const_rows = {}
        self.pb = [self.es.enter_context(
            self.nc.psum_tensor(f"pb{i}", [128, 512], F32)) for i in range(8)]
        self.inp = {}
        self.out = {}

    def uid(self):
        self._uid += 1
        return self._uid

    def shared_lnrows(self):
        if not hasattr(self, "_slnr"):
            self._slnr = [self.sb(f"sh_lr{j}", [1, 512], F32) for j in range(4)]
        return self._slnr

    def shared_abrow(self):
        if not hasattr(self, "_sabr"):
            self._sabr = self.sb("sh_abr", [1, 1024], F32)
        return self._sabr

    def colset(self, name, hbm_ap, n, keyname):
        cols = [self.sb(f"{name}_{j}", [128, 1], F32) for j in range(n)]
        for j in range(n):
            self.load(cols[j][:], hbm_ap[j:j + 1, :], writes=[(keyname,)],
                      tag=f"ld_{name}_{j}")
        return cols

    def shared_sq(self):
        if not hasattr(self, "_ssq"):
            self._ssq = self.sb("sh_sq", [128, ND, 512], F32)
        return self._ssq

    def shared_xt(self):
        if not hasattr(self, "_sxt"):
            self._sxt = [self.sb(f"sh_xt{i}", [128, ND, 512], F32) for i in range(2)]
        return self._sxt

    def sb(self, name, shape, dtype):
        n = int(np.prod(shape[1:])) * {F32: 4, BF16: 2, U8: 1, I32: 4}[dtype]
        self.sbuf_bytes += n * 128
        assert self.sbuf_bytes < 28 * 2 ** 20 - 2 ** 19, \
            f"sbuf budget blown at {name}: {self.sbuf_bytes/2**20:.1f} MiB"
        return self.es.enter_context(self.nc.sbuf_tensor(name, list(shape), dtype))

    def dram_in(self, name, shape, dtype):
        t = self.nc.dram_tensor(name, list(shape), dtype, kind="ExternalInput")
        self.inp[name] = t
        ap = t.ap()
        if not hasattr(self, "inp_ap"):
            self.inp_ap = {}
        self.inp_ap[name] = ap
        return ap

    def dram_out(self, name, shape, dtype):
        t = self.nc.dram_tensor(name, list(shape), dtype, kind="ExternalOutput")
        self.out[name] = t
        return t.ap()

    def dram_tmp(self, name, shape, dtype):
        return self.nc.dram_tensor(name, list(shape), dtype, kind="Internal").ap()

    def constcol(self, val, dtype=F32):
        key = (float(val), dtype)
        if key not in self._const_cols:
            t = self.sb(f"cc{self.uid()}", [128, 1], dtype)
            self.T.op("vector", lambda v, t=t, val=val: v.memset(t[:], val),
                      writes=[("cc", key)], tag="constcol")
            self._const_cols[key] = t
        return self._const_cols[key]

    def constrow(self, val, width, dtype=F32, parts=1):
        key = (float(val), width, dtype, parts)
        if key not in self._const_rows:
            t = self.sb(f"cr{self.uid()}", [parts, width], dtype)
            self.T.op("vector", lambda v, t=t, val=val: v.memset(t[:], val),
                      writes=[("cr", key)], tag="constrow")
            self._const_rows[key] = t
        return self._const_rows[key]

    def kc(self, val, dtype=F32):
        return ("cc", (float(val), dtype))

    def kr(self, val, width, dtype=F32, parts=1):
        return ("cr", (float(val), width, dtype, parts))

    def load(self, dst_ap, src_ap, reads=(), writes=(), eng="sync", tag="load"):
        self.T.dma(eng, lambda q, d=dst_ap, s=src_ap: q.dma_start(out=d, in_=s),
                   reads=reads, writes=writes, tag=tag)

    def store(self, dst_ap, src_ap, reads=(), writes=(), eng="sync", tag="store"):
        self.T.dma(eng, lambda q, d=dst_ap, s=src_ap: q.dma_start(out=d, in_=s),
                   reads=reads, writes=writes, tag=tag)

    def fence(self, tag="fence"):
        if not hasattr(self, "_fsc"):
            self._fsc = self.sb("fsc", [1, 1], F32)
        self.T.fence(lambda v, t=self._fsc: v.memset(t[:], 0.0), tag=tag)

    def finish(self):
        with self.nc.Block() as block:
            self._semes = self.T.finalize(self.nc, block)
        return self.nc


def bcast_row(b, psum_idx, row_ap, row_keys, width, tag="bcast"):
    ones = b.constrow(1.0, 128, F32)
    pk = ("psum", psum_idx)
    b.T.op("tensor",
           lambda t, p=b.pb[psum_idx], o=ones, r=row_ap, w=width:
           t.matmul(p[:, 0:w], o[0:1, :], r, start=True, stop=True),
           reads=[b.kr(1.0, 128, F32)] + list(row_keys), writes=[pk], tag=tag)
    return b.pb[psum_idx][:, 0:width], pk


def emit_ln(b, nm, x_src, T, eps=1e-5, out=None, gamma_beta=None, kind="ln",
            out_hbm=None, out_dtype=BF16, ab_slot=0):
    nc, Tk = b.nc, b.T
    CH = 512
    nch = T // CH
    from_hbm = x_src[0] == "hbm"
    if from_hbm:
        _, x_hbm, x_hbm_keys = x_src
        xt = b.shared_xt()

        def xtile(c, d):
            return xt[c % 2][:, d, :]

        def xkeys(c, d):
            return [("xts", c % 2)]
    else:
        _, xsb, xkeyf = x_src

        def xtile(c, d):
            return xsb[:, d, c * CH:(c + 1) * CH]

        def xkeys(c, d):
            return [xkeyf(c, d)]
    sq = b.shared_sq()
    rws = b.shared_lnrows()
    r_sum, r_sq, r_t, r_b = rws
    ab_t = b.dram_tmp(f"{nm}_ab", [2, T], F32)
    abk = (nm, "ab")
    ones_col = b.constrow(1.0, 1, F32, parts=128)
    okey = b.kr(1.0, 1, F32, 128)
    epsc = b.constcol(eps)
    inv_d = 1.0 / DM
    for c in range(nch):
        if from_hbm:
            b.load(xt[c % 2][:], x_hbm[:, :, c * CH:(c + 1) * CH],
                   reads=list(x_hbm_keys), writes=[("xts", c % 2)],
                   tag=f"{nm}_ldx{c}")
        sqk = ("sqs",)
        for d in range(ND):
            Tk.op("scalar", lambda s, c=c, d=d: s.activation(
                sq[:, d, :], xtile(c, d), AF.Square),
                reads=xkeys(c, d), writes=[sqk], tag=f"{nm}_sq{c}_{d}")
        if kind == "ln":
            for d in range(ND):
                Tk.op("tensor", lambda t, c=c, d=d: t.matmul(
                    b.pb[6][0:1, 0:CH], ones_col[:, 0:1], xtile(c, d),
                    start=(d == 0), stop=(d == ND - 1)),
                    reads=xkeys(c, d) + [okey], writes=[("psum", 6)],
                    tag=f"{nm}_ms{c}")
        for d in range(ND):
            Tk.op("tensor", lambda t, d=d: t.matmul(
                b.pb[7][0:1, 0:CH], ones_col[:, 0:1], sq[:, d, :],
                start=(d == 0), stop=(d == ND - 1)),
                reads=[sqk, okey], writes=[("psum", 7)], tag=f"{nm}_mq{c}")
        if kind == "ln":
            Tk.op("vector", lambda v: v.tensor_scalar_mul(
                r_sum[:], b.pb[6][0:1, 0:CH], inv_d),
                reads=[("psum", 6)], writes=[("lnr", 0)], tag=f"{nm}_mu{c}")
        Tk.op("vector", lambda v: v.tensor_scalar_mul(
            r_sq[:], b.pb[7][0:1, 0:CH], inv_d),
            reads=[("psum", 7)], writes=[("lnr", 1)], tag=f"{nm}_e2{c}")
        if kind == "ln":
            Tk.op("vector", lambda v: v.tensor_mul(r_t[:], r_sum[:], r_sum[:]),
                  reads=[("lnr", 0)], writes=[("lnr", 2)], tag=f"{nm}_m2{c}")
            Tk.op("vector", lambda v: v.tensor_sub(r_sq[:], r_sq[:], r_t[:]),
                  reads=[("lnr", 1), ("lnr", 2)], writes=[("lnr", 1)],
                  tag=f"{nm}_vr{c}")
            Tk.op("vector", lambda v: v.tensor_scalar_max(r_sq[:], r_sq[:], 0.0),
                  reads=[("lnr", 1)], writes=[("lnr", 1)], tag=f"{nm}_vc{c}")
        Tk.op("scalar", lambda s: s.activation(r_sq[:], r_sq[:], AF.Ln,
                                               bias=epsc[0:1, 0:1]),
              reads=[("lnr", 1), b.kc(eps)], writes=[("lnr", 1)], tag=f"{nm}_l{c}")
        Tk.op("scalar", lambda s: s.activation(r_sq[:], r_sq[:], AF.Exp, scale=-0.5),
              reads=[("lnr", 1)], writes=[("lnr", 1)], tag=f"{nm}_x{c}")
        if kind == "ln":
            Tk.op("vector", lambda v: v.tensor_mul(r_b[:], r_sum[:], r_sq[:]),
                  reads=[("lnr", 0), ("lnr", 1)], writes=[("lnr", 3)],
                  tag=f"{nm}_b{c}")
        else:
            Tk.op("vector", lambda v: v.memset(r_b[:], 0.0),
                  writes=[("lnr", 3)], tag=f"{nm}_b0{c}")
        b.store(ab_t[0:1, c * CH:(c + 1) * CH], r_sq[:], reads=[("lnr", 1)],
                writes=[abk], tag=f"{nm}_sa{c}")
        b.store(ab_t[1:2, c * CH:(c + 1) * CH], r_b[:], reads=[("lnr", 3)],
                writes=[abk], tag=f"{nm}_sb{c}")

    if out is not None or out_hbm is not None:
        if out_hbm is not None:
            if not hasattr(b, "_sost"):
                b._sost = {}
            if out_dtype not in b._sost:
                b._sost[out_dtype] = b.sb(f"sh_ost{len(b._sost)}", [128, ND, 512],
                                          out_dtype)
            osb = b._sost[out_dtype]
            okf = lambda c, d: ("osts", str(out_dtype), d)
        else:
            osb, okf = out
        tmp = sq
        for c in range(nch):
            def osl(d, c=c):
                if out_hbm is not None:
                    return osb[:, d, :]
                return osb[:, d, c * CH:(c + 1) * CH]
            if from_hbm:
                b.load(xt[c % 2][:], x_hbm[:, :, c * CH:(c + 1) * CH],
                       reads=list(x_hbm_keys), writes=[("xts", c % 2)],
                       tag=f"{nm}_ldx2_{c}")
            pA, pAk, pB, pBk = ab_bcast(b, ab_t, abk, c, CH, f"{nm}_bc{c}")
            for d in range(ND):
                tk = ("sqs",)
                Tk.op("vector", lambda v, c=c, d=d, pA=pA: v.tensor_tensor(
                    tmp[:, d, :], xtile(c, d), pA, ALU.mult),
                    reads=xkeys(c, d) + [pAk], writes=[tk], tag=f"{nm}_ap1_{c}_{d}")
                if gamma_beta is None:
                    Tk.op("vector", lambda v, d=d, c=c, pB=pB, osl=osl: v.tensor_tensor(
                        osl(d), tmp[:, d, :], pB, ALU.subtract),
                        reads=[tk, pBk], writes=[okf(c, d)], tag=f"{nm}_ap2_{c}_{d}")
                else:
                    gsb, bsb, gkeys = gamma_beta
                    Tk.op("vector", lambda v, d=d, pB=pB: v.tensor_tensor(
                        tmp[:, d, :], tmp[:, d, :], pB, ALU.subtract),
                        reads=[tk, pBk], writes=[tk], tag=f"{nm}_ap2_{c}_{d}")
                    Tk.op("vector", lambda v, d=d, c=c, osl=osl: v.tensor_scalar(
                        osl(d), tmp[:, d, :],
                        gsb[d][:, 0:1], bsb[d][:, 0:1], ALU.mult, ALU.add),
                        reads=[tk] + list(gkeys), writes=[okf(c, d)],
                        tag=f"{nm}_gb_{c}_{d}")
            if out_hbm is not None:
                b.store(out_hbm[:, :, c * CH:(c + 1) * CH], osb[:],
                        reads=[okf(c, d) for d in range(ND)], tag=f"{nm}_ost{c}")
    return ab_t, abk


def ab_bcast(b, ab_t, abk, c, CH, tag):
    rows = b.shared_abrow()
    b.load(rows[:, 0:CH], ab_t[0:1, c * CH:(c + 1) * CH], reads=[abk],
           writes=[("abr",)], tag=tag + "la")
    b.load(rows[:, CH:2 * CH], ab_t[1:2, c * CH:(c + 1) * CH], reads=[abk],
           writes=[("abr",)], tag=tag + "lb")
    pA, pAk = bcast_row(b, 4, rows[0:1, 0:CH], [("abr",)], CH, tag + "a")
    pB, pBk = bcast_row(b, 5, rows[0:1, CH:2 * CH], [("abr",)], CH, tag + "b")
    return pA, pAk, pB, pBk


def emit_qk_bound(b, tag, qT, kT, qkeys, kkeys, t64a, TQ=S_, TS=S_,
                  KP=64, out_row=None, out_key=None, tkey=("t64a",)):
    Tk = b.T
    ones_c = b.constrow(1.0, 1, BF16, parts=128)
    onesck = b.kr(1.0, 1, BF16, 128)
    if not hasattr(b, "_qkb"):
        b._qkb = b.sb("qkb", [1, 4], F32)
    if hasattr(b, "_sabr"):
        qkr, qkrk = b._sabr[0:1, 512:1024], ("abr",)
    else:
        if not hasattr(b, "_qkr"):
            b._qkr = b.sb("qkr", [1, 512], F32)
        qkr, qkrk = b._qkr[0:1, :], ("qkr",)
    qkb = b._qkb
    if out_row is None:
        Tk.op("vector", lambda v: v.memset(kT[64:65, :], 1.0),
              reads=kkeys, writes=[("kaug",)], tag=tag + "k1")
    Tk.op("vector", lambda v: v.memset(qkb[0:1, 0:1], -1e30),
          writes=[("qkb",)], tag=tag + "km0")
    kchunks = ([(i * 512, 512) for i in range(TS // 512)]
               if TS >= 512 else [(0, TS)])
    for ch, (koff, kw) in enumerate(kchunks):
        Tk.op("scalar", lambda s, koff=koff, kw=kw: s.activation(
            t64a[:, 0:kw], kT[0:KP, koff:koff + kw], AF.Square),
            reads=kkeys, writes=[tkey], tag=tag + f"ks{ch}")
        Tk.op("tensor", lambda t, kw=kw: t.matmul(
            b.pb[4][0:1, 0:kw], ones_c[0:KP, 0:1], t64a[:, 0:kw],
            start=True, stop=True),
            reads=[tkey, onesck], writes=[("psum", 4)], tag=tag + f"km{ch}")
        Tk.op("vector", lambda v, kw=kw: v.reduce_max(
            qkb[0:1, 1:2], b.pb[4][0:1, 0:kw], axis=AX.X),
            reads=[("psum", 4)], writes=[("qkb",)], tag=tag + f"kr{ch}")
        Tk.op("vector", lambda v: v.tensor_max(
            qkb[0:1, 0:1], qkb[0:1, 0:1], qkb[0:1, 1:2]),
            reads=[("qkb",)], writes=[("qkb",)], tag=tag + f"kx{ch}")
    Tk.op("scalar", lambda s: s.activation(qkb[0:1, 2:3], qkb[0:1, 0:1], AF.Ln),
          reads=[("qkb",)], writes=[("qkb",)], tag=tag + "kl")
    for ch in range(TQ // 512):
        Tk.op("scalar", lambda s, ch=ch: s.activation(
            t64a[:, 0:512], qT[0:KP, ch * 512:(ch + 1) * 512], AF.Square),
            reads=qkeys, writes=[tkey], tag=tag + f"qs{ch}")
        Tk.op("tensor", lambda t: t.matmul(
            b.pb[4][0:1, 0:512], ones_c[0:KP, 0:1], t64a[:, 0:512],
            start=True, stop=True),
            reads=[tkey, onesck], writes=[("psum", 4)], tag=tag + f"qm{ch}")
        Tk.op("scalar", lambda s: s.activation(
            qkr, b.pb[4][0:1, 0:512], AF.Ln),
            reads=[("psum", 4)], writes=[qkrk], tag=tag + f"ql{ch}")
        Tk.op("vector", lambda v: v.tensor_scalar_add(
            qkr, qkr, qkb[0:1, 2:3]),
            reads=[qkrk, ("qkb",)], writes=[qkrk], tag=tag + f"qa{ch}")
        Tk.op("scalar", lambda s: s.activation(
            qkr, qkr, AF.Exp, scale=0.5),
            reads=[qkrk], writes=[qkrk], tag=tag + f"qe{ch}")
        if out_row is None:
            Tk.op("vector", lambda v, ch=ch: v.tensor_scalar_mul(
                qT[64:65, ch * 512:(ch + 1) * 512], qkr, -1.0),
                reads=[qkrk], writes=[("qaug",)], tag=tag + f"qw{ch}")
        else:
            Tk.op("vector", lambda v, ch=ch: v.tensor_scalar_mul(
                out_row[0:1, ch * 512:(ch + 1) * 512], qkr, -1.0),
                reads=[qkrk], writes=[out_key], tag=tag + f"qw{ch}")


def emit_attn_l0(b):
    nc, Tk = b.nc, b.T
    H = 8
    HD = 64
    NS = 32
    NCH = 8

    lnf_h = b.dram_in("lnf", [128, 8, S_], BF16)
    wq_h = b.dram_in("wq", [128, 8, 512], F32)
    wk_h = b.dram_in("wk", [128, 8, 512], F32)
    wv_h = b.dram_in("wv", [128, 8, 512], F32)
    wo_h = b.dram_in("wo", [128, 4, 8, 128], F32)
    bq_h = b.dram_in("bq", [64, 8], F32)
    bk_h = b.dram_in("bk", [64, 8], F32)
    bv_h = b.dram_in("bv", [1, 512], F32)
    cos_h = b.dram_in("cosT", [128, S_], F32)
    sin_h = b.dram_in("sinT", [128, S_], F32)
    pm_h = b.dram_in("pm", [64, 64], F32)
    po_h = b.dram_out("po", [128, 8, S_], F32)

    lnf = b.sb("s_lnf", [128, ND, S_], BF16)
    vhat = b.sb("vhat", [128, NS, H, 65], BF16)
    cosb = b.sb("cosb", [128, S_], BF16)
    sinb = b.sb("sinb", [128, S_], BF16)
    wqh = b.sb("s_wqh", [128, 2, ND, 64], BF16)
    wv = b.sb("s_wv", [128, ND, 512], BF16)
    wo = b.sb("s_wo", [128, 4, 8, 128], BF16)
    bq = b.sb("s_bq", [64, H], F32)
    bk = b.sb("s_bk", [64, H], F32)
    bv = b.sb("s_bv", [1, 512], BF16)
    pm = b.sb("s_pm", [64, 64], BF16)
    del_unused = None
    kT = b.sb("kT", [65, S_], BF16)
    qT = b.sb("qT", [65, S_], BF16)
    zc0 = b.constcol(0.0)
    et = b.sb("et", [128, 2, 512], BF16)
    att = b.sb("att", [128, 4, S_], BF16)
    rz = b.sb("rz", [1, 512], F32)
    rzb = b.sb("rzb", [64, 512], F32)
    t64a = b.sb("t64a", [64, 512], BF16)
    t64b = b.sb("t64b", [64, 512], BF16)
    sc1 = b.sb("sc1", [64, 1], F32)
    sc2 = b.sb("sc2", [1, 4], F32)
    negc = b.sb("negc", [128, 1], F32)
    post = b.sb("post", [128, 2, 512], F32)

    ones_c = b.constrow(1.0, 1, F32, parts=128)
    onesck = b.kr(1.0, 1, F32, 128)
    ones_r = b.constrow(1.0, 128, F32)
    onesrk = b.kr(1.0, 128, F32)
    ones_rb = b.constrow(1.0, 128, BF16)
    onesrbk = b.kr(1.0, 128, BF16)

    b.load(lnf[:], lnf_h, writes=[("lnf", c) for c in range(NCH)], tag="ld_lnf")
    b.load(wv[:], wv_h, writes=[("wv",)], eng="gpsimd", tag="ld_wv")
    b.load(wo[:], wo_h, writes=[("wo",)], eng="gpsimd", tag="ld_wo")
    b.load(bq[:], bq_h, writes=[("bq",)], tag="ld_bq")
    b.load(bk[:], bk_h, writes=[("bk",)], tag="ld_bk")
    b.load(bv[:], bv_h, writes=[("bv",)], eng="gpsimd", tag="ld_bv")
    b.load(cosb[:], cos_h, writes=[("cos",)], eng="gpsimd", tag="ld_cos")
    b.load(sinb[:], sin_h, writes=[("sin",)], eng="gpsimd", tag="ld_sin")
    b.load(pm[:], pm_h, writes=[("pm",)], eng="gpsimd", tag="ld_pm")

    lnf_keys = [("lnf", c) for c in range(NCH)]

    Tk.op("vector", lambda v: v.memset(vhat[:, :, :, 64:65], 1.0),
          writes=[("vhat_ones",)], tag="vhat_ones")
    for st in range(NS):
        pk = ("psum", st % 2)
        for d in range(ND):
            Tk.op("tensor", lambda t, st=st, d=d: t.matmul(
                b.pb[st % 2][:, 0:512], lnf[:, d, st * 128:(st + 1) * 128],
                wv[:, d, :], start=(d == 0), stop=False),
                reads=[("lnf", st // 4), ("wv",)], writes=[pk], tag=f"vp{st}")
        Tk.op("tensor", lambda t, st=st: t.matmul(
            b.pb[st % 2][:, 0:512], ones_rb[0:1, :], bv[0:1, :],
            start=False, stop=True),
            reads=[("bv",), onesrbk], writes=[pk], tag=f"vpb{st}")
        Tk.op("scalar", lambda s, st=st: s.activation(
            vhat[:, st, :, 0:64], b.pb[st % 2][:, 0:512], AF.Copy),
            reads=[pk], writes=[("vhat", st)], tag=f"vev{st}")

    for h in range(H):
        hb, hh = h // 2, (h % 2)
        for wi, (which, dst, w_src, bsb, bkey) in enumerate((
                ("k", kT, wk_h, bk, ("bk",)),
                ("q", qT, wq_h, bq, ("bq",)))):
            par = (2 * h + wi) % 2
            wkey = ("wqh", par)
            b.load(wqh[:, par, :, :], w_src[:, :, h * 64:(h + 1) * 64],
                   reads=(), writes=[wkey], eng="gpsimd", tag=f"ldw{which}{h}")
            for ch in range(NCH):
                pk = ("psum", 3)
                for d in range(ND):
                    Tk.op("tensor", lambda t, d=d, ch=ch, par=par: t.matmul(
                        b.pb[3][0:64, 0:512], wqh[:, par, d, :],
                        lnf[:, d, ch * 512:(ch + 1) * 512],
                        start=(d == 0), stop=(d == ND - 1)),
                        reads=[("lnf", ch), wkey], writes=[pk], tag=f"{which}p{h}_{ch}")
                rawk = (which + "T", ch)
                Tk.op("scalar", lambda s, ch=ch, bsb=bsb, h=h, dst=dst: s.activation(
                    dst[0:64, ch * 512:(ch + 1) * 512], b.pb[3][0:64, 0:512],
                    AF.Identity, bias=bsb[:, h:h + 1]),
                    reads=[pk, bkey], writes=[rawk], tag=f"{which}e{h}_{ch}")
                pk2 = ("psum", 2)
                Tk.op("tensor", lambda t, ch=ch, dst=dst: t.matmul(
                    b.pb[2][0:64, 0:512], pm[:, :], dst[0:64, ch * 512:(ch + 1) * 512],
                    start=True, stop=True),
                    reads=[rawk, ("pm",)], writes=[pk2], tag=f"{which}sw{h}_{ch}")
                Tk.op("vector", lambda v, ch=ch, dst=dst: v.tensor_tensor(
                    t64a[:], dst[0:64, ch * 512:(ch + 1) * 512],
                    cosb[0:64, ch * 512:(ch + 1) * 512], ALU.mult),
                    reads=[rawk, ("cos",)], writes=[("t64a",)],
                    tag=f"{which}r1{h}_{ch}")
                Tk.op("vector", lambda v, ch=ch: v.tensor_tensor(
                    t64b[:], b.pb[2][0:64, 0:512],
                    sinb[0:64, ch * 512:(ch + 1) * 512], ALU.mult),
                    reads=[pk2, ("sin",)], writes=[("t64b",)], tag=f"{which}r2{h}_{ch}")
                Tk.op("vector", lambda v, ch=ch, dst=dst: v.tensor_tensor(
                    dst[0:64, ch * 512:(ch + 1) * 512], t64a[:], t64b[:], ALU.add),
                    reads=[("t64a",), ("t64b",)], writes=[rawk],
                    tag=f"{which}r3{h}_{ch}")
        qkeys = [("qT", ch) for ch in range(NCH)]
        kkeys = [("kT", ch) for ch in range(NCH)]
        emit_qk_bound(b, f"qb{h}_", qT, kT, qkeys, kkeys, t64a)


        for tb in range(NCH):
            avk = ("psum", 2)
            for st in range(NS):
                sck = ("psum", st % 2)
                Tk.op("tensor", lambda t, st=st, tb=tb: t.matmul(
                    b.pb[st % 2][:, 0:512], kT[0:65, st * 128:(st + 1) * 128],
                    qT[0:65, tb * 512:(tb + 1) * 512], start=True, stop=True),
                    reads=kkeys + qkeys + [("kaug",), ("qaug",)], writes=[sck], tag=f"sc{h}_{tb}_{st}")
                Tk.op("scalar", lambda s, st=st: s.activation(
                    et[:, st % 2, :], b.pb[st % 2][:, 0:512], AF.Exp,
                    bias=zc0[:, 0:1], scale=0.125),
                    reads=[sck, b.kc(0.0)], writes=[("et", st % 2)],
                    tag=f"ex{h}_{tb}_{st}")
                Tk.op("tensor", lambda t, st=st, h=h: t.matmul(
                    b.pb[2][0:65, 0:512], vhat[:, st, h, :], et[:, st % 2, :],
                    start=(st == 0), stop=(st == NS - 1)),
                    reads=[("et", st % 2), ("vhat", st), ("vhat_ones",)],
                    writes=[avk], tag=f"av{h}_{tb}_{st}")
            Tk.op("vector", lambda v: v.reciprocal(rz[0:1, :], b.pb[2][64:65, 0:512]),
                  reads=[avk], writes=[("rz",)], tag=f"rcp{h}_{tb}")
            Tk.op("tensor", lambda t: t.matmul(
                b.pb[3][0:64, 0:512], ones_r[0:1, 0:64], rz[0:1, :],
                start=True, stop=True),
                reads=[("rz",), onesrk], writes=[("psum", 3)], tag=f"rzb{h}_{tb}")
            Tk.op("vector", lambda v: v.tensor_copy(rzb[:], b.pb[3][0:64, 0:512]),
                  reads=[("psum", 3)], writes=[("rzb",)], tag=f"rzc{h}_{tb}")
            if hh == 0:
                Tk.op("vector", lambda v, hb=hb, tb=tb: v.tensor_tensor(
                    att[0:64, hb, tb * 512:(tb + 1) * 512], b.pb[2][0:64, 0:512],
                    rzb[:], ALU.mult),
                    reads=[avk, ("rzb",)], writes=[("att", hb, 0, tb)],
                    tag=f"nm{h}_{tb}")
            else:
                Tk.op("vector", lambda v: v.tensor_tensor(
                    t64a[:], b.pb[2][0:64, 0:512], rzb[:], ALU.mult),
                    reads=[avk, ("rzb",)], writes=[("t64a",)], tag=f"nm{h}_{tb}")
                Tk.op("scalar", lambda s, hb=hb, tb=tb: s.activation(
                    att[64:128, hb, tb * 512:(tb + 1) * 512], t64a[:], AF.Copy),
                    reads=[("t64a",)], writes=[("att", hb, 1, tb)],
                    tag=f"nmc{h}_{tb}")

    for ob in range(8):
        for ch in range(NCH):
            pk = ("psum", (ob * NCH + ch) % 2)
            for hb in range(4):
                Tk.op("tensor", lambda t, ob=ob, ch=ch, hb=hb, pk=pk: t.matmul(
                    b.pb[pk[1]][:, 0:512], wo[:, hb, ob, :],
                    att[:, hb, ch * 512:(ch + 1) * 512],
                    start=(hb == 0), stop=(hb == 3)),
                    reads=[("wo",), ("att", hb, 0, ch), ("att", hb, 1, ch)],
                    writes=[pk], tag=f"op{ob}_{ch}_{hb}")
            Tk.op("vector", lambda v, ch=ch, pk=pk: v.tensor_copy(
                post[:, ch % 2, :], b.pb[pk[1]][:, 0:512]),
                reads=[pk], writes=[("post", ch % 2)], tag=f"oe{ob}_{ch}")
            b.store(po_h[:, ob, ch * 512:(ch + 1) * 512], post[:, ch % 2, :],
                    reads=[("post", ch % 2)], tag=f"os{ob}_{ch}")


def build_A0():
    b = Builder("A0")
    emit_attn_l0(b)
    b.finish()
    return b


def build_E00():
    b = Builder("E00")
    xl = b.dram_in("xl", [128, 8, TL], F32)
    lno = b.dram_out("lnl", [128, 8, TL], BF16)
    lnsb = b.sb("lnsb", [128, ND, TL], BF16)
    emit_ln(b, "l", ("hbm", xl, []), TL, out=(lnsb, lambda c, d: ("o", c, d)))
    for c in range(TL // 512):
        b.store(lno[:, :, c * 512:(c + 1) * 512], lnsb[:, :, c * 512:(c + 1) * 512],
                reads=[("o", c, d) for d in range(ND)], tag=f"st{c}")
    b.finish()
    return b


def build_E0(debug=False):
    b = Builder("E0")
    Tk = b.T
    xl_h = b.dram_in("xl", [128, 8, TL], F32)
    p0_h = b.dram_in("p0", [128, 8, TL], F32)
    p1_h = b.dram_in("p1", [128, 8, TL], F32)
    bo_h = b.dram_in("bo", [8, 128], F32)
    gam_h = b.dram_in("gam", [8, 128], F32)
    bet_h = b.dram_in("bet", [8, 128], F32)
    x1_h = b.dram_out("x1", [128, 8, TL], F32)
    ln1_h = b.dram_out("ln1", [128, 8, TL], BF16)
    mk = b.dram_out if debug else b.dram_tmp
    h_t = mk("h_t", [128, 8, TL], F32)
    y_t = mk("y_t", [128, 8, TL], F32)

    bo_c = b.colset("s_bo", bo_h, 8, "bo")
    gam_c = b.colset("s_gam", gam_h, 8, "gam")
    bet_c = b.colset("s_bet", bet_h, 8, "bet")

    xt = b.shared_xt()
    pt = [b.sb(f"e_p{i}", [128, ND, 512], F32) for i in range(2)]
    ot = b.sb("e_o", [128, ND, 512], F32)

    CH = 512
    nch = TL // CH
    ab0, ab0k = emit_ln(b, "ln0", ("hbm", xl_h, []), TL)
    for c in range(nch):
        xk = ("xts", c % 2)
        pk0 = ("ep", c % 2)
        b.load(xt[c % 2][:], xl_h[:, :, c * CH:(c + 1) * CH], writes=[xk],
               tag=f"h_ldx{c}")
        b.load(pt[c % 2][:], p0_h[:, :, c * CH:(c + 1) * CH], writes=[pk0],
               tag=f"h_ldp0{c}")
        pA, pAk, pB, pBk = ab_bcast(b, ab0, ab0k, c, CH, f"h_bc{c}")
        for d in range(ND):
            hk = ("eo", d)
            Tk.op("vector", lambda v, c=c, d=d, pA=pA: v.tensor_tensor(
                ot[:, d, :], xt[c % 2][:, d, :], pA, ALU.mult),
                reads=[xk, pAk], writes=[hk], tag=f"h1_{c}_{d}")
            Tk.op("vector", lambda v, c=c, d=d, pB=pB: v.tensor_tensor(
                ot[:, d, :], ot[:, d, :], pB, ALU.subtract),
                reads=[hk, pBk], writes=[hk], tag=f"h2_{c}_{d}")
            Tk.op("vector", lambda v, c=c, d=d: v.tensor_scalar_add(
                ot[:, d, :], ot[:, d, :], bo_c[d][:, 0:1]),
                reads=[("bo",), hk], writes=[hk], tag=f"h3a_{c}_{d}")
            Tk.op("vector", lambda v, c=c, d=d: v.tensor_tensor(
                ot[:, d, :], ot[:, d, :], pt[c % 2][:, d, :], ALU.add),
                reads=[pk0, hk], writes=[hk], tag=f"h3b_{c}_{d}")
        b.load(pt[c % 2][:], p1_h[:, :, c * CH:(c + 1) * CH], writes=[pk0],
               tag=f"h_ldp1{c}")
        for d in range(ND):
            hk = ("eo", d)
            Tk.op("vector", lambda v, c=c, d=d: v.tensor_tensor(
                ot[:, d, :], ot[:, d, :], pt[c % 2][:, d, :], ALU.add),
                reads=[pk0, hk], writes=[hk], tag=f"h4_{c}_{d}")
        b.store(h_t[:, :, c * CH:(c + 1) * CH], ot[:],
                reads=[("eo", d) for d in range(ND)], tag=f"h_st{c}")
    emit_ln(b, "lny", ("hbm", h_t, []), TL, out_hbm=y_t, out_dtype=F32,
            gamma_beta=(gam_c, bet_c, [("gam",), ("bet",)]))
    ab2, ab2k = emit_ln(b, "ln2", ("hbm", y_t, []), TL)
    for c in range(nch):
        xk = ("xts", c % 2)
        pk0 = ("ep", c % 2)
        b.load(xt[c % 2][:], xl_h[:, :, c * CH:(c + 1) * CH], writes=[xk],
               tag=f"x1_ldx{c}")
        b.load(pt[c % 2][:], y_t[:, :, c * CH:(c + 1) * CH], writes=[pk0],
               tag=f"x1_ldy{c}")
        pA, pAk, pB, pBk = ab_bcast(b, ab2, ab2k, c, CH, f"x1_bc{c}")
        for d in range(ND):
            hk = ("eo", d)
            Tk.op("vector", lambda v, c=c, d=d, pA=pA: v.tensor_tensor(
                ot[:, d, :], pt[c % 2][:, d, :], pA, ALU.mult),
                reads=[pk0, pAk], writes=[hk], tag=f"x1a_{c}_{d}")
            Tk.op("vector", lambda v, c=c, d=d, pB=pB: v.tensor_tensor(
                ot[:, d, :], ot[:, d, :], pB, ALU.subtract),
                reads=[hk, pBk], writes=[hk], tag=f"x1b_{c}_{d}")
            Tk.op("vector", lambda v, c=c, d=d: v.tensor_tensor(
                ot[:, d, :], ot[:, d, :], xt[c % 2][:, d, :], ALU.add),
                reads=[hk, xk], writes=[hk], tag=f"x1c_{c}_{d}")
        b.store(x1_h[:, :, c * CH:(c + 1) * CH], ot[:],
                reads=[("eo", d) for d in range(ND)], writes=[("x1out",)],
                tag=f"x1s{c}")
    emit_ln(b, "ln3", ("hbm", x1_h, [("x1out",)]), TL, out_hbm=ln1_h)
    b.finish()
    return b


_DEINT = np.concatenate([np.arange(0, 64, 2), np.arange(1, 64, 2)])


def _tile_w(w):
    din, dout = w.shape
    return np.ascontiguousarray(
        w.reshape(din // 128, 128, dout // 128, 128).transpose(0, 2, 1, 3))


def _fm(x):
    return np.ascontiguousarray(x.T.reshape(8, 128, -1).transpose(1, 0, 2))


def _unfm(a):
    return np.ascontiguousarray(np.asarray(a).transpose(1, 0, 2).reshape(1024, -1).T)


def _rope_tables():
    inv = (1.0 / 10000.0 ** (np.arange(0, 64, 2, dtype=np.float64) / 64.0))
    pos = np.arange(S_, dtype=np.float64)
    f = np.outer(inv, pos)
    f4 = np.tile(f, (4, 1))
    return (np.cos(f4).astype(np.float32), np.sin(f4).astype(np.float32))


def _pm_mat():
    p = np.zeros((64, 64), np.float32)
    for i in range(32):
        p[i + 32, i] = -1.0
    for i in range(32, 64):
        p[i - 32, i] = 1.0
    return p


def prep_A0(ins):
    Wqkv = np.asarray(ins["L0_attn_Wqkv"])
    bqkv = np.asarray(ins["L0_attn_bqkv"])
    Wo = np.asarray(ins["L0_attn_Wo"])
    cos, sin = _rope_tables()
    pm = _pm_mat()
    Wq = Wqkv[:, 0:1024]
    Wk = Wqkv[:, 1024:2048]
    Wv = Wqkv[:, 2048:3072]
    bq = bqkv[0:1024]
    bk = bqkv[1024:2048]
    bv = bqkv[2048:3072]
    Wq_de = np.concatenate([Wq[:, h * 64 + _DEINT] for h in range(16)], axis=1)
    Wk_de = np.concatenate([Wk[:, h * 64 + _DEINT] for h in range(16)], axis=1)
    bq_de = np.concatenate([bq[h * 64 + _DEINT] for h in range(16)])
    bk_de = np.concatenate([bk[h * 64 + _DEINT] for h in range(16)])
    maps = []
    for c in range(8):
        g = c % 2
        cols = slice(g * 512, (g + 1) * 512)
        m = {
            "wq": np.ascontiguousarray(Wq_de[:, cols].reshape(8, 128, 512).transpose(1, 0, 2)),
            "wk": np.ascontiguousarray(Wk_de[:, cols].reshape(8, 128, 512).transpose(1, 0, 2)),
            "wv": np.ascontiguousarray(Wv[:, cols].reshape(8, 128, 512).transpose(1, 0, 2)),
            "wo": np.ascontiguousarray(
                Wo[g * 512:(g + 1) * 512].reshape(4, 128, 8, 128).transpose(1, 0, 2, 3)),
            "bq": np.ascontiguousarray(bq_de[cols].reshape(8, 64).T),
            "bk": np.ascontiguousarray(bk_de[cols].reshape(8, 64).T),
            "bv": np.ascontiguousarray(bv[cols].reshape(1, 512)),
            "cosT": cos, "sinT": sin, "pm": pm,
        }
        maps.append(m)
    return maps


def prep_E0(ins):
    bo = np.asarray(ins["L0_attn_bo"])
    gam = np.asarray(ins["L0_attn_gamma"])
    bet = np.asarray(ins["L0_attn_beta"])
    m = {"bo": np.ascontiguousarray(bo.reshape(8, 128)),
         "gam": np.ascontiguousarray(gam.reshape(8, 128)),
         "bet": np.ascontiguousarray(bet.reshape(8, 128))}
    return [dict(m) for _ in range(8)]


def build_L0F(phases=4, nexp=9, dbg=False, race_detect=True):
    b = Builder("L0F", race_detect)
    Tk = b.T
    NF = 32
    TH = TL // 2

    xl_h = b.dram_in("xl", [128, ND, TL], F32)
    rT_h = b.dram_in("rT", [128, ND, 8], F32)
    rb_h = b.dram_in("rb", [1, 8], F32)
    sw1_h = b.dram_in("sw1", [128, ND, 4096], F32)
    sb1_h = b.dram_in("sb1", [32, 128], F32)
    sw2_h = b.dram_in("sw2", [128, 32, 1024], F32)
    sb2_h = b.dram_in("sb2", [8, 128], F32)
    ew1_h = b.dram_in("ew1", [8, 128, ND, 4096], F32)
    eb1_h = b.dram_in("eb1", [8, 32, 128], F32)
    ew2_h = b.dram_in("ew2", [8, 128, 32, 1024], F32)
    eb2_h = b.dram_in("eb2", [8, 8, 128], F32)
    ident_h = b.dram_in("ident", [128, 128], F32)
    x2_h = b.dram_out("x2", [128, ND, TL], F32)
    ln2_h = b.dram_out("ln2", [128, ND, TL], BF16)
    u_t = b.dram_tmp("u_t", [128, ND, TL], BF16)
    acc_t = b.dram_tmp("acc_t", [128, ND, TL], F32)
    wrow_t = (b.dram_out if dbg else b.dram_tmp)("wrow_t", [8, TL], F32)
    lg_o = b.dram_out("lg_o", [128, 16, 8], F32) if dbg else None
    cb_o = b.dram_out("cb_o", [128, 16, 8], F32) if dbg else None

    uh = b.sb("uh", [128, ND, TL // 2], BF16)
    h1h = b.sb("h1h", [128, 32, TL // 2], BF16)
    acc = b.sb("acc", [128, ND, TL // 2], F32)
    h1v = h1h.bitcast(F32)
    b._sxt = [h1v[:, 0:8, :], h1v[:, 8:16, :]]
    b._ssq = h1v[:, 16:24, :]

    ident = b.sb("s_id", [128, 128], F32)
    b.load(ident[:], ident_h, writes=[("id",)], tag="ld_id")
    rT = b.sb("s_rT", [128, ND, 8], F32)
    b.load(rT[:], rT_h, writes=[("rT",)], tag="ld_rT")
    rb = b.sb("s_rb", [1, 8], F32)
    b.load(rb[:], rb_h, writes=[("rb",)], tag="ld_rb")

    ab0, ab0k = emit_ln(b, "l0f_ln", ("hbm", xl_h, []), TL, out_hbm=u_t)

    xt = b.shared_xt()
    u32 = b.sb("u32", [128, ND, 128], F32)
    lg = b.sb("lg", [128, 8], F32)
    m1 = b.sb("m1", [128, 1], F32)
    m2 = b.sb("m2", [128, 1], F32)
    e1 = b.sb("e1", [128, 8], F32)
    wr8 = b.sb("wr8", [8, TL], F32)
    ones_r = b.constrow(1.0, 128, F32)
    onesrk = b.kr(1.0, 128, F32)
    for t in range(TL // 128):
        c, off = t // 4, (t % 4) * 128
        xk = ("xts", c % 2)
        if off == 0:
            b.load(xt[c % 2][:], xl_h[:, :, c * 512:(c + 1) * 512], writes=[xk],
                   tag=f"r_ldx{c}")
            bcasts = ab_bcast(b, ab0, ab0k, c, 512, f"r_bc{c}")
        pA, pAk, pB, pBk = bcasts
        for d in range(ND):
            Tk.op("vector", lambda v, d=d, off=off, c=c, pA=pA: v.tensor_tensor(
                u32[:, d, :], xt[c % 2][:, d, off:off + 128], pA[:, off:off + 128],
                ALU.mult), reads=[xk, pAk], writes=[("u32", d)], tag=f"r_u1_{t}_{d}")
            Tk.op("vector", lambda v, d=d, off=off, pB=pB: v.tensor_tensor(
                u32[:, d, :], u32[:, d, :], pB[:, off:off + 128], ALU.subtract),
                reads=[("u32", d), pBk], writes=[("u32", d)], tag=f"r_u2_{t}_{d}")
        for d in range(ND):
            Tk.op("tensor", lambda t_, d=d: t_.matmul(
                b.pb[6][0:128, 0:8], u32[:, d, :], rT[:, d, :],
                start=(d == 0), stop=False),
                reads=[("u32", d), ("rT",)], writes=[("psum", 6)], tag=f"r_mm{t}_{d}")
        Tk.op("tensor", lambda t_: t_.matmul(
            b.pb[6][0:128, 0:8], ones_r[0:1, :], rb[0:1, :], start=False, stop=True),
            reads=[("rb",), onesrk], writes=[("psum", 6)], tag=f"r_mb{t}")
        Tk.op("vector", lambda v: v.tensor_copy(lg[:], b.pb[6][0:128, 0:8]),
              reads=[("psum", 6)], writes=[("lg",)], tag=f"r_cp{t}")
        if dbg:
            b.store(lg_o[:, t, :], lg[:], reads=[("lg",)], tag=f"r_dbg{t}")
        Tk.op("vector", lambda v: v.reduce_max(m1[:], lg[:], axis=AX.X),
              reads=[("lg",)], writes=[("m1",)], tag=f"r_m1{t}")
        Tk.op("vector", lambda v: v.tensor_single_scalar(
            e1[:], lg[:], m1[:, 0:1], ALU.is_equal),
            reads=[("lg",), ("m1",)], writes=[("e1",)], tag=f"r_eq{t}")
        Tk.op("vector", lambda v: v.scalar_tensor_tensor(
            e1[:], e1[:], -1e30, lg[:], ALU.mult, ALU.add),
            reads=[("e1",), ("lg",)], writes=[("e1",)], tag=f"r_msk{t}")
        Tk.op("vector", lambda v: v.reduce_max(m2[:], e1[:], axis=AX.X),
              reads=[("e1",)], writes=[("m2",)], tag=f"r_m2{t}")
        Tk.op("vector", lambda v: v.tensor_scalar_mul(m1[:], m1[:], -1.0),
              reads=[("m1",)], writes=[("m1",)], tag=f"r_ng{t}")
        Tk.op("scalar", lambda s: s.activation(e1[:], lg[:], AF.Exp,
                                               bias=m1[:, 0:1]),
              reads=[("lg",), ("m1",)], writes=[("e1",)], tag=f"r_ex{t}")
        Tk.op("vector", lambda v: v.tensor_single_scalar(
            lg[:], lg[:], m2[:, 0:1], ALU.is_ge),
            reads=[("lg",), ("m2",)], writes=[("lg",)], tag=f"r_ge{t}")
        Tk.op("vector", lambda v: v.tensor_mul(e1[:], e1[:], lg[:]),
              reads=[("e1",), ("lg",)], writes=[("e1",)], tag=f"r_ew{t}")
        Tk.op("vector", lambda v: v.reduce_sum(m2[:], e1[:], axis=AX.X),
              reads=[("e1",)], writes=[("m2",)], tag=f"r_sm{t}")
        Tk.op("vector", lambda v: v.reciprocal(m1[:], m2[:]),
              reads=[("m2",)], writes=[("m1",)], tag=f"r_rc{t}")
        Tk.op("vector", lambda v: v.tensor_scalar_mul(e1[:], e1[:], m1[:, 0:1]),
              reads=[("e1",), ("m1",)], writes=[("e1",)], tag=f"r_cw{t}")
        if dbg:
            b.store(cb_o[:, t, :], e1[:], reads=[("e1",)], tag=f"r_dbg2{t}")
        Tk.op("tensor", lambda t_: t_.transpose(
            b.pb[7][0:8, 0:128], e1[:], ident[:]),
            reads=[("e1",), ("id",)], writes=[("psum", 7)], tag=f"r_tr{t}")
        Tk.op("vector", lambda v, t=t: v.tensor_copy(
            wr8[:, t * 128:(t + 1) * 128], b.pb[7][0:8, 0:128]),
            reads=[("psum", 7)], writes=[("wr8",)], tag=f"r_wr{t}")
    b.store(wrow_t, wr8[:], reads=[("wr8",)], tag="st_wr")

    if phases < 3:
        b.finish()
        return b
    b.fence("f_l0f_p3")
    w1r = [b.sb(f"w1r{i}", [128, ND, 128], BF16) for i in range(2)]
    w2r = [b.sb(f"w2r{i}", [128, NF, 128], BF16) for i in range(2)]
    b1p = b.sb("b1p", [128, 32], F32)
    b2p = b.sb("b2p", [128, 8], F32)
    wer = b.sb("wer", [1, TL], F32)
    h2s = b.sb("h2s", [128, 512], F32)

    for half in range(2):
        hsl = slice(half * TH, (half + 1) * TH)
        b.load(uh[:], u_t[:, :, hsl], reads=[("wrt",)], writes=[("uh",)],
               tag=f"ld_uh{half}")
        for e in range(nexp):
            if e == 0:
                w1src, b1src = sw1_h, sb1_h
                w2src, b2src = sw2_h, sb2_h
            else:
                w1src, b1src = ew1_h[e - 1], eb1_h[e - 1]
                w2src, b2src = ew2_h[e - 1], eb2_h[e - 1]
            for j in range(32):
                b.load(b1p[:, j:j + 1], b1src[j:j + 1, :], writes=[("b1p",)],
                       tag=f"lb1_{half}_{e}_{j}")
            for j in range(8):
                b.load(b2p[:, j:j + 1], b2src[j:j + 1, :], writes=[("b2p",)],
                       tag=f"lb2_{half}_{e}_{j}")
            if e > 0:
                b.load(wer[:], wrow_t[e - 1:e, :], reads=[("wrt",)],
                       writes=[("wer",)], tag=f"ldw_{half}_{e}")
            for ffb in range(NF):
                wk = ("w1r", ffb % 2)
                b.load(w1r[ffb % 2][:], w1src[:, :, ffb * 128:(ffb + 1) * 128],
                       writes=[wk], eng="gpsimd", tag=f"lw1_{half}_{e}_{ffb}")
                for ch in range(2):
                    pk = ("psum", ch)
                    for d in range(ND):
                        Tk.op("tensor", lambda t_, d=d, ch=ch, ffb=ffb, wk=wk: t_.matmul(
                            b.pb[ch][:, 0:512],
                            w1r[ffb % 2][:, d, :],
                            uh[:, d, ch * 512:(ch + 1) * 512],
                            start=(d == 0), stop=(d == ND - 1)),
                            reads=[wk, ("uh",)], writes=[pk],
                            tag=f"h1m_{half}_{e}_{ffb}_{ch}")
                    Tk.op("scalar", lambda s, ffb=ffb, ch=ch: s.activation(
                        h1h[:, ffb, ch * 512:(ch + 1) * 512], b.pb[ch][:, 0:512],
                        AF.Gelu_apprx_tanh, bias=b1p[:, ffb:ffb + 1]),
                        reads=[pk, ("b1p",)], writes=[("h1h", ffb)],
                        tag=f"h1e_{half}_{e}_{ffb}_{ch}")
            for ob in range(8):
                wk = ("w2r", ob % 2)
                b.load(w2r[ob % 2][:], w2src[:, :, ob * 128:(ob + 1) * 128],
                       writes=[wk], eng="gpsimd", tag=f"lw2_{half}_{e}_{ob}")
                for ch in range(2):
                    pk = ("psum", 2 + ch)
                    for ffb in range(NF):
                        Tk.op("tensor", lambda t_, ffb=ffb, ob=ob, ch=ch, wk=wk: t_.matmul(
                            b.pb[2 + ch][:, 0:512],
                            w2r[ob % 2][:, ffb, :],
                            h1h[:, ffb, ch * 512:(ch + 1) * 512],
                            start=(ffb == 0), stop=(ffb == NF - 1)),
                            reads=[wk, ("h1h", ffb)], writes=[pk],
                            tag=f"h2m_{half}_{e}_{ob}_{ch}")
                    if e == 0:
                        Tk.op("scalar", lambda s, ob=ob, ch=ch: s.activation(
                            acc[:, ob, ch * 512:(ch + 1) * 512], b.pb[2 + ch][:, 0:512],
                            AF.Identity, bias=b2p[:, ob:ob + 1]),
                            reads=[pk, ("b2p",)], writes=[("acc", ob)],
                            tag=f"h2e_{half}_{e}_{ob}_{ch}")
                    else:
                        Tk.op("scalar", lambda s, ob=ob, ch=ch: s.activation(
                            h2s[:], b.pb[2 + ch][:, 0:512],
                            AF.Identity, bias=b2p[:, ob:ob + 1]),
                            reads=[pk, ("b2p",)], writes=[("h2s",)],
                            tag=f"h2e_{half}_{e}_{ob}_{ch}")
                        pW, pWk = bcast_row(
                            b, 4 + ch,
                            wer[0:1, half * TH + ch * 512: half * TH + (ch + 1) * 512],
                            [("wer",)], 512, f"h2w_{half}_{e}_{ob}_{ch}")
                        Tk.op("vector", lambda v, ob=ob, ch=ch, pW=pW: v.tensor_mul(
                            h2s[:], h2s[:], pW),
                            reads=[("h2s",), pWk], writes=[("h2s",)],
                            tag=f"h2x_{half}_{e}_{ob}_{ch}")
                        Tk.op("vector", lambda v, ob=ob, ch=ch: v.tensor_tensor(
                            acc[:, ob, ch * 512:(ch + 1) * 512],
                            acc[:, ob, ch * 512:(ch + 1) * 512], h2s[:], ALU.add),
                            reads=[("h2s",), ("acc", ob)], writes=[("acc", ob)],
                            tag=f"h2a_{half}_{e}_{ob}_{ch}")
        b.store(acc_t[:, :, hsl], acc[:],
                reads=[("acc", ob) for ob in range(8)], writes=[("acct",)],
                tag=f"st_acc{half}")

    if phases < 4:
        if dbg:
            acc_o = b.dram_out("acc_o", [128, ND, TL], F32)
            b.load(h1v[:, 24:32, :], acc_t[:, :, 0:512], reads=[("acct",)], writes=[("cp0",)], tag="cpa")
        b.finish()
        return b
    b.fence("f_l0f_p4")
    ab2, ab2k = emit_ln(b, "l0f_ln2", ("hbm", acc_t, [("acct",)]), TL)
    pt = h1v[:, 24:32, :]
    ot = acc[:, :, 0:512]
    for c in range(TL // 512):
        xk = ("xts", c % 2)
        b.load(xt[c % 2][:], xl_h[:, :, c * 512:(c + 1) * 512], writes=[xk],
               tag=f"f_ldx{c}")
        b.load(pt[:, :, :], acc_t[:, :, c * 512:(c + 1) * 512], reads=[("acct",)],
               writes=[("fp",)], tag=f"f_lda{c}")
        pA, pAk, pB, pBk = ab_bcast(b, ab2, ab2k, c, 512, f"f_bc{c}")
        for d in range(ND):
            hk = ("fo", d)
            Tk.op("vector", lambda v, d=d, pA=pA: v.tensor_tensor(
                ot[:, d, :], pt[:, d, :], pA, ALU.mult),
                reads=[("fp",), pAk], writes=[hk], tag=f"f_a{c}_{d}")
            Tk.op("vector", lambda v, d=d, pB=pB: v.tensor_tensor(
                ot[:, d, :], ot[:, d, :], pB, ALU.subtract),
                reads=[hk, pBk], writes=[hk], tag=f"f_b{c}_{d}")
            Tk.op("vector", lambda v, d=d, c=c: v.tensor_tensor(
                ot[:, d, :], ot[:, d, :], xt[c % 2][:, d, :], ALU.add),
                reads=[hk, xk], writes=[hk], tag=f"f_c{c}_{d}")
        b.store(x2_h[:, :, c * 512:(c + 1) * 512], ot[:, :, :],
                reads=[("fo", d) for d in range(ND)], writes=[("x2o",)],
                tag=f"f_st{c}")
    emit_ln(b, "l0f_ln3", ("hbm", x2_h, [("x2o",)]), TL, out_hbm=ln2_h)
    b.finish()
    return b


def prep_L0F(ins):
    def pmaj(wT):
        return np.ascontiguousarray(wT.reshape(8, 128, -1).transpose(1, 0, 2))

    sw1 = pmaj(np.asarray(ins["L0_ffn_shared_w1"]).T)
    sw2 = np.ascontiguousarray(
        np.asarray(ins["L0_ffn_shared_w2"]).T.reshape(32, 128, 1024)
        .transpose(1, 0, 2))
    ew1 = np.stack([pmaj(np.asarray(ins["L0_ffn_spec_w1"][e]).T) for e in range(8)])
    ew2 = np.stack([np.ascontiguousarray(
        np.asarray(ins["L0_ffn_spec_w2"][e]).T.reshape(32, 128, 1024)
        .transpose(1, 0, 2)) for e in range(8)])
    m = {
        "rT": pmaj(np.asarray(ins["L0_ffn_router_w"]).T),
        "rb": np.asarray(ins["L0_ffn_router_b"]).reshape(1, 8),
        "sw1": sw1, "sb1": np.asarray(ins["L0_ffn_shared_b1"]).reshape(32, 128),
        "sw2": sw2, "sb2": np.asarray(ins["L0_ffn_shared_b2"]).reshape(8, 128),
        "ew1": ew1, "eb1": np.asarray(ins["L0_ffn_spec_b1"]).reshape(8, 32, 128),
        "ew2": ew2, "eb2": np.asarray(ins["L0_ffn_spec_b2"]).reshape(8, 8, 128),
        "ident": np.eye(128, dtype=np.float32),
    }
    return [m for _ in range(8)]


def build_A1():
    b = Builder("A1")
    Tk = b.T
    H, NS, NCH = 8, 32, 8

    lnf_h = b.dram_in("lnf", [128, 8, S_], BF16)
    wq_h = b.dram_in("wq", [128, 8, 512], F32)
    wk_h = b.dram_in("wk", [128, 8, 512], F32)
    wv_h = b.dram_in("wv", [128, 8, 512], F32)
    wo_h = b.dram_in("wo", [128, 4, 8, 128], F32)
    band_h = b.dram_in("band", [8, 128, 1664], F32)
    cm_h = b.dram_in("cm", [128, 1152], F32)
    tau_h = b.dram_in("tau", [1, 8], F32)
    invrow_h = b.dram_in("invrow", [1, S_], F32)
    cos_h = b.dram_in("cosT", [128, S_], F32)
    sin_h = b.dram_in("sinT", [128, S_], F32)
    pm_h = b.dram_in("pm", [64, 64], F32)
    po_h = b.dram_out("po", [128, 8, S_], F32)

    lnf = b.sb("s_lnf", [128, ND, S_], BF16)
    vhat = b.sb("vhat", [128, NS, H, 64], BF16)
    cosb = b.sb("cosb", [128, S_], BF16)
    sinb = b.sb("sinb", [128, S_], BF16)
    wqh = b.sb("s_wqh", [128, 2, ND, 64], BF16)
    wv = b.sb("s_wv", [128, ND, 512], BF16)
    wo = wv.reshape([128, 4, 8, 128])
    pm = b.sb("s_pm", [64, 64], BF16)
    kT = b.sb("kT", [65, S_], BF16)
    qT = b.sb("qT", [65, S_], BF16)
    zc0 = b.constcol(0.0)
    TBW = 256
    ebuf = b.sb("ebuf", [128, NS, TBW], BF16)
    eh = b.sb("eh", [128, TBW], BF16)
    att = b.sb("att", [128, 4, S_], BF16)
    bandb = b.sb("bandb", [128, 1, 1664], BF16)
    cmb = b.sb("cmb", [128, 1152], BF16)
    tau = b.sb("s_tau", [1, 8], F32)
    invr = b.sb("s_invr", [1, TBW], F32)
    zrow = b.sb("zrow", [1, TBW], F32)
    grow = b.sb("grow", [1, TBW], F32)
    gb = b.sb("gbf", [128, TBW], BF16)
    rz = b.sb("rz", [1, TBW], F32)
    rzb = b.sb("rzb", [64, TBW], F32)
    t64a = b.sb("t64a", [64, 512], BF16)
    t64b = b.sb("t64b", [64, 512], BF16)
    sc1 = b.sb("sc1", [64, 1], F32)
    sc2 = b.sb("sc2", [1, 4], F32)
    negc = b.sb("negc", [128, 1], F32)
    post = b.sb("post", [128, 1, 512], F32)

    ones_c = b.constrow(1.0, 1, F32, parts=128)
    onesck = b.kr(1.0, 1, F32, 128)
    ones_cb = b.constrow(1.0, 1, BF16, parts=128)
    onescbk = b.kr(1.0, 1, BF16, 128)
    ones_r = b.constrow(1.0, 128, F32)
    onesrk = b.kr(1.0, 128, F32)

    b.load(lnf[:], lnf_h, writes=[("lnf", c) for c in range(NCH)], tag="ld_lnf")
    b.load(wv[:], wv_h, writes=[("wv",)], eng="gpsimd", tag="ld_wv")

    b.load(cosb[:], cos_h, writes=[("cos",)], eng="gpsimd", tag="ld_cos")
    b.load(sinb[:], sin_h, writes=[("sin",)], eng="gpsimd", tag="ld_sin")
    b.load(pm[:], pm_h, writes=[("pm",)], eng="gpsimd", tag="ld_pm")
    b.load(cmb[:], cm_h, writes=[("cm",)], eng="gpsimd", tag="ld_cm")
    b.load(tau[:], tau_h, writes=[("tau",)], tag="ld_tau")


    lnf_keys = [("lnf", c) for c in range(NCH)]

    for st in range(NS):
        pk = ("psum", st % 2)
        for d in range(ND):
            Tk.op("tensor", lambda t, st=st, d=d: t.matmul(
                b.pb[st % 2][:, 0:512], lnf[:, d, st * 128:(st + 1) * 128],
                wv[:, d, :], start=(d == 0), stop=(d == ND - 1)),
                reads=[("lnf", st // 4), ("wv",)], writes=[pk], tag=f"vp{st}")
        Tk.op("scalar", lambda s, st=st: s.activation(
            vhat[:, st, :, :], b.pb[st % 2][:, 0:512], AF.Copy),
            reads=[pk], writes=[("vhat", st)], tag=f"vev{st}")

    for h in range(H):
        hb, hh = h // 2, h % 2
        bk_ = ("band",)
        b.load(bandb[:, 0, :], band_h[h], writes=[bk_], eng="gpsimd",
               tag=f"ldband{h}")
        for wi, (which, dst, w_src) in enumerate((("k", kT, wk_h), ("q", qT, wq_h))):
            par = (2 * h + wi) % 2
            wkey = ("wqh", par)
            b.load(wqh[:, par, :, :], w_src[:, :, h * 64:(h + 1) * 64],
                   writes=[wkey], eng="gpsimd", tag=f"ldw{which}{h}")
            for ch in range(NCH):
                pk = ("psum", 3)
                for d in range(ND):
                    Tk.op("tensor", lambda t, d=d, ch=ch, par=par: t.matmul(
                        b.pb[3][0:64, 0:512], wqh[:, par, d, :],
                        lnf[:, d, ch * 512:(ch + 1) * 512],
                        start=(d == 0), stop=(d == ND - 1)),
                        reads=[("lnf", ch), wkey], writes=[pk], tag=f"{which}p{h}_{ch}")
                rawk = (which + "T", ch)
                Tk.op("scalar", lambda s, ch=ch, dst=dst: s.activation(
                    dst[0:64, ch * 512:(ch + 1) * 512], b.pb[3][0:64, 0:512], AF.Copy),
                    reads=[pk], writes=[rawk], tag=f"{which}e{h}_{ch}")
                pk2 = ("psum", 2)
                Tk.op("tensor", lambda t, ch=ch, dst=dst: t.matmul(
                    b.pb[2][0:64, 0:512], pm[:, :], dst[0:64, ch * 512:(ch + 1) * 512],
                    start=True, stop=True),
                    reads=[rawk, ("pm",)], writes=[pk2], tag=f"{which}sw{h}_{ch}")
                Tk.op("vector", lambda v, ch=ch, dst=dst: v.tensor_tensor(
                    t64a[:], dst[0:64, ch * 512:(ch + 1) * 512],
                    cosb[0:64, ch * 512:(ch + 1) * 512], ALU.mult),
                    reads=[rawk, ("cos",)], writes=[("t64a",)], tag=f"{which}r1{h}_{ch}")
                Tk.op("vector", lambda v, ch=ch: v.tensor_tensor(
                    t64b[:], b.pb[2][0:64, 0:512],
                    sinb[0:64, ch * 512:(ch + 1) * 512], ALU.mult),
                    reads=[pk2, ("sin",)], writes=[("t64b",)], tag=f"{which}r2{h}_{ch}")
                Tk.op("vector", lambda v, ch=ch, dst=dst: v.tensor_tensor(
                    dst[0:64, ch * 512:(ch + 1) * 512], t64a[:], t64b[:], ALU.add),
                    reads=[("t64a",), ("t64b",)], writes=[rawk], tag=f"{which}r3{h}_{ch}")
        qkeys = [("qT", ch) for ch in range(NCH)]
        kkeys = [("kT", ch) for ch in range(NCH)]
        emit_qk_bound(b, f"qb{h}_", qT, kT, qkeys, kkeys, t64a)


        for tb in range(S_ // TBW):
            t0 = tb * TBW
            b.load(invr[:], invrow_h[0:1, t0:t0 + TBW], writes=[("invr",)],
                   tag=f"ld_invr{h}_{tb}")
            SCB = (0, 1, 6, 7)
            laststar = min(NS - 1, (t0 + TBW - 1) // 128)
            for st in range(NS):
                s0 = st * 128
                dead = s0 > t0 + TBW - 1
                if dead:
                    Tk.op("vector", lambda v, st=st: v.memset(ebuf[:, st, :], 0.0),
                          writes=[("eb", st)], tag=f"edead{h}_{tb}_{st}")
                    continue
                bi = SCB[st % 4]
                sck = ("psum", bi)
                Tk.op("tensor", lambda t, st=st, t0=t0, bi=bi: t.matmul(
                    b.pb[bi][:, 0:TBW], kT[0:65, st * 128:(st + 1) * 128],
                    qT[0:65, t0:t0 + TBW], start=True, stop=True),
                    reads=kkeys + qkeys + [("kaug",), ("qaug",)], writes=[sck], tag=f"sc{h}_{tb}_{st}")
                dts = t0 - s0
                if -(TBW - 1) <= dts <= 639:
                    off = dts + 512
                    Tk.op("vector", lambda v, off=off, bi=bi: v.tensor_tensor(
                        b.pb[bi][:, 0:TBW], b.pb[bi][:, 0:TBW],
                        bandb[:, 0, off:off + TBW], ALU.add),
                        reads=[sck, bk_], writes=[sck], tag=f"bnd{h}_{tb}_{st}")
                if -(TBW - 1) < dts < 128:
                    off = dts + 512
                    Tk.op("vector", lambda v, off=off, bi=bi: v.tensor_tensor(
                        b.pb[bi][:, 0:TBW], b.pb[bi][:, 0:TBW],
                        cmb[:, off:off + TBW], ALU.add),
                        reads=[sck, ("cm",)], writes=[sck], tag=f"cm{h}_{tb}_{st}")
                Tk.op("scalar", lambda s, st=st, bi=bi: s.activation(
                    ebuf[:, st, :], b.pb[bi][:, 0:TBW], AF.Exp,
                    bias=zc0[:, 0:1], scale=0.125),
                    reads=[sck, b.kc(0.0)], writes=[("eb", st)], tag=f"ex{h}_{tb}_{st}")
            for st in range(laststar + 1):
                Tk.op("tensor", lambda t, st=st, laststar=laststar: t.matmul(
                    b.pb[5][0:1, 0:TBW], ones_cb[:, 0:1], ebuf[:, st, :],
                    start=(st == 0), stop=(st == laststar)),
                    reads=[("eb", st), onescbk], writes=[("psum", 5)],
                    tag=f"z{h}_{tb}_{st}")
            Tk.op("vector", lambda v: v.tensor_copy(zrow[:], b.pb[5][0:1, 0:TBW]),
                  reads=[("psum", 5)], writes=[("zrow",)], tag=f"zc{h}_{tb}")
            Tk.op("vector", lambda v, tb=tb: v.tensor_tensor(
                grow[:], zrow[:], invr[0:1, :], ALU.mult),
                reads=[("zrow",), ("invr",)], writes=[("grow",)], tag=f"g1{h}_{tb}")
            Tk.op("vector", lambda v, h=h: v.tensor_scalar_mul(
                grow[:], grow[:], tau[0:1, h:h + 1]),
                reads=[("grow",), ("tau",)], writes=[("grow",)], tag=f"g2{h}_{tb}")
            pG, pGk = bcast_row(b, 3, grow[0:1, :], [("grow",)], TBW, f"gb{h}_{tb}")
            Tk.op("vector", lambda v: v.tensor_copy(gb[:], pG),
                  reads=[pGk], writes=[("gb",)], tag=f"gc{h}_{tb}")
            for st in range(NS):
                Tk.op("vector", lambda v, st=st: v.tensor_tensor(
                    ebuf[:, st, :], ebuf[:, st, :], gb[:], ALU.add),
                    reads=[("eb", st), ("gb",)], writes=[("eb", st)],
                    tag=f"ea{h}_{tb}_{st}")
                Tk.op("vector", lambda v, st=st: v.tensor_scalar_max(
                    ebuf[:, st, :], ebuf[:, st, :], 0.0),
                    reads=[("eb", st)], writes=[("eb", st)],
                    tag=f"er{h}_{tb}_{st}")
                Tk.op("tensor", lambda t, st=st, h=h: t.matmul(
                    b.pb[2][0:64, 0:TBW], vhat[:, st, h, :], ebuf[:, st, :],
                    start=(st == 0), stop=(st == NS - 1)),
                    reads=[("eb", st), ("vhat", st)],
                    writes=[("psum", 2)], tag=f"av{h}_{tb}_{st}")
            Tk.op("vector", lambda v: v.reciprocal(rz[0:1, :], zrow[0:1, :]),
                  reads=[("zrow",)], writes=[("rz",)], tag=f"rcp{h}_{tb}")
            Tk.op("tensor", lambda t: t.matmul(
                b.pb[3][0:64, 0:TBW], ones_r[0:1, 0:64], rz[0:1, :],
                start=True, stop=True),
                reads=[("rz",), onesrk], writes=[("psum", 3)], tag=f"rzb{h}_{tb}")
            Tk.op("vector", lambda v: v.tensor_copy(rzb[:], b.pb[3][0:64, 0:TBW]),
                  reads=[("psum", 3)], writes=[("rzb",)], tag=f"rzc{h}_{tb}")
            if hh == 0:
                Tk.op("vector", lambda v, hb=hb, t0=t0: v.tensor_tensor(
                    att[0:64, hb, t0:t0 + TBW], b.pb[2][0:64, 0:TBW],
                    rzb[:], ALU.mult),
                    reads=[("psum", 2), ("rzb",)], writes=[("att", hb, 0, tb // 2)],
                    tag=f"nm{h}_{tb}")
            else:
                Tk.op("vector", lambda v, t0=t0: v.tensor_tensor(
                    t64a[0:64, 0:TBW], b.pb[2][0:64, 0:TBW], rzb[:], ALU.mult),
                    reads=[("psum", 2), ("rzb",)], writes=[("t64a",)], tag=f"nm{h}_{tb}")
                Tk.op("scalar", lambda s, hb=hb, t0=t0: s.activation(
                    att[64:128, hb, t0:t0 + TBW], t64a[0:64, 0:TBW], AF.Copy),
                    reads=[("t64a",)], writes=[("att", hb, 1, tb // 2)],
                    tag=f"nmc{h}_{tb}")

    b.load(wo[:], wo_h, writes=[("wo",), ("wv",)], eng="gpsimd", tag="ld_wo")
    for ob in range(8):
        for ch in range(NCH):
            pk = ("psum", (ob * NCH + ch) % 2)
            for hb in range(4):
                Tk.op("tensor", lambda t, ob=ob, ch=ch, hb=hb, pk=pk: t.matmul(
                    b.pb[pk[1]][:, 0:512], wo[:, hb, ob, :],
                    att[:, hb, ch * 512:(ch + 1) * 512],
                    start=(hb == 0), stop=(hb == 3)),
                    reads=[("wo",), ("att", hb, 0, ch), ("att", hb, 1, ch)],
                    writes=[pk], tag=f"op{ob}_{ch}_{hb}")
            Tk.op("vector", lambda v, ch=ch, pk=pk: v.tensor_copy(
                post[:, 0, :], b.pb[pk[1]][:, 0:512]),
                reads=[pk], writes=[("post",)], tag=f"oe{ob}_{ch}")
            b.store(po_h[:, ob, ch * 512:(ch + 1) * 512], post[:, 0, :],
                    reads=[("post",)], tag=f"os{ob}_{ch}")
    b.finish()
    return b


def build_E1(bias=False):
    b = Builder("E3" if bias else "E1")
    Tk = b.T
    xl_h = b.dram_in("xl", [128, 8, TL], F32)
    p0_h = b.dram_in("p0", [128, 8, TL], F32)
    p1_h = b.dram_in("p1", [128, 8, TL], F32)
    x1_h = b.dram_out("x1", [128, 8, TL], F32)
    ln1_h = b.dram_out("ln1", [128, 8, TL], BF16)
    o_t = b.dram_tmp("o_t", [128, 8, TL], F32)
    if bias:
        bo_h = b.dram_in("bo", [8, 128], F32)
        boc = b.colset("e3_bo", bo_h, 8, "boc")

    xt = b.shared_xt()
    pt = [b.sb(f"e_p{i}", [128, ND, 512], F32) for i in range(2)]
    ot = b.sb("e_o", [128, ND, 512], F32)
    CH = 512
    nch = TL // CH
    for c in range(nch):
        pk0 = ("ep", c % 2)
        b.load(pt[c % 2][:], p0_h[:, :, c * CH:(c + 1) * CH], writes=[pk0],
               tag=f"o_ldp0{c}")
        for d in range(ND):
            Tk.op("vector", lambda v, c=c, d=d: v.tensor_copy(
                ot[:, d, :], pt[c % 2][:, d, :]),
                reads=[pk0], writes=[("eo", d)], tag=f"o1_{c}_{d}")
        b.load(pt[c % 2][:], p1_h[:, :, c * CH:(c + 1) * CH], writes=[pk0],
               tag=f"o_ldp1{c}")
        for d in range(ND):
            Tk.op("vector", lambda v, c=c, d=d: v.tensor_tensor(
                ot[:, d, :], ot[:, d, :], pt[c % 2][:, d, :], ALU.add),
                reads=[pk0, ("eo", d)], writes=[("eo", d)], tag=f"o2_{c}_{d}")
        if bias:
            for d in range(ND):
                Tk.op("vector", lambda v, c=c, d=d: v.tensor_scalar_add(
                    ot[:, d, :], ot[:, d, :], boc[d][:, 0:1]),
                    reads=[("boc",), ("eo", d)], writes=[("eo", d)],
                    tag=f"o3_{c}_{d}")
        b.store(o_t[:, :, c * CH:(c + 1) * CH], ot[:],
                reads=[("eo", d) for d in range(ND)], writes=[("ot_w",)],
                tag=f"o_st{c}")
    ab2, ab2k = emit_ln(b, "e1_ln", ("hbm", o_t, [("ot_w",)]), TL)
    for c in range(nch):
        xk = ("xts", c % 2)
        pk0 = ("ep", c % 2)
        b.load(xt[c % 2][:], xl_h[:, :, c * CH:(c + 1) * CH], writes=[xk],
               tag=f"x1_ldx{c}")
        b.load(pt[c % 2][:], o_t[:, :, c * CH:(c + 1) * CH], reads=[("ot_w",)],
               writes=[pk0], tag=f"x1_ldo{c}")
        pA, pAk, pB, pBk = ab_bcast(b, ab2, ab2k, c, CH, f"x1_bc{c}")
        for d in range(ND):
            hk = ("eo", d)
            Tk.op("vector", lambda v, c=c, d=d, pA=pA: v.tensor_tensor(
                ot[:, d, :], pt[c % 2][:, d, :], pA, ALU.mult),
                reads=[pk0, pAk], writes=[hk], tag=f"x1a_{c}_{d}")
            Tk.op("vector", lambda v, c=c, d=d, pB=pB: v.tensor_tensor(
                ot[:, d, :], ot[:, d, :], pB, ALU.subtract),
                reads=[hk, pBk], writes=[hk], tag=f"x1b_{c}_{d}")
            Tk.op("vector", lambda v, c=c, d=d: v.tensor_tensor(
                ot[:, d, :], ot[:, d, :], xt[c % 2][:, d, :], ALU.add),
                reads=[hk, xk], writes=[hk], tag=f"x1c_{c}_{d}")
        b.store(x1_h[:, :, c * CH:(c + 1) * CH], ot[:],
                reads=[("eo", d) for d in range(ND)], writes=[("x1out",)],
                tag=f"x1s{c}")
    emit_ln(b, "e1_ln3", ("hbm", x1_h, [("x1out",)]), TL, out_hbm=ln1_h)
    b.finish()
    return b


def prep_A1(ins):
    Wq = np.asarray(ins["L1_attn_Wq"])
    Wk = np.asarray(ins["L1_attn_Wk"])
    Wv = np.asarray(ins["L1_attn_Wv"])
    Wo = np.asarray(ins["L1_attn_Wo"])
    biases = np.asarray(ins["L1_attn_biases"])
    tau = np.asarray(ins["L1_attn_tau"])
    cos, sin = _rope_tables()
    pm = _pm_mat()
    ext = np.zeros((16, 1790 + 130), np.float32)
    ext[:, 638:638 + 513] = biases
    u = np.arange(1664)[None, :]
    p = np.arange(128)[:, None]
    idx = (u - p - 512) + 638
    band = np.ascontiguousarray(ext[:, idx])
    cmv = np.where((np.arange(1152)[None, :] - 512 - p) >= 0, 0.0, -1e9
                   ).astype(np.float32)
    invrow = (1.0 / np.arange(1, S_ + 1, dtype=np.float64)).astype(np.float32)
    maps = []
    for c in range(8):
        g = c % 2
        cols = slice(g * 512, (g + 1) * 512)
        m = {
            "wq": np.ascontiguousarray(
                Wq.T[:, cols].reshape(8, 128, 512).transpose(1, 0, 2)),
            "wk": np.ascontiguousarray(
                Wk.T[:, cols].reshape(8, 128, 512).transpose(1, 0, 2)),
            "wv": np.ascontiguousarray(
                Wv.T[:, cols].reshape(8, 128, 512).transpose(1, 0, 2)),
            "wo": np.ascontiguousarray(
                Wo.T[g * 512:(g + 1) * 512].reshape(4, 128, 8, 128)
                .transpose(1, 0, 2, 3)),
            "band": np.ascontiguousarray(band[g * 8:(g + 1) * 8]),
            "cm": cmv,
            "tau": np.ascontiguousarray(tau[g * 8:(g + 1) * 8].reshape(1, 8)),
            "invrow": invrow.reshape(1, S_),
            "cosT": cos, "sinT": sin, "pm": pm,
        }
        maps.append(m)
    return maps


def build_A1F(dbg=False, stop_after=99, nheads=8, race_detect=True):
    b = Builder("A1F", race_detect)
    Tk = b.T
    H, NS = 8, 32
    TBW = 256

    xf_h = b.dram_in("xf", [128, 8, S_], F32)
    wq_h = b.dram_in("wq", [128, 8, 512], F32)
    wk_h = b.dram_in("wk", [128, 8, 512], F32)
    wv_h = b.dram_in("wv", [128, 8, 512], F32)
    wo_h = b.dram_in("wo", [128, 4, 8, 128], F32)
    rw_h = b.dram_in("rw", [128, 8], F32)
    rb_h = b.dram_in("rb", [1, 1], F32)
    g1_h = b.dram_in("g1", [8, 128], F32)
    mT_h = b.dram_in("mT", [S_, S_], F32)
    posf_h = b.dram_in("posf", [1, S_], F32)
    if4_h = b.dram_in("if4", [1, 128], F32)
    pm_h = b.dram_in("pm", [64, 64], F32)
    ident_h = b.dram_in("ident", [128, 128], F32)
    po_h = b.dram_out("po", [128, 8, S_], F32)
    thr_h = b.dram_out("thr", [1, 1], F32)
    if dbg:
        et_o = b.dram_out("et_o", [128, 2, 256], BF16)
        ngc_o = b.dram_out("ngc_o", [128, 1], F32)
        hn_o = b.dram_out("hn_o", [128, 8, S_], BF16)
        att_o = b.dram_out("att_o", [128, 4, S_], BF16)
        kT_o = b.dram_out("kT_o", [64, S_], BF16)
        nmc_o = b.dram_out("nmc_o", [128, 32], F32)

    hn = b.sb("hn", [128, ND, S_], BF16)
    vhat = b.sb("vhat", [128, NS, H, 65], BF16)
    att = b.sb("att", [128, 4, S_], BF16)
    attv = att.bitcast(F32)
    b._sxt = [attv[:, 0:2, :].rearrange("p a (b c) -> p (a b) c", c=512),
              attv[:, 2:4, :].rearrange("p a (b c) -> p (a b) c", c=512)]
    sq = b.shared_sq()
    sqI = sq.bitcast(I32)
    mbuf = sq.bitcast(BF16).rearrange("p a (b c) -> p (a b) c", c=TBW)

    cosb = b.sb("cosb", [128, S_], BF16)
    sinb = b.sb("sinb", [128, S_], BF16)
    wqh = b.sb("s_wqh", [128, 2, ND, 64], BF16)
    wv = b.sb("s_wv", [128, ND, 512], BF16)
    wo = wv.reshape([128, 4, 8, 128])
    pm = b.sb("s_pm", [64, 64], BF16)
    ident = b.sb("s_id", [128, 128], F32)
    rw = b.sb("s_rw", [128, ND], F32)
    rb = b.sb("s_rb", [1, 1], F32)
    kT = b.sb("kT", [65, S_], BF16)
    qT = b.sb("qT", [65, S_], BF16)
    zc0 = b.constcol(0.0)
    et = b.sb("et", [128, 2, TBW], BF16)
    rz = b.sb("rz", [1, TBW], F32)
    rzb = b.sb("rzb", [64, TBW], BF16)
    t64a = b.sb("t64a", [64, 512], BF16)
    t64b = b.sb("t64b", [64, 512], BF16)
    sc1 = b.sb("sc1", [64, 1], F32)
    sc2 = b.sb("sc2", [1, 4], F32)
    negc = b.sb("negc", [128, 1], F32)
    post = sq[:, 4:5, :]
    u32 = sq[:, :, 0:128]
    wcol = b.sb("wcol", [128, NS], F32)
    nmc = b.sb("nmc", [128, NS], F32)
    srow = b.sb("srow", [1, 8], F32)
    scol = b.sb("scol", [128, 2], F32)
    s32 = b.sb("s32", [32, 4], F32)
    prow = b.sb("prow", [1, 640], F32)

    ones_c = b.constrow(1.0, 1, F32, parts=128)
    onesck = b.kr(1.0, 1, F32, 128)
    ones_r = b.constrow(1.0, 128, F32)
    onesrk = b.kr(1.0, 128, F32)

    for nmh, dst, eng in (("pm", pm, "gpsimd"), ("ident", ident, "sync"),
                          ("rb", rb, "sync")):
        b.load(dst[:], b.inp_ap[nmh], writes=[(nmh,)], eng=eng, tag=f"ld{nmh}")
    b.load(rw[:], rw_h, writes=[("rw",)], tag="ld_rw")
    gcols = b.colset("s_g1c", g1_h, 8, "g1")

    ab0, ab0k = emit_ln(b, "af_ln", ("hbm", xf_h, []), S_)

    xt = b.shared_xt()
    for t in range(NS):
        c, off = t // 4, (t % 4) * 128
        xk = ("xts", c % 2)
        if off == 0:
            b.load(xt[c % 2][:], xf_h[:, :, c * 512:(c + 1) * 512], writes=[xk],
                   tag=f"r_ldx{c}")
            bc = ab_bcast(b, ab0, ab0k, c, 512, f"r_bc{c}")
        pA, pAk, pB, pBk = bc
        for d in range(ND):
            Tk.op("vector", lambda v, d=d, off=off, c=c, pA=pA: v.tensor_tensor(
                u32[:, d, :], xt[c % 2][:, d, off:off + 128], pA[:, off:off + 128],
                ALU.mult), reads=[xk, pAk], writes=[("u32", d)], tag=f"r_u1_{t}_{d}")
            Tk.op("vector", lambda v, d=d, off=off, pB=pB: v.tensor_tensor(
                u32[:, d, :], u32[:, d, :], pB[:, off:off + 128], ALU.subtract),
                reads=[("u32", d), pBk], writes=[("u32", d)], tag=f"r_u2_{t}_{d}")
        for d in range(ND):
            Tk.op("tensor", lambda t_, d=d: t_.matmul(
                b.pb[6][0:128, 0:1], u32[:, d, :], rw[:, d:d + 1],
                start=(d == 0), stop=False),
                reads=[("u32", d), ("rw",)], writes=[("psum", 6)], tag=f"r_mm{t}_{d}")
        Tk.op("tensor", lambda t_: t_.matmul(
            b.pb[6][0:128, 0:1], ones_r[0:1, :], rb[0:1, :], start=False, stop=True),
            reads=[("rb",), onesrk], writes=[("psum", 6)], tag=f"r_mb{t}")
        Tk.op("vector", lambda v, t=t: v.tensor_copy(
            wcol[:, t:t + 1], b.pb[6][0:128, 0:1]),
            reads=[("psum", 6)], writes=[("wcol",)], tag=f"r_cp{t}")

    def pmax(dst_slice, srct, srckeys, tg):
        Tk.op("tensor", lambda t_: t_.transpose(
            b.pb[7][0:NS, 0:128], srct[:], ident[:]),
            reads=srckeys + [("ident",)], writes=[("psum", 7)], tag=tg + "t")
        Tk.op("vector", lambda v: v.reduce_max(
            s32[:, 0:1], b.pb[7][0:NS, 0:128], axis=AX.X),
            reads=[("psum", 7)], writes=[("s32",)], tag=tg + "r")
        Tk.op("tensor", lambda t_: t_.transpose(
            b.pb[7][0:1, 0:NS], s32[0:NS, 0:1], ident[0:NS, 0:NS]),
            reads=[("s32",), ("ident",)], writes=[("psum", 7)], tag=tg + "t2")
        Tk.op("vector", lambda v, dst_slice=dst_slice: v.reduce_max(
            srow[0:1, dst_slice:dst_slice + 1], b.pb[7][0:1, 0:NS], axis=AX.X),
            reads=[("psum", 7)], writes=[("srow",)], tag=tg + "r2")

    pmax(1, wcol, [("wcol",)], "hi")
    Tk.op("vector", lambda v: v.tensor_scalar_mul(nmc[:], wcol[:], -1.0),
          reads=[("wcol",)], writes=[("nmc",)], tag="negw")
    pmax(0, nmc, [("nmc",)], "lo")
    Tk.op("vector", lambda v: v.tensor_scalar(
        srow[0:1, 0:1], srow[0:1, 0:1], -1.0, -1.0, ALU.mult, ALU.add),
        reads=[("srow",)], writes=[("srow",)], tag="lofix")
    for it in range(46):
        Tk.op("vector", lambda v: v.tensor_tensor(
            srow[0:1, 2:3], srow[0:1, 0:1], srow[0:1, 1:2], ALU.add),
            reads=[("srow",)], writes=[("srow",)], tag=f"bi_m{it}")
        Tk.op("vector", lambda v: v.tensor_scalar_mul(
            srow[0:1, 2:3], srow[0:1, 2:3], 0.5),
            reads=[("srow",)], writes=[("srow",)], tag=f"bi_m2{it}")
        Tk.op("tensor", lambda t_: t_.matmul(
            b.pb[4][:, 0:1], ones_r[0:1, :], srow[0:1, 2:3], start=True, stop=True),
            reads=[("srow",), onesrk], writes=[("psum", 4)], tag=f"bi_bc{it}")
        Tk.op("vector", lambda v: v.tensor_copy(scol[:, 0:1], b.pb[4][:, 0:1]),
              reads=[("psum", 4)], writes=[("scol",)], tag=f"bi_cc{it}")
        Tk.op("vector", lambda v: v.tensor_single_scalar(
            nmc[:], wcol[:], scol[:, 0:1], ALU.is_gt),
            reads=[("wcol",), ("scol",)], writes=[("nmc",)], tag=f"bi_gt{it}")
        Tk.op("vector", lambda v: v.reduce_sum(scol[:, 1:2], nmc[:], axis=AX.X),
              reads=[("nmc",)], writes=[("scol",)], tag=f"bi_rs{it}")
        Tk.op("tensor", lambda t_: t_.matmul(
            b.pb[4][0:1, 0:1], scol[:, 1:2], ones_c[:, 0:1], start=True, stop=True),
            reads=[("scol",), onesck], writes=[("psum", 4)], tag=f"bi_cnt{it}")
        Tk.op("vector", lambda v: v.tensor_copy(srow[0:1, 3:4], b.pb[4][0:1, 0:1]),
              reads=[("psum", 4)], writes=[("srow",)], tag=f"bi_cn{it}")
        Tk.op("vector", lambda v: v.tensor_single_scalar(
            srow[0:1, 4:5], srow[0:1, 3:4], 511.5, ALU.is_ge),
            reads=[("srow",)], writes=[("srow",)], tag=f"bi_sel{it}")
        Tk.op("vector", lambda v: v.tensor_sub(
            srow[0:1, 6:7], srow[0:1, 2:3], srow[0:1, 0:1]),
            reads=[("srow",)], writes=[("srow",)], tag=f"bi_d1{it}")
        Tk.op("vector", lambda v: v.tensor_mul(
            srow[0:1, 6:7], srow[0:1, 6:7], srow[0:1, 4:5]),
            reads=[("srow",)], writes=[("srow",)], tag=f"bi_d1b{it}")
        Tk.op("vector", lambda v: v.tensor_tensor(
            srow[0:1, 0:1], srow[0:1, 0:1], srow[0:1, 6:7], ALU.add),
            reads=[("srow",)], writes=[("srow",)], tag=f"bi_lo{it}")
        Tk.op("vector", lambda v: v.tensor_sub(
            srow[0:1, 7:8], srow[0:1, 1:2], srow[0:1, 2:3]),
            reads=[("srow",)], writes=[("srow",)], tag=f"bi_d2{it}")
        Tk.op("vector", lambda v: v.tensor_mul(
            srow[0:1, 7:8], srow[0:1, 7:8], srow[0:1, 4:5]),
            reads=[("srow",)], writes=[("srow",)], tag=f"bi_d2b{it}")
        Tk.op("vector", lambda v: v.tensor_tensor(
            srow[0:1, 1:2], srow[0:1, 2:3], srow[0:1, 7:8], ALU.add),
            reads=[("srow",)], writes=[("srow",)], tag=f"bi_hi{it}")
    Tk.op("tensor", lambda t_: t_.matmul(
        b.pb[4][:, 0:1], ones_r[0:1, :], srow[0:1, 1:2], start=True, stop=True),
        reads=[("srow",), onesrk], writes=[("psum", 4)], tag="th_bc")
    Tk.op("vector", lambda v: v.tensor_copy(scol[:, 0:1], b.pb[4][:, 0:1]),
          reads=[("psum", 4)], writes=[("scol",)], tag="th_cc")
    Tk.op("vector", lambda v: v.tensor_single_scalar(
        nmc[:], wcol[:], scol[:, 0:1], ALU.is_gt),
        reads=[("wcol",), ("scol",)], writes=[("nmc",)], tag="th_gt")
    Tk.op("vector", lambda v: v.scalar_tensor_tensor(
        nmc[:], nmc[:], -1e30, wcol[:], ALU.mult, ALU.add),
        reads=[("nmc",), ("wcol",)], writes=[("nmc",)], tag="th_msk")
    pmax(5, nmc, [("nmc",)], "th")
    b.store(thr_h, srow[0:1, 5:6], reads=[("srow",)], tag="st_thr")
    Tk.op("tensor", lambda t_: t_.matmul(
        b.pb[4][:, 0:1], ones_r[0:1, :], srow[0:1, 5:6], start=True, stop=True),
        reads=[("srow",), onesrk], writes=[("psum", 4)], tag="nm_bc")
    Tk.op("vector", lambda v: v.tensor_copy(scol[:, 0:1], b.pb[4][:, 0:1]),
          reads=[("psum", 4)], writes=[("scol",)], tag="nm_cc")
    Tk.op("vector", lambda v: v.tensor_single_scalar(
        nmc[:], wcol[:], scol[:, 0:1], ALU.is_gt),
        reads=[("wcol",), ("scol",)], writes=[("nmc",)], tag="nm_gt")
    Tk.op("vector", lambda v: v.tensor_scalar(
        nmc[:], nmc[:], 1e9, -1e9, ALU.mult, ALU.add),
        reads=[("nmc",)], writes=[("nmc",)], tag="nm_sc")

    if stop_after < 3:
        b.finish()
        return b
    b.load(prow[0:1, 512:640], if4_h, writes=[("if4",)], tag="ld_if4")
    TWO_PI = 6.283185307179586
    rpf, rpg = sq[:, 0, :], sq[:, 1, :]
    rpi = sqI[:, 2, :]
    for half, dstt, dk in ((0, cosb, ("cos",)), (1, sinb, ("sin",))):
        for ch in range(8):
            b.load(prow[0:1, 0:512], posf_h[0:1, ch * 512:(ch + 1) * 512],
                   writes=[("posc",)], tag=f"ldpos{half}_{ch}")
            pk = ("psum", 2)
            Tk.op("tensor", lambda t_, ch=ch: t_.matmul(
                b.pb[2][:, 0:512], prow[0:1, 512:640], prow[0:1, 0:512],
                start=True, stop=True),
                reads=[("posc",), ("if4",)], writes=[pk], tag=f"rp_f{half}_{ch}")
            fk = ("sqs",)
            Tk.op("vector", lambda v, half=half: v.tensor_scalar(
                rpf, b.pb[2][:, 0:512], 1.0, (TWO_PI / 4) if half == 0 else 0.0,
                ALU.mult, ALU.add),
                reads=[pk, ("nmc",)] + [("u32", d) for d in range(ND)],
                writes=[fk], tag=f"rp_c{half}_{ch}")
            Tk.op("vector", lambda v: v.tensor_scalar(
                rpg, rpf, 1.0 / TWO_PI, 0.5, ALU.mult, ALU.add),
                reads=[fk], writes=[fk], tag=f"rp_d{half}_{ch}")
            Tk.op("vector", lambda v: v.tensor_copy(rpi, rpg),
                  reads=[fk], writes=[fk], tag=f"rp_t{half}_{ch}")
            Tk.op("vector", lambda v: v.tensor_copy(rpg, rpi),
                  reads=[fk], writes=[fk], tag=f"rp_t2{half}_{ch}")
            Tk.op("vector", lambda v: v.scalar_tensor_tensor(
                rpf, rpg, -TWO_PI, rpf, ALU.mult, ALU.add),
                reads=[fk], writes=[fk], tag=f"rp_r{half}_{ch}")
            Tk.op("scalar", lambda s, dstt=dstt, ch=ch: s.activation(
                dstt[:, ch * 512:(ch + 1) * 512], rpf, AF.Sin),
                reads=[fk], writes=[dk], tag=f"rp_s{half}_{ch}")

    b.fence("f_a1f_p4")
    rws = b.shared_lnrows()
    abr = b.shared_abrow()
    for c in range(8):
        xk = ("xts", c % 2)
        b.load(xt[c % 2][:], xf_h[:, :, c * 512:(c + 1) * 512], writes=[xk],
               tag=f"h_ldx{c}")
        b.load(abr[:, 0:512], ab0[0:1, c * 512:(c + 1) * 512], reads=[ab0k],
               writes=[("abr",)], tag=f"h_la{c}")
        b.load(abr[:, 512:1024], ab0[1:2, c * 512:(c + 1) * 512], reads=[ab0k],
               writes=[("abr",)], tag=f"h_lb{c}")
        Tk.op("scalar", lambda s: s.activation(rws[0][:], abr[0:1, 0:512], AF.Square),
              reads=[("abr",)], writes=[("lnr", 0)], tag=f"h_a2{c}")
        Tk.op("vector", lambda v: v.tensor_scalar(
            rws[0][:], rws[0][:], -1e-5, 1.0 + 1e-6, ALU.mult, ALU.add),
            reads=[("lnr", 0)], writes=[("lnr", 0)], tag=f"h_m{c}")
        Tk.op("scalar", lambda s: s.activation(rws[0][:], rws[0][:], AF.Ln),
              reads=[("lnr", 0)], writes=[("lnr", 0)], tag=f"h_l{c}")
        Tk.op("scalar", lambda s: s.activation(rws[0][:], rws[0][:], AF.Exp,
                                               scale=-0.5),
              reads=[("lnr", 0)], writes=[("lnr", 0)], tag=f"h_x{c}")
        Tk.op("vector", lambda v: v.tensor_tensor(
            rws[1][:], abr[0:1, 0:512], rws[0][:], ALU.mult),
            reads=[("abr",), ("lnr", 0)], writes=[("lnr", 1)], tag=f"h_A{c}")
        Tk.op("vector", lambda v: v.tensor_tensor(
            rws[2][:], abr[0:1, 512:1024], rws[0][:], ALU.mult),
            reads=[("abr",), ("lnr", 0)], writes=[("lnr", 2)], tag=f"h_B{c}")
        pA, pAk = bcast_row(b, 4, rws[1][:], [("lnr", 1)], 512, f"h_bca{c}")
        pB, pBk = bcast_row(b, 5, rws[2][:], [("lnr", 2)], 512, f"h_bcb{c}")
        for d in range(ND):
            tk = ("sqs",)
            Tk.op("vector", lambda v, c=c, d=d, pA=pA: v.tensor_tensor(
                sq[:, d, :], xt[c % 2][:, d, :], pA, ALU.mult),
                reads=[xk, pAk], writes=[tk], tag=f"h1_{c}_{d}")
            Tk.op("vector", lambda v, c=c, d=d, pB=pB: v.tensor_tensor(
                sq[:, d, :], sq[:, d, :], pB, ALU.subtract),
                reads=[tk, pBk], writes=[tk], tag=f"h2_{c}_{d}")
            Tk.op("vector", lambda v, c=c, d=d: v.tensor_scalar_mul(
                hn[:, d, c * 512:(c + 1) * 512], sq[:, d, :],
                gcols[d][:, 0:1]),
                reads=[tk, ("g1",)], writes=[("hn", c)], tag=f"h3_{c}_{d}")

    hn_keys = [("hn", c) for c in range(8)]
    if stop_after < 5:
        b.finish()
        return b
    b.load(wv[:], wv_h, writes=[("wv",)], eng="gpsimd", tag="ld_wv")
    Tk.op("vector", lambda v: v.memset(vhat[:, :, :, 64:65], 1.0),
          reads=hn_keys, writes=[("vhat_ones",)], tag="vhat_ones")
    for st in range(NS):
        pk = ("psum", st % 2)
        for d in range(ND):
            Tk.op("tensor", lambda t, st=st, d=d: t.matmul(
                b.pb[st % 2][:, 0:512], hn[:, d, st * 128:(st + 1) * 128],
                wv[:, d, :], start=(d == 0), stop=(d == ND - 1)),
                reads=[("hn", st // 4), ("wv",), ("vhat_ones",)], writes=[pk],
                tag=f"vp{st}")
        Tk.op("scalar", lambda s, st=st: s.activation(
            vhat[:, st, :, 0:64], b.pb[st % 2][:, 0:512], AF.Copy),
            reads=[pk], writes=[("vhat", st)], tag=f"vev{st}")

    if stop_after < 6:
        b.finish()
        return b
    b.fence("f_a1f_heads")
    for h in range(nheads if stop_after >= 7 else 1):
        hb, hh = h // 2, h % 2
        for wi, (which, dst, w_src) in enumerate((("k", kT, wk_h), ("q", qT, wq_h))):
            par = (2 * h + wi) % 2
            wkey = ("wqh", par)
            b.load(wqh[:, par, :, :], w_src[:, :, h * 64:(h + 1) * 64],
                   writes=[wkey], eng="gpsimd", tag=f"ldw{which}{h}")
            for ch in range(8):
                pk = ("psum", 3)
                for d in range(ND):
                    Tk.op("tensor", lambda t, d=d, ch=ch, par=par: t.matmul(
                        b.pb[3][0:64, 0:512], wqh[:, par, d, :],
                        hn[:, d, ch * 512:(ch + 1) * 512],
                        start=(d == 0), stop=(d == ND - 1)),
                        reads=[("hn", ch), wkey], writes=[pk], tag=f"{which}p{h}_{ch}")
                rawk = (which + "T", ch)
                if which == "q":
                    Tk.op("scalar", lambda s, ch=ch, dst=dst: s.mul(
                        dst[0:64, ch * 512:(ch + 1) * 512], b.pb[3][0:64, 0:512],
                        0.125),
                        reads=[pk], writes=[rawk], tag=f"{which}e{h}_{ch}")
                else:
                    Tk.op("scalar", lambda s, ch=ch, dst=dst: s.activation(
                        dst[0:64, ch * 512:(ch + 1) * 512], b.pb[3][0:64, 0:512],
                        AF.Copy),
                        reads=[pk], writes=[rawk], tag=f"{which}e{h}_{ch}")
                pk2 = ("psum", 2)
                Tk.op("tensor", lambda t, ch=ch, dst=dst: t.matmul(
                    b.pb[2][0:64, 0:512], pm[:, :], dst[0:64, ch * 512:(ch + 1) * 512],
                    start=True, stop=True),
                    reads=[rawk, ("pm",)], writes=[pk2], tag=f"{which}sw{h}_{ch}")
                Tk.op("vector", lambda v, ch=ch, dst=dst: v.tensor_tensor(
                    t64a[:], dst[0:64, ch * 512:(ch + 1) * 512],
                    cosb[0:64, ch * 512:(ch + 1) * 512], ALU.mult),
                    reads=[rawk, ("cos",)], writes=[("t64a",)], tag=f"{which}r1{h}_{ch}")
                Tk.op("vector", lambda v, ch=ch: v.tensor_tensor(
                    t64b[:], b.pb[2][0:64, 0:512],
                    sinb[0:64, ch * 512:(ch + 1) * 512], ALU.mult),
                    reads=[pk2, ("sin",)], writes=[("t64b",)], tag=f"{which}r2{h}_{ch}")
                Tk.op("vector", lambda v, ch=ch, dst=dst: v.tensor_tensor(
                    dst[0:64, ch * 512:(ch + 1) * 512], t64a[:], t64b[:], ALU.add),
                    reads=[("t64a",), ("t64b",)], writes=[rawk], tag=f"{which}r3{h}_{ch}")
        qkeys = [("qT", ch) for ch in range(8)]
        kkeys = [("kT", ch) for ch in range(8)]
        emit_qk_bound(b, f"qb{h}_", qT, kT, qkeys, kkeys, t64a)


        for tb in range(S_ // TBW):
            t0 = tb * TBW
            if h % 2 == 0 or True:
                pass
            SCB = (0, 1, 6, 7)
            for st in range(NS):
                bi = SCB[st % 4]
                sck = ("psum", bi)
                Tk.op("tensor", lambda t, st=st, t0=t0, bi=bi: t.matmul(
                    b.pb[bi][:, 0:TBW], kT[0:65, st * 128:(st + 1) * 128],
                    qT[0:65, t0:t0 + TBW], start=True, stop=True),
                    reads=kkeys + qkeys + [("kaug",), ("qaug",)], writes=[sck], tag=f"sc{h}_{tb}_{st}")
                if h == 0:
                    b.load(mbuf[:, st, :], mT_h[st * 128:(st + 1) * 128, t0:t0 + TBW],
                           reads=[("sqs",)], writes=[("mb", st)], eng="gpsimd",
                           tag=f"ldm{tb}_{st}")
                Tk.op("vector", lambda v, st=st, bi=bi: v.tensor_tensor(
                    b.pb[bi][:, 0:TBW], b.pb[bi][:, 0:TBW], mbuf[:, st, :],
                    ALU.add), reads=[sck, ("mb", st)], writes=[sck],
                    tag=f"ma{h}_{tb}_{st}")
                Tk.op("scalar", lambda s, st=st, bi=bi: s.activation(
                    et[:, st % 2, :], b.pb[bi][:, 0:TBW], AF.Exp,
                    bias=nmc[:, st:st + 1], scale=1.0),
                    reads=[sck, ("nmc",)], writes=[("et", st % 2)],
                    tag=f"ex{h}_{tb}_{st}")
                Tk.op("tensor", lambda t, st=st, h=h: t.matmul(
                    b.pb[2][0:65, 0:TBW], vhat[:, st, h, :], et[:, st % 2, :],
                    start=(st == 0), stop=(st == NS - 1)),
                    reads=[("et", st % 2), ("vhat", st), ("vhat_ones",)],
                    writes=[("psum", 2)], tag=f"av{h}_{tb}_{st}")
            Tk.op("vector", lambda v: v.tensor_scalar_max(
                rz[0:1, :], b.pb[2][64:65, 0:TBW], 1e-30),
                reads=[("psum", 2)], writes=[("rz",)], tag=f"zg{h}_{tb}")
            Tk.op("vector", lambda v: v.reciprocal(rz[0:1, :], rz[0:1, :]),
                  reads=[("rz",)], writes=[("rz",)], tag=f"rcp{h}_{tb}")
            Tk.op("tensor", lambda t: t.matmul(
                b.pb[3][0:64, 0:TBW], ones_r[0:1, 0:64], rz[0:1, :],
                start=True, stop=True),
                reads=[("rz",), onesrk], writes=[("psum", 3)], tag=f"rzb{h}_{tb}")
            Tk.op("vector", lambda v: v.tensor_copy(rzb[:], b.pb[3][0:64, 0:TBW]),
                  reads=[("psum", 3)], writes=[("rzb",)], tag=f"rzc{h}_{tb}")
            if hh == 0:
                Tk.op("vector", lambda v, hb=hb, t0=t0: v.tensor_tensor(
                    att[0:64, hb, t0:t0 + TBW], b.pb[2][0:64, 0:TBW], rzb[:],
                    ALU.mult),
                    reads=[("psum", 2), ("rzb",)], writes=[("att", hb, 0, tb // 2)],
                    tag=f"nrm{h}_{tb}")
            else:
                Tk.op("vector", lambda v, t0=t0: v.tensor_tensor(
                    t64a[0:64, 0:TBW], b.pb[2][0:64, 0:TBW], rzb[:], ALU.mult),
                    reads=[("psum", 2), ("rzb",)], writes=[("t64a",)],
                    tag=f"nrm{h}_{tb}")
                Tk.op("scalar", lambda s, hb=hb, t0=t0: s.activation(
                    att[64:128, hb, t0:t0 + TBW], t64a[0:64, 0:TBW], AF.Copy),
                    reads=[("t64a",)], writes=[("att", hb, 1, tb // 2)],
                    tag=f"nrmc{h}_{tb}")

    if dbg:
        b.store(et_o, et[:], reads=[("et", 0), ("et", 1)], tag="dbg_et")
        b.store(ngc_o, negc[:], reads=[("negc",)], tag="dbg_ngc")
        b.store(hn_o, hn[:], reads=hn_keys, tag="dbg_hn")
        b.store(att_o, att[:], reads=[("att", hb_, half_, ch_) for hb_ in range(4)
                                      for half_ in range(2) for ch_ in range(8)],
                tag="dbg_att")
        b.store(kT_o, kT[:], reads=[("kT", ch_) for ch_ in range(8)], tag="dbg_kT")
        b.store(nmc_o, nmc[:], reads=[("nmc",)], tag="dbg_nmc")
    b.fence("f_a1f_oproj")
    b.load(wo[:], wo_h, writes=[("wo",), ("wv",)], eng="gpsimd", tag="ld_wo")
    for ob in range(8):
        for ch in range(8):
            pk = ("psum", (ob * 8 + ch) % 2)
            for hb in range(4):
                Tk.op("tensor", lambda t, ob=ob, ch=ch, hb=hb, pk=pk: t.matmul(
                    b.pb[pk[1]][:, 0:512], wo[:, hb, ob, :],
                    att[:, hb, ch * 512:(ch + 1) * 512],
                    start=(hb == 0), stop=(hb == 3)),
                    reads=[("wo",), ("att", hb, 0, ch), ("att", hb, 1, ch)],
                    writes=[pk], tag=f"op{ob}_{ch}_{hb}")
            Tk.op("vector", lambda v, ch=ch, pk=pk: v.tensor_copy(
                post[:, 0, :], b.pb[pk[1]][:, 0:512]),
                reads=[pk], writes=[("post",)], tag=f"oe{ob}_{ch}")
            b.store(po_h[:, ob, ch * 512:(ch + 1) * 512], post[:, 0, :],
                    reads=[("post",)], tag=f"os{ob}_{ch}")
    b.finish()
    return b


def build_E1F():
    b = Builder("E1F")
    Tk = b.T
    NF = 32
    TH = TL // 2

    xl_h = b.dram_in("xl", [128, ND, TL], F32)
    p0_h = b.dram_in("p0", [128, ND, TL], F32)
    p1_h = b.dram_in("p1", [128, ND, TL], F32)
    thr_h = b.dram_in("thr", [1, 1], F32)
    rw_h = b.dram_in("rw", [128, 8], F32)
    rb_h = b.dram_in("rb", [1, 1], F32)
    g2_h = b.dram_in("g2", [8, 128], F32)
    w1_h = b.dram_in("w1", [128, ND, 4096], F32)
    w2_h = b.dram_in("w2", [128, 32, 1024], F32)
    ident_h = b.dram_in("ident", [128, 128], F32)
    xn_h = b.dram_out("xn", [128, ND, TL], F32)
    lnn_h = b.dram_out("lnn", [128, ND, TL], BF16)
    x1_t = b.dram_tmp("x1_t", [128, ND, TL], F32)
    f_t = b.dram_tmp("f_t", [128, ND, TL], F32)
    o_t = b.dram_tmp("o_t", [128, ND, TL], F32)
    rows_t = b.dram_tmp("rows_t", [2, 16, 128], F32)

    hn2 = b.sb("hn2", [128, ND, TL], BF16)
    h1h = b.sb("h1h", [128, NF, TH], BF16)
    h1v = h1h.bitcast(F32)
    b._sxt = [h1v[:, 0:8, :], h1v[:, 8:16, :]]
    b._ssq = h1v[:, 16:24, :]
    pt = h1v[:, 24:32, :]
    w1r = [b.sb(f"w1r{i}", [128, ND, 128], BF16) for i in range(2)]
    w2r = [b.sb(f"w2r{i}", [128, NF, 128], BF16) for i in range(2)]
    ident = b.sb("s_id", [128, 128], F32)
    rw = b.sb("s_rw", [128, ND], F32)
    rb = b.sb("s_rb", [1, 1], F32)
    thrc = b.sb("thrc", [128, 1], F32)
    u32 = b.shared_sq()[:, :, 0:128]
    wcol = b.sb("wcol", [128, 16], F32)
    scol = b.sb("scol", [128, 2], F32)
    s16 = b.sb("s16", [16, 128], F32)
    wrow = b.sb("wrow", [1, 1024], F32)
    ot2 = b.sb("ot2", [128, ND, 512], F32)

    ones_r = b.constrow(1.0, 128, F32)
    onesrk = b.kr(1.0, 128, F32)
    sq = b.shared_sq()
    xt = b.shared_xt()

    for nmh, dst in (("ident", ident), ("rb", rb)):
        b.load(dst[:], b.inp_ap[nmh], writes=[(nmh,)], tag=f"ld{nmh}")
    b.load(rw[:], rw_h, writes=[("rw",)], tag="ld_rw")
    gcols = b.colset("s_g2c", g2_h, 8, "g2")

    abu, abuk = emit_ln(b, "ef_ln", ("hbm", xl_h, []), TL)
    for t in range(16):
        c, off = t // 4, (t % 4) * 128
        xk = ("xts", c % 2)
        if off == 0:
            b.load(xt[c % 2][:], xl_h[:, :, c * 512:(c + 1) * 512], writes=[xk],
                   tag=f"r_ldx{c}")
            bc = ab_bcast(b, abu, abuk, c, 512, f"r_bc{c}")
        pA, pAk, pB, pBk = bc
        for d in range(ND):
            Tk.op("vector", lambda v, d=d, off=off, c=c, pA=pA: v.tensor_tensor(
                u32[:, d, :], xt[c % 2][:, d, off:off + 128], pA[:, off:off + 128],
                ALU.mult), reads=[xk, pAk], writes=[("u32", d)], tag=f"r_u1_{t}_{d}")
            Tk.op("vector", lambda v, d=d, off=off, pB=pB: v.tensor_tensor(
                u32[:, d, :], u32[:, d, :], pB[:, off:off + 128], ALU.subtract),
                reads=[("u32", d), pBk], writes=[("u32", d)], tag=f"r_u2_{t}_{d}")
        for d in range(ND):
            Tk.op("tensor", lambda t_, d=d: t_.matmul(
                b.pb[6][0:128, 0:1], u32[:, d, :], rw[:, d:d + 1],
                start=(d == 0), stop=False),
                reads=[("u32", d), ("rw",)], writes=[("psum", 6)], tag=f"r_mm{t}_{d}")
        Tk.op("tensor", lambda t_: t_.matmul(
            b.pb[6][0:128, 0:1], ones_r[0:1, :], rb[0:1, :], start=False, stop=True),
            reads=[("rb",), onesrk], writes=[("psum", 6)], tag=f"r_mb{t}")
        Tk.op("vector", lambda v, t=t: v.tensor_copy(
            wcol[:, t:t + 1], b.pb[6][0:128, 0:1]),
            reads=[("psum", 6)], writes=[("wcol",)], tag=f"r_cp{t}")
    b.load(thrc[0:1, 0:1], thr_h, writes=[("thr",)], tag="ld_thr")
    Tk.op("tensor", lambda t_: t_.matmul(
        b.pb[4][:, 0:1], ones_r[0:1, :], thrc[0:1, 0:1], start=True, stop=True),
        reads=[("thr",), onesrk], writes=[("psum", 4)], tag="th_bc")
    Tk.op("vector", lambda v: v.tensor_copy(thrc[:], b.pb[4][:, 0:1]),
          reads=[("psum", 4)], writes=[("thr",)], tag="th_cc")
    selc = b.sb("selc", [128, 16], F32)
    Tk.op("vector", lambda v: v.tensor_single_scalar(
        selc[:], wcol[:], thrc[:, 0:1], ALU.is_gt),
        reads=[("wcol",), ("thr",)], writes=[("selc",)], tag="selc")
    Tk.op("vector", lambda v: v.tensor_mul(wcol[:], wcol[:], selc[:]),
          reads=[("wcol",), ("selc",)], writes=[("wcol",)], tag="wsel")
    Tk.op("vector", lambda v: v.tensor_scalar(
        selc[:], selc[:], -1.0, 1.0, ALU.mult, ALU.add),
        reads=[("selc",)], writes=[("selc",)], tag="isel")
    for j, (srct, kj) in enumerate(((wcol, ("wcol",)), (selc, ("selc",)))):
        Tk.op("tensor", lambda t_, srct=srct, kj=kj: t_.transpose(
            b.pb[7][0:16, 0:128], srct[:], ident[:]),
            reads=[kj, ("ident",)], writes=[("psum", 7)], tag=f"rt{j}")
        Tk.op("vector", lambda v: v.tensor_copy(s16[:], b.pb[7][0:16, 0:128]),
              reads=[("psum", 7)], writes=[("s16",)], tag=f"rtc{j}")
        b.store(rows_t[j], s16[:], reads=[("s16",)], writes=[("rowst",)],
                tag=f"rts{j}")

    ptl = [b.sb(f"e_pt{i}", [128, ND, 512], F32) for i in range(2)]
    for c in range(TL // 512):
        xk = ("xts", c % 2)
        pk0 = ("ep", c % 2)
        b.load(xt[c % 2][:], xl_h[:, :, c * 512:(c + 1) * 512], writes=[xk],
               tag=f"x1_ldx{c}")
        b.load(ptl[c % 2][:], p0_h[:, :, c * 512:(c + 1) * 512], writes=[pk0],
               tag=f"x1_ldp0{c}")
        pA, pAk, pB, pBk = ab_bcast(b, abu, abuk, c, 512, f"x1_bc{c}")
        for d in range(ND):
            hk = ("eo", d)
            Tk.op("vector", lambda v, c=c, d=d, pA=pA: v.tensor_tensor(
                ot2[:, d, :], xt[c % 2][:, d, :], pA, ALU.mult),
                reads=[xk, pAk], writes=[hk], tag=f"x1a_{c}_{d}")
            Tk.op("vector", lambda v, c=c, d=d, pB=pB: v.tensor_tensor(
                ot2[:, d, :], ot2[:, d, :], pB, ALU.subtract),
                reads=[hk, pBk], writes=[hk], tag=f"x1b_{c}_{d}")
            Tk.op("vector", lambda v, c=c, d=d: v.tensor_tensor(
                ot2[:, d, :], ot2[:, d, :], ptl[c % 2][:, d, :], ALU.add),
                reads=[hk, pk0], writes=[hk], tag=f"x1c_{c}_{d}")
        b.load(ptl[c % 2][:], p1_h[:, :, c * 512:(c + 1) * 512], writes=[pk0],
               tag=f"x1_ldp1{c}")
        for d in range(ND):
            hk = ("eo", d)
            Tk.op("vector", lambda v, c=c, d=d: v.tensor_tensor(
                ot2[:, d, :], ot2[:, d, :], ptl[c % 2][:, d, :], ALU.add),
                reads=[hk, pk0], writes=[hk], tag=f"x1d_{c}_{d}")
        b.store(x1_t[:, :, c * 512:(c + 1) * 512], ot2[:],
                reads=[("eo", d) for d in range(ND)], writes=[("x1t",)],
                tag=f"x1s{c}")

    zcols = [b.constcol(0.0)] * ND
    emit_ln(b, "ef_rms", ("hbm", x1_t, [("x1t",)]), TL, eps=1e-6, kind="rms",
            out=(hn2, lambda c, d: ("hn2", c)),
            gamma_beta=(gcols, zcols, [("g2",), b.kc(0.0)]))

    b.fence("f_e1f_p5")
    hn2_keys = [("hn2", c) for c in range(TL // 512)]
    for half in range(2):
        hsl = slice(half * TH, (half + 1) * TH)
        for ffb in range(NF):
            wk = ("w1r", ffb % 2)
            b.load(w1r[ffb % 2][:], w1_h[:, :, ffb * 128:(ffb + 1) * 128],
                   reads=[("x1t",)], writes=[wk], eng="gpsimd",
                   tag=f"lw1_{half}_{ffb}")
            for ch in range(2):
                pk = ("psum", ch)
                for d in range(ND):
                    Tk.op("tensor", lambda t_, d=d, ch=ch, ffb=ffb, half=half, wk=wk: t_.matmul(
                        b.pb[ch][:, 0:512], w1r[ffb % 2][:, d, :],
                        hn2[:, d, half * TH + ch * 512: half * TH + (ch + 1) * 512],
                        start=(d == 0), stop=(d == ND - 1)),
                        reads=[wk] + hn2_keys, writes=[pk],
                        tag=f"h1m_{half}_{ffb}_{ch}")
                Tk.op("scalar", lambda s, ffb=ffb, ch=ch: s.activation(
                    h1h[:, ffb, ch * 512:(ch + 1) * 512], b.pb[ch][:, 0:512],
                    AF.Gelu_apprx_tanh),
                    reads=[pk], writes=[("h1h", ffb)], tag=f"h1e_{half}_{ffb}_{ch}")
        for ob in range(8):
            wk = ("w2r", ob % 2)
            b.load(w2r[ob % 2][:], w2_h[:, :, ob * 128:(ob + 1) * 128],
                   writes=[wk], eng="gpsimd", tag=f"lw2_{half}_{ob}")
            for ch in range(2):
                pk = ("psum", 2 + ch)
                for ffb in range(NF):
                    Tk.op("tensor", lambda t_, ffb=ffb, ob=ob, ch=ch, wk=wk: t_.matmul(
                        b.pb[2 + ch][:, 0:512], w2r[ob % 2][:, ffb, :],
                        h1h[:, ffb, ch * 512:(ch + 1) * 512],
                        start=(ffb == 0), stop=(ffb == NF - 1)),
                        reads=[wk, ("h1h", ffb)], writes=[pk],
                        tag=f"h2m_{half}_{ob}_{ch}")
                Tk.op("vector", lambda v, ob=ob, ch=ch: v.tensor_copy(
                    ot2[:, ch, :], b.pb[2 + ch][:, 0:512]),
                    reads=[pk], writes=[("fst", ch)], tag=f"h2e_{half}_{ob}_{ch}")
                b.store(f_t[:, ob, half * TH + ch * 512: half * TH + (ch + 1) * 512],
                        ot2[:, ch, :], reads=[("fst", ch)], writes=[("ft",)],
                        tag=f"h2s_{half}_{ob}_{ch}")

    b.fence("f_e1f_p6")
    for c in range(TL // 512):
        xk = ("xts", c % 2)
        pk0 = ("ep", c % 2)
        b.load(xt[c % 2][:], xl_h[:, :, c * 512:(c + 1) * 512], writes=[xk],
               tag=f"bl_ldx{c}")
        b.load(ptl[c % 2][:], x1_t[:, :, c * 512:(c + 1) * 512], reads=[("x1t",)],
               writes=[pk0], tag=f"bl_ldx1{c}")
        b.load(pt[:, :, :], f_t[:, :, c * 512:(c + 1) * 512], reads=[("ft",)],
               writes=[("fpt",)], tag=f"bl_ldf{c}")
        b.load(wrow[0:1, 0:512], rows_t[0, c * 4:(c + 1) * 4, :],
               reads=[("rowst",)], writes=[("wrow",)], tag=f"bl_ldw{c}")
        b.load(wrow[0:1, 512:1024], rows_t[1, c * 4:(c + 1) * 4, :],
               reads=[("rowst",)], writes=[("wrow",)], tag=f"bl_ldi{c}")
        pA, pAk, pB, pBk = ab_bcast(b, abu, abuk, c, 512, f"bl_bc{c}")
        pW, pWk = bcast_row(b, 2, wrow[0:1, 0:512], [("wrow",)], 512, f"bl_w{c}")
        pI, pIk = bcast_row(b, 3, wrow[0:1, 512:1024], [("wrow",)], 512, f"bl_i{c}")
        for d in range(ND):
            hk = ("eo", d)
            uk = ("sqs",)
            Tk.op("vector", lambda v, c=c, d=d, pA=pA: v.tensor_tensor(
                sq[:, d, :], xt[c % 2][:, d, :], pA, ALU.mult),
                reads=[xk, pAk], writes=[uk], tag=f"bl_u1_{c}_{d}")
            Tk.op("vector", lambda v, c=c, d=d, pB=pB: v.tensor_tensor(
                sq[:, d, :], sq[:, d, :], pB, ALU.subtract),
                reads=[uk, pBk], writes=[uk], tag=f"bl_u2_{c}_{d}")
            Tk.op("vector", lambda v, c=c, d=d, pI=pI: v.tensor_tensor(
                sq[:, d, :], sq[:, d, :], pI, ALU.mult),
                reads=[uk, pIk], writes=[uk], tag=f"bl_u3_{c}_{d}")
            Tk.op("vector", lambda v, c=c, d=d: v.tensor_tensor(
                ot2[:, d, :], ptl[c % 2][:, d, :], pt[:, d, :], ALU.add),
                reads=[pk0, ("fpt",)], writes=[hk], tag=f"bl_x2_{c}_{d}")
            Tk.op("vector", lambda v, c=c, d=d, pW=pW: v.tensor_tensor(
                ot2[:, d, :], ot2[:, d, :], pW, ALU.mult),
                reads=[hk, pWk], writes=[hk], tag=f"bl_pw_{c}_{d}")
            Tk.op("vector", lambda v, c=c, d=d: v.tensor_tensor(
                ot2[:, d, :], ot2[:, d, :], sq[:, d, :], ALU.add),
                reads=[hk, uk], writes=[hk], tag=f"bl_o_{c}_{d}")
        b.store(o_t[:, :, c * 512:(c + 1) * 512], ot2[:],
                reads=[("eo", d) for d in range(ND)], writes=[("otw",)],
                tag=f"bl_st{c}")

    b.fence("f_e1f_p7")
    abo, abok = emit_ln(b, "ef_ln2", ("hbm", o_t, [("otw",)]), TL)
    for c in range(TL // 512):
        xk = ("xts", c % 2)
        pk0 = ("ep", c % 2)
        b.load(xt[c % 2][:], xl_h[:, :, c * 512:(c + 1) * 512], writes=[xk],
               tag=f"xn_ldx{c}")
        b.load(ptl[c % 2][:], o_t[:, :, c * 512:(c + 1) * 512], reads=[("otw",)],
               writes=[pk0], tag=f"xn_ldo{c}")
        pA, pAk, pB, pBk = ab_bcast(b, abo, abok, c, 512, f"xn_bc{c}")
        for d in range(ND):
            hk = ("eo", d)
            Tk.op("vector", lambda v, c=c, d=d, pA=pA: v.tensor_tensor(
                ot2[:, d, :], ptl[c % 2][:, d, :], pA, ALU.mult),
                reads=[pk0, pAk], writes=[hk], tag=f"xn_a{c}_{d}")
            Tk.op("vector", lambda v, c=c, d=d, pB=pB: v.tensor_tensor(
                ot2[:, d, :], ot2[:, d, :], pB, ALU.subtract),
                reads=[hk, pBk], writes=[hk], tag=f"xn_b{c}_{d}")
            Tk.op("vector", lambda v, c=c, d=d: v.tensor_tensor(
                ot2[:, d, :], ot2[:, d, :], xt[c % 2][:, d, :], ALU.add),
                reads=[hk, xk], writes=[hk], tag=f"xn_c{c}_{d}")
        b.store(xn_h[:, :, c * 512:(c + 1) * 512], ot2[:],
                reads=[("eo", d) for d in range(ND)], writes=[("xnw",)],
                tag=f"xn_st{c}")
    emit_ln(b, "ef_ln3", ("hbm", xn_h, [("xnw",)]), TL, out_hbm=lnn_h)
    b.finish()
    return b


def prep_A1F(ins):
    Wq = np.asarray(ins["L1_ffn_wq"])
    Wk = np.asarray(ins["L1_ffn_wk"])
    Wv = np.asarray(ins["L1_ffn_wv"])
    Wo = np.asarray(ins["L1_ffn_wo"])
    rw = np.asarray(ins["L1_ffn_router_w"])
    rbv = np.asarray(ins["L1_ffn_router_b"])
    g1 = np.asarray(ins["L1_ffn_ln1"])
    mask = np.asarray(ins["L1_ffn_attention_mask"])
    pos = np.asarray(ins["L1_ffn_position_ids"])
    inv = (1.0 / 10000.0 ** (np.arange(0, 64, 2, dtype=np.float64) / 64.0)
           ).astype(np.float32)
    if4 = np.tile(inv, 4).reshape(1, 128)
    pm = _pm_mat()
    maps = []
    mTs = [np.ascontiguousarray(mask[bt, 0].T) for bt in range(4)]
    for c in range(8):
        bt, g = c // 2, c % 2
        cols = slice(g * 512, (g + 1) * 512)
        m = {
            "wq": np.ascontiguousarray(Wq[:, cols].reshape(8, 128, 512).transpose(1, 0, 2)),
            "wk": np.ascontiguousarray(Wk[:, cols].reshape(8, 128, 512).transpose(1, 0, 2)),
            "wv": np.ascontiguousarray(Wv[:, cols].reshape(8, 128, 512).transpose(1, 0, 2)),
            "wo": np.ascontiguousarray(
                Wo[g * 512:(g + 1) * 512].reshape(4, 128, 8, 128).transpose(1, 0, 2, 3)),
            "rw": np.ascontiguousarray(rw.reshape(8, 128).T),
            "rb": rbv.reshape(1, 1),
            "g1": np.ascontiguousarray(g1.reshape(8, 128)),
            "mT": mTs[bt],
            "posf": pos[bt].astype(np.float32).reshape(1, S_),
            "if4": if4, "pm": pm, "ident": np.eye(128, dtype=np.float32),
        }
        maps.append(m)
    return maps


def prep_E1F(ins):
    w1 = np.asarray(ins["L1_ffn_w1"])
    w2 = np.asarray(ins["L1_ffn_w2"])
    rw = np.asarray(ins["L1_ffn_router_w"])
    rbv = np.asarray(ins["L1_ffn_router_b"])
    g2 = np.asarray(ins["L1_ffn_ln2"])
    m = {
        "w1": np.ascontiguousarray(w1.reshape(8, 128, 4096).transpose(1, 0, 2)),
        "w2": np.ascontiguousarray(w2.reshape(32, 128, 1024).transpose(1, 0, 2)),
        "rw": np.ascontiguousarray(rw.reshape(8, 128).T),
        "rb": rbv.reshape(1, 1),
        "g2": np.ascontiguousarray(g2.reshape(8, 128)),
        "ident": np.eye(128, dtype=np.float32),
    }
    return [dict(m) for _ in range(8)]


def build_F2():
    b = Builder("F2")
    Tk = b.T
    h_h = b.dram_in("lnf", [128, 8, TL], BF16)
    xl_h = b.dram_in("xl", [128, 8, TL], F32)
    wt_h = b.dram_in("wt", [128, 8, 8, 128], F32)
    bv_h = b.dram_in("bv", [8, 128], F32)
    da_h = b.dram_in("da", [128, 8, 100], F32)
    ldt_h = b.dram_in("ldt", [100, 4], F32)
    lut_h = b.dram_in("lut", [4, 50], F32)
    ua_h = b.dram_in("ua", [50, 8, 128], F32)
    xn_h = b.dram_out("xn", [128, 8, TL], F32)
    ln3_h = b.dram_out("ln3", [128, 8, TL], BF16)
    f_t = b.dram_tmp("f2f", [128, 8, TL], F32)

    ht = b.sb("f_h", [128, ND, TL], BF16)
    wt = b.sb("f_wt", [128, ND, ND, 128], BF16)
    da = b.sb("f_da", [128, ND, 100], BF16)
    ldt = b.sb("f_ldt", [100, 4], BF16)
    lut = b.sb("f_lut", [4, 50], BF16)
    ua = b.sb("f_ua", [50, ND, 128], BF16)
    aT = b.sb("f_aT", [100, TL], BF16)
    bT = b.sb("f_bT", [4, TL], BF16)
    cT = b.sb("f_cT", [50, TL], BF16)
    ft = b.sb("f_ft", [128, ND, 512], F32)
    pt = b.sb("f_pt", [128, ND, 512], F32)
    xt = b.shared_xt()
    bv = b.colset("f_bv", bv_h, 8, "bvc")

    b.load(ht[:], h_h[:], writes=[("fh",)], tag="f2_ldh")
    b.load(wt[:], wt_h[:], writes=[("fw",)], eng="gpsimd", tag="f2_ldw")
    b.load(da[:], da_h[:], writes=[("fda",)], eng="gpsimd", tag="f2_ldda")
    b.load(ldt[:], ldt_h[:], writes=[("fld",)], eng="gpsimd", tag="f2_ldld")
    b.load(lut[:], lut_h[:], writes=[("flu",)], eng="gpsimd", tag="f2_ldlu")
    b.load(ua[:], ua_h[:], writes=[("fua",)], eng="gpsimd", tag="f2_ldua")

    CH = 512
    nch = TL // CH
    for c in range(nch):
        ts = slice(c * CH, (c + 1) * CH)
        for d in range(ND):
            Tk.op("tensor", lambda t, c=c, d=d, ts=ts: t.matmul(
                b.pb[0][0:100, 0:CH], da[:, d, :], ht[:, d, ts],
                start=(d == 0), stop=(d == ND - 1)),
                reads=[("fda",), ("fh",)], writes=[("psum", 0)], tag=f"f2a{c}")
        Tk.op("scalar", lambda s, ts=ts: s.activation(
            aT[:, ts], b.pb[0][0:100, 0:CH], AF.Identity),
            reads=[("psum", 0)], writes=[("faT",)], tag=f"f2ae{c}")
        Tk.op("tensor", lambda t, ts=ts: t.matmul(
            b.pb[1][0:4, 0:CH], ldt[:, :], aT[:, ts], start=True, stop=True),
            reads=[("fld",), ("faT",)], writes=[("psum", 1)], tag=f"f2b{c}")
        Tk.op("scalar", lambda s, ts=ts: s.activation(
            bT[:, ts], b.pb[1][0:4, 0:CH], AF.Identity),
            reads=[("psum", 1)], writes=[("fbT",)], tag=f"f2be{c}")
        Tk.op("tensor", lambda t, ts=ts: t.matmul(
            b.pb[2][0:50, 0:CH], lut[:, :], bT[:, ts], start=True, stop=True),
            reads=[("flu",), ("fbT",)], writes=[("psum", 2)], tag=f"f2c{c}")
        Tk.op("scalar", lambda s, ts=ts: s.activation(
            cT[:, ts], b.pb[2][0:50, 0:CH], AF.Identity),
            reads=[("psum", 2)], writes=[("fcT",)], tag=f"f2ce{c}")
    for c in range(nch):
        ts = slice(c * CH, (c + 1) * CH)
        for fo in range(ND):
            for d in range(ND):
                Tk.op("tensor", lambda t, d=d, fo=fo, ts=ts: t.matmul(
                    b.pb[fo][:, 0:CH], wt[:, d, fo, :], ht[:, d, ts],
                    start=(d == 0), stop=False),
                    reads=[("fw",), ("fh",)], writes=[("psum", fo)],
                    tag=f"f2m{c}_{fo}")
            Tk.op("tensor", lambda t, fo=fo, ts=ts: t.matmul(
                b.pb[fo][:, 0:CH], ua[0:50, fo, :], cT[:, ts],
                start=False, stop=True),
                reads=[("fua",), ("fcT",)], writes=[("psum", fo)],
                tag=f"f2d{c}_{fo}")
            Tk.op("scalar", lambda s, fo=fo: s.activation(
                ft[:, fo, :], b.pb[fo][:, 0:CH], AF.Identity,
                bias=bv[fo][:, 0:1]),
                reads=[("psum", fo), ("bvc",)], writes=[("fft", fo)],
                tag=f"f2e{c}_{fo}")
        b.store(f_t[:, :, ts], ft[:],
                reads=[("fft", d) for d in range(ND)], writes=[("ftw",)],
                tag=f"f2s{c}")
    ab2, ab2k = emit_ln(b, "f2ln", ("hbm", f_t, [("ftw",)]), TL)
    for c in range(nch):
        ts = slice(c * CH, (c + 1) * CH)
        xk = ("xts", c % 2)
        b.load(xt[c % 2][:], xl_h[:, :, ts], writes=[xk], tag=f"f2_ldx{c}")
        b.load(pt[:], f_t[:, :, ts], reads=[("ftw",)], writes=[("fpt",)],
               tag=f"f2_ldf{c}")
        pA, pAk, pB, pBk = ab_bcast(b, ab2, ab2k, c, CH, f"f2bc{c}")
        for d in range(ND):
            hk = ("fft", d)
            Tk.op("vector", lambda v, d=d, pA=pA: v.tensor_tensor(
                ft[:, d, :], pt[:, d, :], pA, ALU.mult),
                reads=[("fpt",), pAk], writes=[hk], tag=f"f2x1_{c}_{d}")
            Tk.op("vector", lambda v, d=d, pB=pB: v.tensor_tensor(
                ft[:, d, :], ft[:, d, :], pB, ALU.subtract),
                reads=[hk, pBk], writes=[hk], tag=f"f2x2_{c}_{d}")
            Tk.op("vector", lambda v, c=c, d=d: v.tensor_tensor(
                ft[:, d, :], ft[:, d, :], xt[c % 2][:, d, :], ALU.add),
                reads=[hk, xk], writes=[hk], tag=f"f2x3_{c}_{d}")
        b.store(xn_h[:, :, ts], ft[:],
                reads=[("fft", d) for d in range(ND)], writes=[("xnout",)],
                tag=f"f2xs{c}")
    emit_ln(b, "f2ln3", ("hbm", xn_h, [("xnout",)]), TL, out_hbm=ln3_h)
    b.finish()
    return b


def prep_F2(ins):
    W = np.asarray(ins["L2_ffn_W"])
    bvec = np.asarray(ins["L2_ffn_b"])
    ld = np.asarray(ins["L2_ffn_lora_down"])
    lu = np.asarray(ins["L2_ffn_lora_up"])
    dax = np.asarray(ins["L2_ffn_down_aux"])
    uax = np.asarray(ins["L2_ffn_up_aux"])
    m = {
        "wt": np.ascontiguousarray(
            W.T.reshape(8, 128, 8, 128).transpose(1, 0, 2, 3)),
        "bv": bvec.reshape(8, 128),
        "da": np.ascontiguousarray(dax.T.reshape(8, 128, 100).transpose(1, 0, 2)),
        "ldt": np.ascontiguousarray(ld.T),
        "lut": np.ascontiguousarray(lu.T),
        "ua": np.ascontiguousarray(uax.T.reshape(50, 8, 128)),
    }
    return [dict(m) for _ in range(8)]


def build_A2(dbg=False):
    b = Builder("A2")
    Tk = b.T
    NCH = 8
    NS = 32
    lnf_h = b.dram_in("lnf", [128, 8, S_], BF16)
    if dbg:
        dbg_o = {nm: b.dram_out("d_" + nm, sh, dt) for nm, sh, dt in [
            ("qlT", [64, 64], F32), ("klT", [64, 64], F32),
            ("k2", [64, 64], F32), ("nsV", [64, 64], F32),
            ("r3", [64, 64], F32), ("w2e", [64, 65], BF16),
            ("e3b", [64, S_], BF16), ("svT", [64, S_], BF16),
            ("e3T", [128, 32, 64], BF16), ("vN", [128, 32, 64], BF16),
            ("qT", [65, S_], BF16), ("kT", [64, S_], BF16),
            ("vT", [64, S_], BF16),
            ("xlm", [128, ND, 64], BF16)]}
    wqkv_h = b.dram_in("wqkv", [128, 8, 3, 4, 128], F32)
    wp_h = b.dram_in("wp", [128, 4, 8, 128], F32)
    ident_h = b.dram_in("ident", [128, 128], F32)
    po_h = b.dram_out("po", [128, 8, S_], F32)
    sv_t = b.dram_tmp("sv_t", [64, 8, S_], BF16)

    lnf = b.sb("a2_ln", [128, ND, S_], BF16)
    wqkv = b.sb("a2_w", [128, ND, 3, 4, 128], BF16)
    wp = b.sb("a2_wp", [128, 4, ND, 128], BF16)
    identf = b.sb("a2_id", [128, 128], F32)
    identb = b.sb("a2_idb", [128, 128], BF16)
    qTa = b.sb("a2_q", [65, 2, S_], BF16)
    kTp = b.sb("a2_k", [64, 2, S_], BF16)
    vTp = b.sb("a2_v", [64, 2, S_], BF16)
    xlmf = b.sb("a2_xlf", [128, ND, 64], F32)
    xlm = b.sb("a2_xlm", [128, ND, 64], BF16)
    qlT = b.sb("a2_qlT", [64, 64], F32)
    klT = b.sb("a2_klT", [64, 64], F32)
    qlTb = b.sb("a2_qlb", [64, 64], BF16)
    klA = b.sb("a2_klA", [65, 64], BF16)
    e3b = b.sb("a2_e3", [64, S_], BF16)
    e3T = b.sb("a2_e3T", [128, NS, 64], BF16)
    vN = b.sb("a2_vN", [128, NS, 64], BF16)
    svT = b.sb("a2_sv", [64, S_], BF16)
    svb2 = b.sb("a2_svb", [128, 4, 512], BF16)
    pob = b.sb("a2_po", [128, ND, 512], F32)
    t64a = b.sb("t64a", [64, 512], BF16)
    tp64 = b.sb("a2_tp", [64, 128], F32)
    mx3a = b.sb("a2_mx3", [64, NCH + 2], F32)
    rzrow = b.sb("a2_rz", [1, 512], F32)
    rzb = b.sb("a2_rzb", [64, 512], BF16)
    k2 = b.sb("ns_k2", [64, 64], F32)
    k2T = b.sb("ns_k2T", [64, 64], F32)
    nsV = b.sb("ns_V", [64, 64], F32)
    nsVT = b.sb("ns_VT", [64, 64], F32)
    nsKV = b.sb("ns_KV", [64, 64], F32)
    nsKVT = b.sb("ns_KVT", [64, 64], F32)
    nsT1 = b.sb("ns_T1", [64, 64], F32)
    nsT2 = b.sb("ns_T2", [64, 64], F32)
    eyes = b.sb("ns_eye", [64, 3, 64], F32)
    w2e = b.sb("a2_w2e", [64, 65], BF16)
    colA = b.sb("a2_colA", [64, 4], F32)
    zc0 = b.constcol(0.0)
    ones_c = b.constrow(1.0, 1, BF16, parts=128)
    onesck = b.kr(1.0, 1, BF16, 128)
    ones_cf = b.constrow(1.0, 1, F32, parts=64)
    onescfk = b.kr(1.0, 1, F32, 64)
    ones_r = b.constrow(1.0, 128, F32)
    onesrk = b.kr(1.0, 128, F32)

    b.load(lnf[:], lnf_h[:], writes=[("lnf",)], tag="a2_ldln")
    b.load(wqkv[:], wqkv_h[:], writes=[("wqkv",)], eng="gpsimd", tag="a2_ldw")
    b.load(wp[:], wp_h[:], writes=[("wp",)], eng="gpsimd", tag="a2_ldwp")
    b.load(identf[:], ident_h[:], writes=[("idf",)], tag="a2_ldid")
    b.load(identb[:], ident_h[:], writes=[("idb",)], eng="gpsimd", tag="a2_ldib")
    for j, sc in enumerate([7.0, 15.0, 13.0]):
        Tk.op("scalar", lambda s, j=j, sc=sc: s.activation(
            eyes[:, j, :], identf[0:64, 0:64], AF.Identity, scale=sc),
            reads=[("idf",)], writes=[("eyes",)], tag=f"a2_eye{j}")

    for d in range(ND):
        Tk.op("vector", lambda v, d=d: v.reduce_sum(
            xlmf[:, d, :, None], lnf[:, d, :].rearrange("p (m u) -> p m u", u=64),
            axis=AX.X),
            reads=[("lnf",)], writes=[("xlmf",)], tag=f"a2_lm{d}")
        Tk.op("scalar", lambda s, d=d: s.activation(
            xlm[:, d, :], xlmf[:, d, :], AF.Identity, scale=1.0 / 64.0),
            reads=[("xlmf",)], writes=[("xlm",)], tag=f"a2_lms{d}")

    qk = ("a2qT",)
    kk = ("a2kT",)
    vk = ("a2vT",)
    for hp in range(4):
        for ti, (buf, bkey, scl) in enumerate(
                [(qTa, qk, 0.125), (kTp, kk, 1.0), (vTp, vk, 1.0)]):
            for c in range(NCH):
                pk = ("psum", ti)
                for d in range(ND):
                    Tk.op("tensor", lambda t, ti=ti, c=c, d=d, hp=hp: t.matmul(
                        b.pb[ti][:, 0:512], wqkv[:, d, ti, hp, :],
                        lnf[:, d, c * 512:(c + 1) * 512],
                        start=(d == 0), stop=(d == ND - 1)),
                        reads=[("wqkv",), ("lnf",)], writes=[pk],
                        tag=f"a2p{hp}_{ti}_{c}")
                for j in range(2):
                    Tk.op("scalar", lambda s, ti=ti, c=c, j=j, buf=buf, scl=scl:
                          s.activation(buf[0:64, j, c * 512:(c + 1) * 512],
                                       b.pb[ti][64 * j:64 * j + 64, 0:512],
                                       AF.Identity, scale=scl),
                          reads=[pk], writes=[bkey], tag=f"a2pe{hp}_{ti}_{c}_{j}")
        for j in range(2):
            h = 2 * hp + j
            tg = f"h{h}_"
            qT_h = qTa[:, j, :]
            kT_h = kTp[:, j, :]
            vT_h = vTp[:, j, :]
            for ti, (obuf, okey2, scl) in enumerate(
                    [(qlT, ("qlT",), 0.125), (klT, ("klT",), 1.0)]):
                for d in range(ND):
                    Tk.op("tensor", lambda t, ti=ti, d=d, hp=hp, j=j: t.matmul(
                        b.pb[ti][0:128, 0:64], wqkv[:, d, ti, hp, :],
                        xlm[:, d, :], start=(d == 0), stop=(d == ND - 1)),
                        reads=[("wqkv",), ("xlm",)], writes=[("psum", ti)],
                        tag=tg + f"lp{ti}")
                Tk.op("scalar", lambda s, ti=ti, j=j, obuf=obuf, scl=scl:
                      s.activation(obuf[:, :], b.pb[ti][64 * j:64 * j + 64, 0:64],
                                   AF.Identity, scale=scl),
                      reads=[("psum", ti)], writes=[okey2], tag=tg + f"le{ti}")
            Tk.op("scalar", lambda s: s.activation(qlTb[:, :], qlT[:, :],
                                                   AF.Identity),
                  reads=[("qlT",)], writes=[("qlTb",)], tag=tg + "qlb")
            Tk.op("scalar", lambda s: s.activation(klA[0:64, :], klT[:, :],
                                                   AF.Identity),
                  reads=[("klT",)], writes=[("klA",)], tag=tg + "klb")
            emit_qk_bound(b, tg + "qb", qT_h, klA, [qk], [("klA",)], t64a,
                          TQ=S_, TS=64)
            Tk.op("tensor", lambda t: t.matmul(
                b.pb[0][0:64, 0:64], qlT[:, :], klT[:, :], start=True, stop=True),
                reads=[("qlT",), ("klT",)], writes=[("psum", 0)], tag=tg + "s2")
            Tk.op("vector", lambda v: v.reduce_max(
                colA[:, 0:1], b.pb[0][0:64, 0:64], axis=AX.X, negate=True),
                reads=[("psum", 0)], writes=[("colA",)], tag=tg + "s2m")
            Tk.op("scalar", lambda s: s.activation(
                k2[:, :], b.pb[0][0:64, 0:64], AF.Exp, bias=colA[:, 0:1]),
                reads=[("psum", 0), ("colA",)], writes=[("k2",)], tag=tg + "s2e")
            Tk.op("vector", lambda v: v.reduce_sum(
                colA[:, 1:2], k2[:, :], axis=AX.X),
                reads=[("k2",)], writes=[("colA",)], tag=tg + "s2z")
            Tk.op("vector", lambda v: v.reciprocal(colA[:, 2:3], colA[:, 1:2]),
                  reads=[("colA",)], writes=[("colA",)], tag=tg + "s2r")
            Tk.op("vector", lambda v: v.tensor_scalar_mul(
                k2[:, :], k2[:, :], colA[:, 2:3]),
                reads=[("k2",), ("colA",)], writes=[("k2",)], tag=tg + "s2n")
            Tk.op("tensor", lambda t: t.matmul(
                b.pb[1][0:1, 0:64], ones_cf[:, 0:1], k2[:, :],
                start=True, stop=True),
                reads=[("k2",), onescfk], writes=[("psum", 1)], tag=tg + "cs")
            Tk.op("vector", lambda v: v.reduce_max(
                rzrow[0:1, 0:1], b.pb[1][0:1, 0:64], axis=AX.X),
                reads=[("psum", 1)], writes=[("rzr",)], tag=tg + "csm")
            Tk.op("vector", lambda v: v.reciprocal(rzrow[0:1, 1:2],
                                                   rzrow[0:1, 0:1]),
                  reads=[("rzr",)], writes=[("rzr",)], tag=tg + "csr")
            Tk.op("tensor", lambda t: t.matmul(
                b.pb[2][0:64, 0:1], ones_r[0:1, 0:64], rzrow[0:1, 1:2],
                start=True, stop=True),
                reads=[("rzr",), onesrk], writes=[("psum", 2)], tag=tg + "csb")
            Tk.op("vector", lambda v: v.tensor_copy(colA[:, 3:4],
                                                    b.pb[2][0:64, 0:1]),
                  reads=[("psum", 2)], writes=[("colA",)], tag=tg + "csc")
            Tk.op("tensor", lambda t: t.transpose(
                b.pb[0][0:64, 0:64], k2[:, :], identf[0:64, 0:64]),
                reads=[("k2",), ("idf",)], writes=[("psum", 0)], tag=tg + "ktr")
            Tk.op("scalar", lambda s: s.activation(
                k2T[:, :], b.pb[0][0:64, 0:64], AF.Identity),
                reads=[("psum", 0)], writes=[("k2T",)], tag=tg + "kte")
            Tk.op("vector", lambda v: v.tensor_scalar_mul(
                nsV[:, :], k2T[:, :], colA[:, 3:4]),
                reads=[("k2T",), ("colA",)], writes=[("nsV",)], tag=tg + "v0")
            for it in range(6):
                tgi = tg + f"ns{it}_"
                Tk.op("tensor", lambda t: t.matmul(
                    b.pb[0][0:64, 0:64], k2T[:, :], nsV[:, :],
                    start=True, stop=True),
                    reads=[("k2T",), ("nsV",)], writes=[("psum", 0)],
                    tag=tgi + "kv")
                Tk.op("scalar", lambda s: s.activation(
                    nsKV[:, :], b.pb[0][0:64, 0:64], AF.Identity),
                    reads=[("psum", 0)], writes=[("nsKV",)], tag=tgi + "kve")
                Tk.op("tensor", lambda t: t.transpose(
                    b.pb[1][0:64, 0:64], nsKV[:, :], identf[0:64, 0:64]),
                    reads=[("nsKV",), ("idf",)], writes=[("psum", 1)],
                    tag=tgi + "kvt")
                Tk.op("scalar", lambda s: s.activation(
                    nsKVT[:, :], b.pb[1][0:64, 0:64], AF.Identity),
                    reads=[("psum", 1)], writes=[("nsKVT",)], tag=tgi + "kvte")
                Tk.op("vector", lambda v: v.scalar_tensor_tensor(
                    nsT1[:, :], nsKV[:, :], -1.0, eyes[:, 0, :],
                    ALU.mult, ALU.add),
                    reads=[("nsKV",), ("eyes",)], writes=[("nsT1",)],
                    tag=tgi + "t1")
                Tk.op("tensor", lambda t: t.matmul(
                    b.pb[0][0:64, 0:64], nsKVT[:, :], nsT1[:, :],
                    start=True, stop=True),
                    reads=[("nsKVT",), ("nsT1",)], writes=[("psum", 0)],
                    tag=tgi + "m2")
                Tk.op("vector", lambda v: v.scalar_tensor_tensor(
                    nsT2[:, :], b.pb[0][0:64, 0:64], -1.0, eyes[:, 1, :],
                    ALU.mult, ALU.add),
                    reads=[("psum", 0), ("eyes",)], writes=[("nsT2",)],
                    tag=tgi + "t2")
                Tk.op("tensor", lambda t: t.matmul(
                    b.pb[0][0:64, 0:64], nsKVT[:, :], nsT2[:, :],
                    start=True, stop=True),
                    reads=[("nsKVT",), ("nsT2",)], writes=[("psum", 0)],
                    tag=tgi + "m3")
                Tk.op("vector", lambda v: v.scalar_tensor_tensor(
                    nsT1[:, :], b.pb[0][0:64, 0:64], -1.0, eyes[:, 2, :],
                    ALU.mult, ALU.add),
                    reads=[("psum", 0), ("eyes",)], writes=[("nsT1",)],
                    tag=tgi + "t3")
                Tk.op("tensor", lambda t: t.transpose(
                    b.pb[1][0:64, 0:64], nsV[:, :], identf[0:64, 0:64]),
                    reads=[("nsV",), ("idf",)], writes=[("psum", 1)],
                    tag=tgi + "vt")
                Tk.op("scalar", lambda s: s.activation(
                    nsVT[:, :], b.pb[1][0:64, 0:64], AF.Identity),
                    reads=[("psum", 1)], writes=[("nsVT",)], tag=tgi + "vte")
                Tk.op("tensor", lambda t: t.matmul(
                    b.pb[0][0:64, 0:64], nsVT[:, :], nsT1[:, :],
                    start=True, stop=True),
                    reads=[("nsVT",), ("nsT1",)], writes=[("psum", 0)],
                    tag=tgi + "mv")
                Tk.op("scalar", lambda s: s.activation(
                    nsV[:, :], b.pb[0][0:64, 0:64], AF.Identity, scale=0.25),
                    reads=[("psum", 0)], writes=[("nsV",)], tag=tgi + "ve")
            Tk.op("tensor", lambda t: t.transpose(
                b.pb[1][0:64, 0:64], nsV[:, :], identf[0:64, 0:64]),
                reads=[("nsV",), ("idf",)], writes=[("psum", 1)], tag=tg + "fvt")
            Tk.op("scalar", lambda s: s.activation(
                nsVT[:, :], b.pb[1][0:64, 0:64], AF.Identity),
                reads=[("psum", 1)], writes=[("nsVT",)], tag=tg + "fvte")
            for c in range(NCH):
                Tk.op("tensor", lambda t, c=c, kT_h=kT_h: t.matmul(
                    b.pb[2][0:64, 0:512], qlTb[:, :],
                    kT_h[:, c * 512:(c + 1) * 512], start=True, stop=True),
                    reads=[("qlTb",), kk], writes=[("psum", 2)], tag=tg + f"s3a{c}")
                Tk.op("vector", lambda v, c=c: v.reduce_max(
                    mx3a[:, c:c + 1], b.pb[2][0:64, 0:512], axis=AX.X),
                    reads=[("psum", 2)], writes=[("mx3",)], tag=tg + f"s3m{c}")
            Tk.op("vector", lambda v: v.reduce_max(
                mx3a[:, NCH:NCH + 1], mx3a[:, 0:NCH], axis=AX.X, negate=True),
                reads=[("mx3",)], writes=[("mx3",)], tag=tg + "s3M")
            for c in range(NCH):
                Tk.op("tensor", lambda t, c=c, kT_h=kT_h: t.matmul(
                    b.pb[2][0:64, 0:512], qlTb[:, :],
                    kT_h[:, c * 512:(c + 1) * 512], start=True, stop=True),
                    reads=[("qlTb",), kk], writes=[("psum", 2)], tag=tg + f"s3b{c}")
                Tk.op("scalar", lambda s, c=c: s.activation(
                    e3b[:, c * 512:(c + 1) * 512], b.pb[2][0:64, 0:512],
                    AF.Exp, bias=mx3a[:, NCH:NCH + 1]),
                    reads=[("psum", 2), ("mx3",)], writes=[("e3b",)],
                    tag=tg + f"s3e{c}")
            Tk.op("vector", lambda v: v.reduce_sum(
                mx3a[:, NCH:NCH + 1], e3b[:, :], axis=AX.X),
                reads=[("e3b",)], writes=[("mx3",)], tag=tg + "s3z")
            Tk.op("vector", lambda v: v.reciprocal(
                mx3a[:, NCH + 1:NCH + 2], mx3a[:, NCH:NCH + 1]),
                reads=[("mx3",)], writes=[("mx3",)], tag=tg + "s3r")
            Tk.op("vector", lambda v: v.tensor_scalar_mul(
                e3b[:, :], e3b[:, :], mx3a[:, NCH + 1:NCH + 2]),
                reads=[("e3b",), ("mx3",)], writes=[("e3b",)], tag=tg + "s3n")
            for st in range(NS):
                Tk.op("scalar", lambda s, st=st: s.activation(
                    tp64[:, :], e3b[:, st * 128:(st + 1) * 128], AF.Identity),
                    reads=[("e3b",)], writes=[("tp64",)], tag=tg + f"e3c{st}")
                Tk.op("tensor", lambda t, st=st: t.transpose(
                    b.pb[3][0:128, 0:64], tp64[:, :], identf[0:64, 0:64]),
                    reads=[("tp64",), ("idf",)], writes=[("psum", 3)],
                    tag=tg + f"e3t{st}")
                Tk.op("scalar", lambda s, st=st: s.activation(
                    e3T[:, st, :], b.pb[3][0:128, 0:64], AF.Identity),
                    reads=[("psum", 3)], writes=[("e3T",)], tag=tg + f"e3e{st}")
                Tk.op("scalar", lambda s, st=st, vT_h=vT_h: s.activation(
                    tp64[:, :], vT_h[:, st * 128:(st + 1) * 128], AF.Identity),
                    reads=[vk], writes=[("tp64",)], tag=tg + f"vnc{st}")
                Tk.op("tensor", lambda t, st=st: t.transpose(
                    b.pb[1][0:128, 0:64], tp64[:, :], identf[0:64, 0:64]),
                    reads=[("tp64",), ("idf",)], writes=[("psum", 1)],
                    tag=tg + f"vnt{st}")
                Tk.op("scalar", lambda s, st=st: s.activation(
                    vN[:, st, :], b.pb[1][0:128, 0:64], AF.Identity),
                    reads=[("psum", 1)], writes=[("vN",)], tag=tg + f"vne{st}")
            for st in range(NS):
                Tk.op("tensor", lambda t, st=st: t.matmul(
                    b.pb[3][0:64, 0:64], e3T[:, st, :], vN[:, st, :],
                    start=(st == 0), stop=(st == NS - 1)),
                    reads=[("e3T",), ("vN",)], writes=[("psum", 3)],
                    tag=tg + f"r3{st}")
            Tk.op("scalar", lambda s: s.activation(
                nsT2[:, :], b.pb[3][0:64, 0:64], AF.Identity),
                reads=[("psum", 3)], writes=[("nsT2",)], tag=tg + "r3e")
            Tk.op("tensor", lambda t: t.matmul(
                b.pb[0][0:64, 0:64], nsVT[:, :], nsT2[:, :],
                start=True, stop=True),
                reads=[("nsVT",), ("nsT2",)], writes=[("psum", 0)], tag=tg + "w2")
            Tk.op("vector", lambda v: v.memset(w2e[:, 64:65], 1.0),
                  writes=[("w2e",)], tag=tg + "w2o")
            Tk.op("scalar", lambda s: s.activation(
                w2e[:, 0:64], b.pb[0][0:64, 0:64], AF.Identity),
                reads=[("psum", 0)], writes=[("w2e",)], tag=tg + "w2b")
            for c in range(NCH):
                cs = slice(c * 512, (c + 1) * 512)
                Tk.op("tensor", lambda t, cs=cs, j=j: t.matmul(
                    b.pb[0][0:64, 0:512], klA[:, :], qTa[:, j, cs],
                    start=True, stop=True),
                    reads=[("klA",), qk, ("qaug",)], writes=[("psum", 0)],
                    tag=tg + f"k1{c}")
                Tk.op("scalar", lambda s, cs=cs: s.activation(
                    t64a[:, :], b.pb[0][0:64, 0:512], AF.Exp,
                    bias=zc0[0:64, 0:1]),
                    reads=[("psum", 0), b.kc(0.0)], writes=[("t64a",)],
                    tag=tg + f"k1e{c}")
                Tk.op("tensor", lambda t: t.matmul(
                    b.pb[1][0:65, 0:512], w2e[:, :], t64a[:, :],
                    start=True, stop=True),
                    reads=[("w2e",), ("t64a",)], writes=[("psum", 1)],
                    tag=tg + f"sv{c}")
                Tk.op("vector", lambda v: v.reciprocal(
                    rzrow[0:1, :], b.pb[1][64:65, 0:512]),
                    reads=[("psum", 1)], writes=[("rzr",)], tag=tg + f"svr{c}")
                Tk.op("tensor", lambda t: t.matmul(
                    b.pb[2][0:64, 0:512], ones_r[0:1, 0:64], rzrow[0:1, :],
                    start=True, stop=True),
                    reads=[("rzr",), onesrk], writes=[("psum", 2)],
                    tag=tg + f"svb{c}")
                Tk.op("scalar", lambda s: s.activation(
                    rzb[:, :], b.pb[2][0:64, 0:512], AF.Identity),
                    reads=[("psum", 2)], writes=[("rzb",)], tag=tg + f"svbe{c}")
                Tk.op("vector", lambda v, cs=cs: v.tensor_tensor(
                    svT[:, cs], b.pb[1][0:64, 0:512], rzb[:, :], ALU.mult),
                    reads=[("psum", 1), ("rzb",)], writes=[("svT",)],
                    tag=tg + f"svn{c}")
            if dbg and h == 0:
                for nm, ap, rk in [("qlT", qlT[:, :], ("qlT",)),
                                   ("klT", klT[:, :], ("klT",)),
                                   ("k2", k2[:, :], ("k2",)),
                                   ("nsV", nsV[:, :], ("nsV",)),
                                   ("r3", nsT2[:, :], ("nsT2",)),
                                   ("w2e", w2e[:, :], ("w2e",)),
                                   ("e3b", e3b[:, :], ("e3b",)),
                                   ("svT", svT[:, :], ("svT",)),
                                   ("e3T", e3T[:], ("e3T",)),
                                   ("vN", vN[:], ("vN",)),
                                   ("qT", qTa[:, 0, :], qk),
                                   ("kT", kTp[:, 0, :], kk),
                                   ("vT", vTp[:, 0, :], vk),
                                   ("xlm", xlm[:], ("xlm",))]:
                    b.store(dbg_o[nm][:], ap, reads=[rk], writes=[("dbgw",)],
                            tag=tg + "dbg" + nm)
            b.store(sv_t[:, h, :], svT[:, :], reads=[("svT",)],
                    writes=[("svt_w",)], tag=tg + "svs")
    for c in range(NCH):
        cs = slice(c * 512, (c + 1) * 512)
        for hp in range(4):
            for j in range(2):
                b.load(svb2[64 * j:64 * j + 64, hp, :],
                       sv_t[:, 2 * hp + j, cs], reads=[("svt_w",)],
                       writes=[("svb2",)], tag=f"a2_lsv{c}_{hp}_{j}")
        for fo in range(ND):
            for hp in range(4):
                Tk.op("tensor", lambda t, fo=fo, hp=hp: t.matmul(
                    b.pb[fo % 4][:, 0:512], wp[:, hp, fo, :], svb2[:, hp, :],
                    start=(hp == 0), stop=(hp == 3)),
                    reads=[("wp",), ("svb2",)], writes=[("psum", fo % 4)],
                    tag=f"a2o{c}_{fo}")
            Tk.op("scalar", lambda s, fo=fo: s.activation(
                pob[:, fo, :], b.pb[fo % 4][:, 0:512], AF.Identity),
                reads=[("psum", fo % 4)], writes=[("pob", fo)],
                tag=f"a2oe{c}_{fo}")
        b.store(po_h[:, :, cs], pob[:],
                reads=[("pob", d) for d in range(ND)], writes=[("po_w",)],
                tag=f"a2os{c}")
    b.finish()
    return b


def prep_A2(ins):
    wqkv = np.asarray(ins["L2_attn_w_qkv"])
    wproj = np.asarray(ins["L2_attn_w_proj"])
    maps = []
    for c in range(8):
        hg = c % 2
        mh = list(range(hg * 8, hg * 8 + 8))
        wt = np.empty((128, 8, 3, 4, 128), np.float32)
        for ti in range(3):
            w = wqkv[:, ti * 1024:(ti + 1) * 1024].reshape(1024, 16, 64)
            w = w[:, mh, :].reshape(1024, 4, 128)
            wt[:, :, ti, :, :] = w.reshape(8, 128, 4, 128).transpose(1, 0, 2, 3)
        wp = wproj.reshape(16, 64, 8, 128)[mh]
        wp = wp.reshape(4, 128, 8, 128).transpose(1, 0, 2, 3)
        m = {
            "wqkv": np.ascontiguousarray(wt),
            "wp": np.ascontiguousarray(wp),
            "ident": np.eye(128, dtype=np.float32),
        }
        maps.append(m)
    return maps


def build_A3():
    b = Builder("A3")
    Tk = b.T
    NCH = 8
    NS = 32
    HD = 128
    QS = 1.0 / math.sqrt(128.0)
    lnf_h = b.dram_in("lnf", [128, 8, S_], BF16)
    wqkv_h = b.dram_in("wqkv", [128, 8, 3, 4, 128], F32)
    wo_h = b.dram_in("wo", [128, 4, 8, 128], F32)
    th_h = b.dram_in("th", [8, 1], F32)
    gb_h = b.dram_in("gb", [8, 128], F32)
    po_h = b.dram_out("po", [128, 8, S_], F32)

    lnf = b.sb("a3_ln", [128, ND, S_], BF16)
    wqkv = b.sb("a3_w", [128, ND, 3, 4, 128], BF16)
    wo = b.sb("a3_wo", [128, 4, ND, 128], BF16)
    qT = b.sb("a3_q", [128, S_], BF16)
    kT = b.sb("a3_k", [128, S_], BF16)
    vN3 = b.sb("a3_v", [128, NS, 128], BF16)
    oln = b.sb("a3_oln", [128, 4, S_], BF16)
    negc = b.sb("a3_ngc", [1, S_], BF16)
    rowarg = b.sb("a3_ra", [8, S_], F32)
    cf = b.sb("a3_cf", [8, 128], F32)
    th8 = b.sb("a3_th", [8, 1], F32)
    nth = b.sb("a3_nth", [8, 1], F32)
    thSb = b.sb("a3_thSb", [8, NS], F32)
    masks = b.sb("a3_msk", [128, 4, 512], BF16)
    rfbuf = b.sb("a3_rf", [8, 512], F32)
    dsb = b.sb("a3_dsb", [128, 512], F32)
    tmpf = b.sb("a3_tmp", [128, 512], F32)
    avb = b.sb("a3_avb", [128, 512], F32)
    sqf = b.sb("a3_sqf", [128, 512], F32)
    ebuf = b.sb("a3_e", [128, 512], BF16)
    rstat = b.sb("a3_rs", [1, 6, 512], F32)
    gbc = b.colset("a3_gb", gb_h, 8, "gbc")
    zc0 = b.constcol(0.0)
    ln8c = b.constcol(-math.log(8.0))
    epsc = b.constcol(1e-5)
    ones_cb = b.constrow(1.0, 1, BF16, parts=128)
    onescbk = b.kr(1.0, 1, BF16, 128)
    ones_cf = b.constrow(1.0, 1, F32, parts=128)
    onescfk = b.kr(1.0, 1, F32, 128)
    ones_rb = b.constrow(1.0, 128, BF16)
    onesrbk = b.kr(1.0, 128, BF16)
    ones_rf = b.constrow(1.0, 128, F32)
    onesrfk = b.kr(1.0, 128, F32)

    b.load(lnf[:], lnf_h[:], writes=[("lnf",)], tag="a3_ldln")
    b.load(wqkv[:], wqkv_h[:], writes=[("wqkv",)], eng="gpsimd", tag="a3_ldw")
    b.load(wo[:], wo_h[:], writes=[("wo",)], eng="gpsimd", tag="a3_ldwo")
    b.load(th8[:], th_h[:], writes=[("th8",)], tag="a3_ldth")

    cfI = cf.bitcast(I32)
    Tk.op("gpsimd", lambda g: g.iota(cfI[:, :], pattern=[[1, 128]], base=0,
                                     channel_multiplier=0),
          writes=[("cf",)], tag="a3_iu")
    Tk.op("vector", lambda v: v.tensor_copy(cf[:, :], cfI[:, :]),
          reads=[("cf",)], writes=[("cf",)], tag="a3_iuc")
    Tk.op("vector", lambda v: v.tensor_scalar_add(cf[:, :], cf[:, :], -63.5),
          reads=[("cf",)], writes=[("cf",)], tag="a3_iua")
    Tk.op("vector", lambda v: v.tensor_scalar_mul(cf[:, :], cf[:, :],
                                                  th8[:, 0:1]),
          reads=[("cf",), ("th8",)], writes=[("cf",)], tag="a3_ium")
    Tk.op("scalar", lambda s: s.activation(cf[:, :], cf[:, :], AF.Exp,
                                           bias=ln8c[0:8, 0:1]),
          reads=[("cf",), b.kc(-math.log(8.0))], writes=[("cf",)], tag="a3_iue")
    Tk.op("vector", lambda v: v.tensor_scalar_mul(nth[:, :], th8[:, :], -1.0),
          reads=[("th8",)], writes=[("nth",)], tag="a3_nth")
    dsbI = dsb.bitcast(I32)
    for c in range(NCH):
        Tk.op("gpsimd", lambda g, c=c: g.iota(dsbI[0:8, :], pattern=[[1, 512]],
                                              base=c * 512,
                                              channel_multiplier=0),
              writes=[("dsb",)], tag=f"a3_it{c}")
        Tk.op("vector", lambda v, c=c: v.tensor_copy(
            rowarg[:, c * 512:(c + 1) * 512], dsbI[0:8, :]),
            reads=[("dsb",)], writes=[("ra",)], tag=f"a3_itc{c}")
    Tk.op("vector", lambda v: v.tensor_scalar_add(rowarg[:, :], rowarg[:, :],
                                                  -63.5),
          reads=[("ra",)], writes=[("ra",)], tag="a3_raa")
    Tk.op("vector", lambda v: v.tensor_scalar_mul(rowarg[:, :], rowarg[:, :],
                                                  nth[:, 0:1]),
          reads=[("ra",), ("nth",)], writes=[("ra",)], tag="a3_ram")
    for si in range(NS):
        Tk.op("vector", lambda v, si=si: v.tensor_scalar_mul(
            thSb[:, si:si + 1], th8[:, :], float(si * 128)),
            reads=[("th8",)], writes=[("thSb",)], tag=f"a3_tb{si}")
    tmpI = tmpf.bitcast(I32)
    ebU = ebuf.bitcast(U8)
    for k in range(4):
        Tk.op("gpsimd", lambda g, k=k: g.iota(tmpI[:, :], pattern=[[1, 512]],
                                              base=-128 * k,
                                              channel_multiplier=-1),
              writes=[("tmpf",)], tag=f"a3_mi{k}")
        Tk.op("vector", lambda v: v.tensor_scalar(
            ebU[:, 0:512], tmpI[:, :], 0, None, ALU.is_ge),
            reads=[("tmpf",)], writes=[("ebuf",)], tag=f"a3_mg{k}")
        Tk.op("vector", lambda v, k=k: v.tensor_copy(masks[:, k, :],
                                                     ebU[:, 0:512]),
              reads=[("ebuf",)], writes=[("msk",)], tag=f"a3_mc{k}")

    ddiag = b.sb("a3_dd", [128, 128], F32)
    thbc = b.sb("a3_thb", [128, 8], F32)
    nwuf = sqf[:, 128:256]
    nwuI = sqf.bitcast(I32)[:, 128:256]
    Tk.op("gpsimd", lambda g: g.iota(nwuI[:, :], pattern=[[-1, 128]], base=0,
                                     channel_multiplier=1),
          writes=[("sqf",)], tag="a3_dwi")
    Tk.op("vector", lambda v: v.tensor_copy(nwuf[:, :], nwuI[:, :]),
          reads=[("sqf",)], writes=[("sqf",)], tag="a3_dwc")
    identf8 = b.sb("a3_id8", [8, 8], F32)
    Tk.op("vector", lambda v: v.memset(identf8[:, :], 0.0),
          writes=[("id8",)], tag="a3_id8z")
    idI = identf8.bitcast(I32)
    Tk.op("gpsimd", lambda g: g.iota(idI[:, :], pattern=[[1, 8]], base=0,
                                     channel_multiplier=-1),
          writes=[("id8",)], tag="a3_id8i")
    Tk.op("vector", lambda v: v.tensor_single_scalar(
        identf8[:, :], idI[:, :], 0, ALU.is_equal),
        reads=[("id8",)], writes=[("id8",)], tag="a3_id8e")
    Tk.op("tensor", lambda t: t.transpose(
        b.pb[7][0:1, 0:8], th8[:, :], identf8[:, :]),
        reads=[("th8",), ("id8",)], writes=[("psum", 7)], tag="a3_tht")
    Tk.op("vector", lambda v: v.tensor_copy(rstat[0:1, 0, 0:8],
                                            b.pb[7][0:1, 0:8]),
          reads=[("psum", 7)], writes=[("rst",)], tag="a3_thtc")
    Tk.op("tensor", lambda t: t.matmul(
        b.pb[7][:, 0:8], ones_rf[0:1, :], rstat[0:1, 0, 0:8],
        start=True, stop=True),
        reads=[("rst",), onesrfk], writes=[("psum", 7)], tag="a3_tb2")
    Tk.op("vector", lambda v: v.tensor_copy(thbc[:, :], b.pb[7][:, 0:8]),
          reads=[("psum", 7)], writes=[("thbc",)], tag="a3_tb2c")
    for j in range(8):
        Tk.op("vector", lambda v, j=j: v.tensor_scalar(
            tmpf[:, 0:128], nwuf[:, :], thbc[:, j:j + 1], 0.0,
            ALU.mult, ALU.min),
            reads=[("sqf",), ("thbc",)], writes=[("tmpf",)], tag=f"a3_da{j}")
        Tk.op("scalar", lambda s: s.activation(
            sqf[:, 0:128], tmpf[:, 0:128], AF.Exp),
            reads=[("tmpf",)], writes=[("sqf",)], tag=f"a3_de{j}")
        if j == 0:
            Tk.op("vector", lambda v: v.tensor_copy(ddiag[:, :], sqf[:, 0:128]),
                  reads=[("sqf",)], writes=[("dd",)], tag=f"a3_dc{j}")
        else:
            Tk.op("vector", lambda v: v.tensor_tensor(
                ddiag[:, :], ddiag[:, :], sqf[:, 0:128], ALU.add),
                reads=[("dd",), ("sqf",)], writes=[("dd",)], tag=f"a3_dc{j}")
    Tk.op("vector", lambda v: v.tensor_scalar_mul(ddiag[:, :], ddiag[:, :],
                                                  0.125),
          reads=[("dd",)], writes=[("dd",)], tag="a3_ds")
    Tk.op("vector", lambda v: v.memset(dsb[:, :], 0.0),
          writes=[("dsb",)], tag="a3_dz")

    qk, kk2, vk = ("a3q",), ("a3k",), ("a3v",)
    for h in range(4):
        tg = f"a3h{h}_"
        for ti, (buf, bkey, scl) in enumerate([(qT, qk, QS), (kT, kk2, 1.0)]):
            for c in range(NCH):
                for d in range(ND):
                    Tk.op("tensor", lambda t, ti=ti, c=c, d=d, h=h: t.matmul(
                        b.pb[ti][:, 0:512], wqkv[:, d, ti, h, :],
                        lnf[:, d, c * 512:(c + 1) * 512],
                        start=(d == 0), stop=(d == ND - 1)),
                        reads=[("wqkv",), ("lnf",)], writes=[("psum", ti)],
                        tag=tg + f"p{ti}_{c}")
                Tk.op("scalar", lambda s, ti=ti, c=c, buf=buf, scl=scl:
                      s.activation(buf[:, c * 512:(c + 1) * 512],
                                   b.pb[ti][:, 0:512], AF.Identity, scale=scl),
                      reads=[("psum", ti)], writes=[bkey], tag=tg + f"pe{ti}_{c}")
        for st in range(NS):
            for d in range(ND):
                Tk.op("tensor", lambda t, st=st, d=d, h=h: t.matmul(
                    b.pb[2][:, 0:128], lnf[:, d, st * 128:(st + 1) * 128],
                    wqkv[:, d, 2, h, :], start=(d == 0), stop=(d == ND - 1)),
                    reads=[("wqkv",), ("lnf",)], writes=[("psum", 2)],
                    tag=tg + f"pv{st}")
            Tk.op("scalar", lambda s, st=st: s.activation(
                vN3[:, st, :], b.pb[2][:, 0:128], AF.Identity),
                reads=[("psum", 2)], writes=[vk], tag=tg + f"pve{st}")
        emit_qk_bound(b, tg + "qb", qT, kT, [qk], [kk2], ebuf, TQ=S_, TS=S_,
                      KP=128, out_row=negc, out_key=("ngc",), tkey=("ebuf",))
        for ti in range(NCH):
            tc = slice(ti * 512, (ti + 1) * 512)
            Tk.op("tensor", lambda t, tc=tc: t.matmul(
                b.pb[4][:, 0:512], ones_rb[0:1, :], negc[0:1, tc],
                start=True, stop=True),
                reads=[("ngc",), onesrbk], writes=[("psum", 4)],
                tag=tg + f"nb{ti}")
            si_last = min(NS - 1, ti * 4 + 3)
            for si in range(si_last + 1):
                kdiag = si - 4 * ti
                w0 = 0 if kdiag < 0 else (kdiag + 1) * 128
                if w0 < 512:
                    Tk.op("scalar", lambda s, si=si, ti=ti, w0=w0: s.activation(
                        rfbuf[:, w0:512],
                        rowarg[:, ti * 512 + w0:(ti + 1) * 512], AF.Exp,
                        bias=thSb[:, si:si + 1]),
                        reads=[("ra",), ("thSb",)], writes=[("rf",)],
                        tag=tg + f"rf{ti}_{si}")
                    Tk.op("tensor", lambda t, w0=w0: t.matmul(
                        b.pb[1][:, w0:512], cf[:, :], rfbuf[:, w0:512],
                        start=True, stop=True),
                        reads=[("cf",), ("rf",)], writes=[("psum", 1)],
                        tag=tg + f"dm{ti}_{si}")
                    Tk.op("vector", lambda v, w0=w0: v.tensor_scalar_min(
                        dsb[:, w0:512], b.pb[1][:, w0:512], 1.0),
                        reads=[("psum", 1)], writes=[("dsb",)],
                        tag=tg + f"dc{ti}_{si}")
                if kdiag >= 0:
                    Tk.op("vector", lambda v, kdiag=kdiag: v.tensor_copy(
                        dsb[:, kdiag * 128:(kdiag + 1) * 128], ddiag[:, :]),
                        reads=[("dd",)], writes=[("dsb",)],
                        tag=tg + f"dd{ti}_{si}")
                Tk.op("tensor", lambda t, si=si, tc=tc: t.matmul(
                    b.pb[0][:, 0:512], kT[:, si * 128:(si + 1) * 128],
                    qT[:, tc], start=True, stop=True),
                    reads=[kk2, qk], writes=[("psum", 0)],
                    tag=tg + f"sc{ti}_{si}")
                Tk.op("vector", lambda v: v.tensor_tensor(
                    tmpf[:, :], b.pb[0][:, 0:512], dsb[:, :], ALU.mult),
                    reads=[("psum", 0), ("dsb",)], writes=[("tmpf",)],
                    tag=tg + f"sd{ti}_{si}")
                Tk.op("vector", lambda v: v.tensor_tensor(
                    tmpf[:, :], tmpf[:, :], b.pb[4][:, 0:512], ALU.add),
                    reads=[("tmpf",), ("psum", 4)], writes=[("tmpf",)],
                    tag=tg + f"sn{ti}_{si}")
                Tk.op("scalar", lambda s: s.activation(
                    ebuf[:, :], tmpf[:, :], AF.Exp, bias=zc0[:, 0:1]),
                    reads=[("tmpf",), b.kc(0.0)], writes=[("ebuf",)],
                    tag=tg + f"se{ti}_{si}")
                if kdiag >= 0:
                    Tk.op("vector", lambda v, kdiag=kdiag: v.tensor_tensor(
                        ebuf[:, :], ebuf[:, :], masks[:, kdiag, :], ALU.mult),
                        reads=[("ebuf",), ("msk",)], writes=[("ebuf",)],
                        tag=tg + f"sm{ti}_{si}")
                Tk.op("tensor", lambda t, si=si, sl=(si == si_last): t.matmul(
                    b.pb[3][0:1, 0:512], ones_cb[:, 0:1], ebuf[:, :],
                    start=(si == 0), stop=sl),
                    reads=[("ebuf",), onescbk], writes=[("psum", 3)],
                    tag=tg + f"z{ti}_{si}")
                Tk.op("tensor", lambda t, si=si, sl=(si == si_last): t.matmul(
                    b.pb[2][:, 0:512], vN3[:, si, :], ebuf[:, :],
                    start=(si == 0), stop=sl),
                    reads=[vk, ("ebuf",)], writes=[("psum", 2)],
                    tag=tg + f"av{ti}_{si}")
            Tk.op("scalar", lambda s: s.activation(
                avb[:, :], b.pb[2][:, 0:512], AF.Identity),
                reads=[("psum", 2)], writes=[("avb",)], tag=tg + f"ae{ti}")
            Tk.op("tensor", lambda t: t.matmul(
                b.pb[5][0:1, 0:512], ones_cf[:, 0:1], avb[:, :],
                start=True, stop=True),
                reads=[("avb",), onescfk], writes=[("psum", 5)],
                tag=tg + f"s1{ti}")
            Tk.op("scalar", lambda s: s.activation(
                sqf[:, :], avb[:, :], AF.Square),
                reads=[("avb",)], writes=[("sqf",)], tag=tg + f"sq{ti}")
            Tk.op("tensor", lambda t: t.matmul(
                b.pb[6][0:1, 0:512], ones_cf[:, 0:1], sqf[:, :],
                start=True, stop=True),
                reads=[("sqf",), onescfk], writes=[("psum", 6)],
                tag=tg + f"s2{ti}")
            Tk.op("vector", lambda v: v.reciprocal(
                rstat[:, 0, :], b.pb[3][0:1, 0:512]),
                reads=[("psum", 3)], writes=[("rst",)], tag=tg + f"zi{ti}")
            Tk.op("vector", lambda v: v.tensor_tensor(
                rstat[:, 1, :], b.pb[5][0:1, 0:512], rstat[:, 0, :], ALU.mult),
                reads=[("psum", 5), ("rst",)], writes=[("rst",)],
                tag=tg + f"mu{ti}")
            Tk.op("vector", lambda v: v.tensor_scalar_mul(
                rstat[:, 1, :], rstat[:, 1, :], 1.0 / 128.0),
                reads=[("rst",)], writes=[("rst",)], tag=tg + f"mus{ti}")
            Tk.op("vector", lambda v: v.tensor_tensor(
                rstat[:, 2, :], rstat[:, 0, :], rstat[:, 0, :], ALU.mult),
                reads=[("rst",)], writes=[("rst",)], tag=tg + f"z2{ti}")
            Tk.op("vector", lambda v: v.tensor_tensor(
                rstat[:, 3, :], b.pb[6][0:1, 0:512], rstat[:, 2, :], ALU.mult),
                reads=[("psum", 6), ("rst",)], writes=[("rst",)],
                tag=tg + f"e2{ti}")
            Tk.op("vector", lambda v: v.tensor_scalar_mul(
                rstat[:, 3, :], rstat[:, 3, :], 1.0 / 128.0),
                reads=[("rst",)], writes=[("rst",)], tag=tg + f"e2s{ti}")
            Tk.op("vector", lambda v: v.tensor_tensor(
                rstat[:, 2, :], rstat[:, 1, :], rstat[:, 1, :], ALU.mult),
                reads=[("rst",)], writes=[("rst",)], tag=tg + f"m2{ti}")
            Tk.op("vector", lambda v: v.tensor_tensor(
                rstat[:, 4, :], rstat[:, 3, :], rstat[:, 2, :], ALU.subtract),
                reads=[("rst",)], writes=[("rst",)], tag=tg + f"vr{ti}")
            Tk.op("vector", lambda v: v.tensor_scalar_max(
                rstat[:, 4, :], rstat[:, 4, :], 0.0),
                reads=[("rst",)], writes=[("rst",)], tag=tg + f"vc{ti}")
            Tk.op("scalar", lambda s: s.activation(
                rstat[:, 4, :], rstat[:, 4, :], AF.Ln, bias=epsc[0:1, 0:1]),
                reads=[("rst",), b.kc(1e-5)], writes=[("rst",)],
                tag=tg + f"lg{ti}")
            Tk.op("scalar", lambda s: s.activation(
                rstat[:, 4, :], rstat[:, 4, :], AF.Exp, scale=-0.5),
                reads=[("rst",)], writes=[("rst",)], tag=tg + f"ex{ti}")
            Tk.op("vector", lambda v: v.tensor_tensor(
                rstat[:, 5, :], rstat[:, 0, :], rstat[:, 4, :], ALU.mult),
                reads=[("rst",)], writes=[("rst",)], tag=tg + f"ar{ti}")
            Tk.op("vector", lambda v: v.tensor_tensor(
                rstat[:, 2, :], rstat[:, 1, :], rstat[:, 4, :], ALU.mult),
                reads=[("rst",)], writes=[("rst",)], tag=tg + f"br{ti}")
            Tk.op("tensor", lambda t: t.matmul(
                b.pb[5][:, 0:512], ones_rf[0:1, :], rstat[:, 5, :],
                start=True, stop=True),
                reads=[("rst",), onesrfk], writes=[("psum", 5)],
                tag=tg + f"ba{ti}")
            Tk.op("tensor", lambda t: t.matmul(
                b.pb[6][:, 0:512], ones_rf[0:1, :], rstat[:, 2, :],
                start=True, stop=True),
                reads=[("rst",), onesrfk], writes=[("psum", 6)],
                tag=tg + f"bb{ti}")
            Tk.op("vector", lambda v: v.tensor_tensor(
                tmpf[:, :], avb[:, :], b.pb[5][:, 0:512], ALU.mult),
                reads=[("avb",), ("psum", 5)], writes=[("tmpf",)],
                tag=tg + f"la{ti}")
            Tk.op("vector", lambda v: v.tensor_tensor(
                tmpf[:, :], tmpf[:, :], b.pb[6][:, 0:512], ALU.subtract),
                reads=[("tmpf",), ("psum", 6)], writes=[("tmpf",)],
                tag=tg + f"lb{ti}")
            Tk.op("scalar", lambda s, h=h, ti=ti: s.activation(
                oln[:, h, ti * 512:(ti + 1) * 512], tmpf[:, :], AF.Identity,
                scale=gbc[h][:, 0:1], bias=gbc[4 + h][:, 0:1]),
                reads=[("tmpf",), ("gbc",)], writes=[("oln",)],
                tag=tg + f"gb{ti}")
    for c in range(NCH):
        cs = slice(c * 512, (c + 1) * 512)
        for fo in range(ND):
            for h in range(4):
                Tk.op("tensor", lambda t, fo=fo, h=h, cs=cs: t.matmul(
                    b.pb[fo % 4][:, 0:512], wo[:, h, fo, :], oln[:, h, cs],
                    start=(h == 0), stop=(h == 3)),
                    reads=[("wo",), ("oln",)], writes=[("psum", fo % 4)],
                    tag=f"a3o{c}_{fo}")
            Tk.op("scalar", lambda s, fo=fo: s.activation(
                avb[:, :], b.pb[fo % 4][:, 0:512], AF.Identity),
                reads=[("psum", fo % 4)], writes=[("avb",)],
                tag=f"a3oe{c}_{fo}")
            b.store(po_h[:, fo, cs], avb[:, :],
                    reads=[("avb",)], writes=[("po_w",)],
                    tag=f"a3os{c}_{fo}")
    b.finish()
    return b


def prep_A3(ins):
    Wq = np.asarray(ins["L3_attn_Wq"])
    Wk = np.asarray(ins["L3_attn_Wk"])
    Wv = np.asarray(ins["L3_attn_Wv"])
    Wo = np.asarray(ins["L3_attn_Wo"])
    th = np.asarray(ins["L3_attn_theta"])
    gam = np.asarray(ins["L3_attn_gamma"])
    bet = np.asarray(ins["L3_attn_beta"])
    maps = []
    for c in range(8):
        hg = c % 2
        mh = list(range(hg * 4, hg * 4 + 4))
        wt = np.empty((128, 8, 3, 4, 128), np.float32)
        for ti, W in enumerate([Wq, Wk, Wv]):
            ww = W.reshape(1024, 8, 128)[:, mh, :]
            wt[:, :, ti, :, :] = ww.reshape(8, 128, 4, 128).transpose(1, 0, 2, 3)
        wop = Wo.reshape(8, 128, 8, 128)[mh].transpose(1, 0, 2, 3)
        gb = np.concatenate([gam.reshape(8, 128)[mh], bet.reshape(8, 128)[mh]],
                            axis=0)
        maps.append({
            "wqkv": np.ascontiguousarray(wt),
            "wo": np.ascontiguousarray(wop),
            "th": th.reshape(8, 1).astype(np.float32),
            "gb": np.ascontiguousarray(gb),
        })
    return maps


def build_F3():
    b = Builder("F3")
    Tk = b.T
    TB = 512
    ntb = TL // TB
    lnf_h = b.dram_in("lnf", [128, 8, TL], BF16)
    xl_h = b.dram_in("xl", [128, 8, TL], F32)
    k0_h = b.dram_in("k0", [2, 128, 8, 4096], U8)
    k1_h = b.dram_in("k1", [128, 32, 4096], U8)
    k2_h = b.dram_in("k2", [128, 32, 1024], U8)
    s0_h = b.dram_in("s0", [1, 1], F32)
    s1_h = b.dram_in("s1", [1, 1], F32)
    s2_h = b.dram_in("s2", [1, 1], F32)
    xn_h = b.dram_out("xn", [128, 8, TL], F32)
    f_t = b.dram_tmp("f3f", [128, 8, TL], F32)

    ht = b.sb("f3_h", [128, ND, TB], BF16)
    hng = b.sb("f3_hn", [128, ND, TB], BF16)
    h1T = b.sb("f3_h1", [128, 32, TB], BF16)
    h2T = b.sb("f3_h2", [128, 32, TB], BF16)
    k0b = b.sb("f3_k0", [128, 2, 2, ND, 128], BF16)
    k1s = b.sb("f3_k1", [128, 2, 32, 128], BF16)
    k2s = b.sb("f3_k2", [128, 2, 32, 128], BF16)
    fbuf = b.sb("f3_f", [128, ND, TB], F32)
    srow = b.sb("f3_sr", [1, 4], F32)
    scol = b.sb("f3_sc", [128, 3], F32)
    r1f = b.sb("f3_r1", [1, 2, TB], F32)
    hilo = b.sb("f3_hl", [1, 4, TB], BF16)
    xt = b.shared_xt()
    ones_cb = b.constrow(1.0, 1, BF16, parts=128)
    onescbk = b.kr(1.0, 1, BF16, 128)
    ones_rf = b.constrow(1.0, 128, F32)
    onesrfk = b.kr(1.0, 128, F32)
    ones_rb1 = b.constrow(1.0, 128, BF16)
    onesrb1k = b.kr(1.0, 128, BF16)

    for i, sh in enumerate([s0_h, s1_h, s2_h]):
        b.load(srow[0:1, i:i + 1], sh[:], writes=[("srow",)], tag=f"f3_lds{i}")
    pS, pSk = bcast_row(b, 5, srow[0:1, 0:3], [("srow",)], 3, "f3_sb")
    Tk.op("vector", lambda v: v.tensor_copy(scol[:, :], pS[:, 0:3]),
          reads=[pSk], writes=[("scol",)], tag="f3_sc1")
    Tk.op("vector", lambda v: v.tensor_scalar_mul(scol[:, 0:1], scol[:, 0:1],
                                                  2.0),
          reads=[("scol",)], writes=[("scol",)], tag="f3_s0x")
    Tk.op("vector", lambda v: v.tensor_scalar_mul(scol[:, 1:2], scol[:, 1:2],
                                                  4.0),
          reads=[("scol",)], writes=[("scol",)], tag="f3_s1x")
    Tk.op("vector", lambda v: v.tensor_scalar_mul(scol[:, 2:3], scol[:, 2:3],
                                                  4.0),
          reads=[("scol",)], writes=[("scol",)], tag="f3_s2x")

    def rowsum_hilo(src, srck, nch, base, tag):
        for gc in range(nch):
            Tk.op("tensor", lambda t, gc=gc, sl=(gc == nch - 1): t.matmul(
                b.pb[6][0:1, 0:TB], ones_cb[:, 0:1], src[:, gc, :],
                start=(gc == 0), stop=sl),
                reads=[srck, onescbk], writes=[("psum", 6)],
                tag=tag + f"rs{gc}")
        Tk.op("scalar", lambda s: s.activation(
            r1f[:, 0, :], b.pb[6][0:1, 0:TB], AF.Identity, scale=-0.5),
            reads=[("psum", 6)], writes=[("r1f",)], tag=tag + "rc")
        Tk.op("scalar", lambda s, base=base: s.activation(
            hilo[:, base, :], r1f[:, 0, :], AF.Identity),
            reads=[("r1f",)], writes=[("hilo",)], tag=tag + "hi")
        Tk.op("vector", lambda v, base=base: v.tensor_tensor(
            r1f[:, 1, :], r1f[:, 0, :], hilo[:, base, :], ALU.subtract),
            reads=[("r1f",), ("hilo",)], writes=[("r1f",)], tag=tag + "lo1")
        Tk.op("scalar", lambda s, base=base: s.activation(
            hilo[:, base + 1, :], r1f[:, 1, :], AF.Identity),
            reads=[("r1f",)], writes=[("hilo",)], tag=tag + "lo")

    for tb in range(ntb):
        ts = slice(tb * TB, (tb + 1) * TB)
        b.load(ht[:], lnf_h[:, :, ts], writes=[("f3h",)], tag=f"f3_ldh{tb}")
        for d in range(ND):
            Tk.op("scalar", lambda s, d=d: s.activation(
                hng[:, d, :], ht[:, d, :], AF.Identity, scale=-1.0),
                reads=[("f3h",)], writes=[("f3hn",)], tag=f"f3_ng{tb}_{d}")
        for gc in range(32):
            r = gc % 2
            b.load(k0b[:, r, 0, :, :], k0_h[0, :, :, gc * 128:(gc + 1) * 128],
                   writes=[("k0b", r)], eng="gpsimd", tag=f"f3_lk0t{tb}_{gc}")
            b.load(k0b[:, r, 1, :, :], k0_h[1, :, :, gc * 128:(gc + 1) * 128],
                   writes=[("k0b", r)], eng="gpsimd", tag=f"f3_lk0b{tb}_{gc}")
            pk = ("psum", gc % 2)
            for d in range(ND):
                Tk.op("tensor", lambda t, r=r, d=d, gc=gc: t.matmul(
                    b.pb[gc % 2][:, 0:TB], k0b[:, r, 0, d, :], ht[:, d, :],
                    start=(d == 0), stop=False),
                    reads=[("k0b", r), ("f3h",)], writes=[pk],
                    tag=f"f3_m0t{tb}_{gc}")
            for d in range(ND):
                Tk.op("tensor", lambda t, r=r, d=d, gc=gc: t.matmul(
                    b.pb[gc % 2][:, 0:TB], k0b[:, r, 1, d, :], hng[:, d, :],
                    start=False, stop=(d == ND - 1)),
                    reads=[("k0b", r), ("f3hn",)], writes=[pk],
                    tag=f"f3_m0b{tb}_{gc}")
            Tk.op("scalar", lambda s, gc=gc: s.activation(
                h1T[:, gc, :], b.pb[gc % 2][:, 0:TB], AF.Relu,
                scale=scol[:, 0:1]),
                reads=[pk, ("scol",)], writes=[("f3h1",)],
                tag=f"f3_e0{tb}_{gc}")
        rowsum_hilo(h1T, ("f3h1",), 32, 0, f"f3_r1_{tb}")
        for go in range(32):
            r = go % 2
            b.load(k1s[:, r, :, :], k1_h[:, :, go * 128:(go + 1) * 128],
                   writes=[("k1s", r)], eng="gpsimd", tag=f"f3_lk1{tb}_{go}")
            pk = ("psum", 2 + go % 2)
            for gc in range(32):
                Tk.op("tensor", lambda t, r=r, gc=gc, go=go: t.matmul(
                    b.pb[2 + go % 2][:, 0:TB], k1s[:, r, gc, :], h1T[:, gc, :],
                    start=(gc == 0), stop=False),
                    reads=[("k1s", r), ("f3h1",)], writes=[pk],
                    tag=f"f3_m1{tb}_{go}")
            for j in range(2):
                Tk.op("tensor", lambda t, j=j, go=go: t.matmul(
                    b.pb[2 + go % 2][:, 0:TB], ones_rb1[0:1, :],
                    hilo[:, j, :], start=False, stop=(j == 1)),
                    reads=[("hilo",), onesrb1k], writes=[pk],
                    tag=f"f3_c1{tb}_{go}_{j}")
            Tk.op("scalar", lambda s, go=go: s.activation(
                h2T[:, go, :], b.pb[2 + go % 2][:, 0:TB], AF.Relu,
                scale=scol[:, 1:2]),
                reads=[pk, ("scol",)], writes=[("f3h2",)],
                tag=f"f3_e1{tb}_{go}")
        rowsum_hilo(h2T, ("f3h2",), 32, 2, f"f3_r2_{tb}")
        for fo in range(ND):
            r = fo % 2
            b.load(k2s[:, r, :, :], k2_h[:, :, fo * 128:(fo + 1) * 128],
                   writes=[("k2s", r)], eng="gpsimd", tag=f"f3_lk2{tb}_{fo}")
            pk = ("psum", 4 + fo % 2)
            for gc in range(32):
                Tk.op("tensor", lambda t, r=r, gc=gc, fo=fo: t.matmul(
                    b.pb[4 + fo % 2][:, 0:TB], k2s[:, r, gc, :], h2T[:, gc, :],
                    start=(gc == 0), stop=False),
                    reads=[("k2s", r), ("f3h2",)], writes=[pk],
                    tag=f"f3_m2{tb}_{fo}")
            for j in range(2):
                Tk.op("tensor", lambda t, j=j, fo=fo: t.matmul(
                    b.pb[4 + fo % 2][:, 0:TB], ones_rb1[0:1, :],
                    hilo[:, 2 + j, :], start=False, stop=(j == 1)),
                    reads=[("hilo",), onesrb1k], writes=[pk],
                    tag=f"f3_c2{tb}_{fo}_{j}")
            Tk.op("scalar", lambda s, fo=fo: s.activation(
                fbuf[:, fo, :], b.pb[4 + fo % 2][:, 0:TB], AF.Identity,
                scale=scol[:, 2:3]),
                reads=[pk, ("scol",)], writes=[("f3f", fo)],
                tag=f"f3_e2{tb}_{fo}")
        b.store(f_t[:, :, ts], fbuf[:],
                reads=[("f3f", d) for d in range(ND)], writes=[("ftw",)],
                tag=f"f3_sf{tb}")
    ab2, ab2k = emit_ln(b, "f3ln", ("hbm", f_t, [("ftw",)]), TL)
    CH = 512
    for c in range(TL // CH):
        ts = slice(c * CH, (c + 1) * CH)
        xk = ("xts", c % 2)
        b.load(xt[c % 2][:], xl_h[:, :, ts], writes=[xk], tag=f"f3_ldx{c}")
        b.load(fbuf[:], f_t[:, :, ts], reads=[("ftw",)],
               writes=[("f3f", d) for d in range(ND)], tag=f"f3_ldf{c}")
        pA, pAk, pB, pBk = ab_bcast(b, ab2, ab2k, c, CH, f"f3bc{c}")
        for d in range(ND):
            hk = ("f3f", d)
            Tk.op("vector", lambda v, d=d, pA=pA: v.tensor_tensor(
                fbuf[:, d, :], fbuf[:, d, :], pA, ALU.mult),
                reads=[("f3f", d), pAk], writes=[hk], tag=f"f3x1_{c}_{d}")
            Tk.op("vector", lambda v, d=d, pB=pB: v.tensor_tensor(
                fbuf[:, d, :], fbuf[:, d, :], pB, ALU.subtract),
                reads=[hk, pBk], writes=[hk], tag=f"f3x2_{c}_{d}")
            Tk.op("vector", lambda v, c=c, d=d: v.tensor_tensor(
                fbuf[:, d, :], fbuf[:, d, :], xt[c % 2][:, d, :], ALU.add),
                reads=[hk, xk], writes=[hk], tag=f"f3x3_{c}_{d}")
        b.store(xn_h[:, :, ts], fbuf[:],
                reads=[("f3f", d) for d in range(ND)], writes=[("xnout",)],
                tag=f"f3xs{c}")
    b.finish()
    return b


def prep_F3(ins):
    k0 = np.asarray(ins["L3_ffn_k0"]).astype(np.uint8)
    k1 = np.asarray(ins["L3_ffn_k1"]).astype(np.uint8)
    k2 = np.asarray(ins["L3_ffn_k2"]).astype(np.uint8)
    m = {
        "k0": np.ascontiguousarray(
            k0.reshape(2, 8, 128, 4096).transpose(0, 2, 1, 3)),
        "k1": np.ascontiguousarray(k1.reshape(32, 128, 4096).transpose(1, 0, 2)),
        "k2": np.ascontiguousarray(k2.reshape(32, 128, 1024).transpose(1, 0, 2)),
        "s0": np.asarray(ins["L3_ffn_s0"], np.float32).reshape(1, 1),
        "s1": np.asarray(ins["L3_ffn_s1"], np.float32).reshape(1, 1),
        "s2": np.asarray(ins["L3_ffn_s2"], np.float32).reshape(1, 1),
    }
    return [dict(m) for _ in range(8)]


_PROGS = {}


def _get_progs():
    if not _PROGS:
        _PROGS["E00"] = build_E00()
        _PROGS["A0"] = build_A0()
        _PROGS["E0"] = build_E0()
        _PROGS["L0F"] = build_L0F()
        _PROGS["A1"] = build_A1()
        _PROGS["E1"] = build_E1()
        _PROGS["A1F"] = build_A1F()
        _PROGS["E1F"] = build_E1F()
        _PROGS["A2"] = build_A2()
        _PROGS["F2"] = build_F2()
        _PROGS["A3"] = build_A3()
        _PROGS["E3"] = build_E1(bias=True)
        _PROGS["F3"] = build_F3()
    return _PROGS


def kernel(**ins):
    P = _get_progs()
    CORES = list(range(8))

    def run(b, maps):
        return run_bass_kernel_spmd(b.nc, maps, core_ids=CORES)

    def cat(parts):
        return [np.ascontiguousarray(np.concatenate(
            [parts[2 * bt], parts[2 * bt + 1]], axis=2)) for bt in range(4)]

    def ts(c):
        return slice((c % 2) * TL, (c % 2 + 1) * TL)

    x0 = np.asarray(ins["L0_attn_x"], np.float32)

    maps = [{"xl": _fm(x0[c // 2, ts(c)])} for c in CORES]
    r = run(P["E00"], maps)
    lnl = [r.results[c]["lnl"] for c in CORES]
    maps = prep_A0(ins)
    lnfull = cat(lnl)
    for c in CORES:
        maps[c]["lnf"] = lnfull[c // 2]
    r = run(P["A0"], maps)
    po = [r.results[c]["po"] for c in CORES]
    maps = prep_E0(ins)
    for c in CORES:
        maps[c]["xl"] = _fm(x0[c // 2, ts(c)])
        maps[c]["p0"] = np.ascontiguousarray(po[2 * (c // 2)][:, :, ts(c)])
        maps[c]["p1"] = np.ascontiguousarray(po[2 * (c // 2) + 1][:, :, ts(c)])
    r = run(P["E0"], maps)
    xh = [r.results[c]["x1"] for c in CORES]
    maps = prep_L0F(ins)
    for c in CORES:
        maps[c] = dict(maps[c])
        maps[c]["xl"] = xh[c]
    r = run(P["L0F"], maps)
    xh = [r.results[c]["x2"] for c in CORES]
    lnl = [r.results[c]["ln2"] for c in CORES]
    maps = prep_A1(ins)
    lnfull = cat(lnl)
    for c in CORES:
        maps[c]["lnf"] = lnfull[c // 2]
    r = run(P["A1"], maps)
    po = [r.results[c]["po"] for c in CORES]
    maps = []
    for c in CORES:
        maps.append({
            "xl": xh[c],
            "p0": np.ascontiguousarray(po[2 * (c // 2)][:, :, ts(c)]),
            "p1": np.ascontiguousarray(po[2 * (c // 2) + 1][:, :, ts(c)]),
        })
    r = run(P["E1"], maps)
    xh = [r.results[c]["x1"] for c in CORES]
    maps = prep_A1F(ins)
    xfull = cat(xh)
    for c in CORES:
        maps[c]["xf"] = xfull[c // 2]
    r = run(P["A1F"], maps)
    po = [r.results[c]["po"] for c in CORES]
    thr = [r.results[c]["thr"] for c in CORES]
    maps = prep_E1F(ins)
    for c in CORES:
        maps[c] = dict(maps[c])
        maps[c]["xl"] = xh[c]
        maps[c]["p0"] = np.ascontiguousarray(po[2 * (c // 2)][:, :, ts(c)])
        maps[c]["p1"] = np.ascontiguousarray(po[2 * (c // 2) + 1][:, :, ts(c)])
        maps[c]["thr"] = thr[2 * (c // 2)]
    r = run(P["E1F"], maps)
    xh = [r.results[c]["xn"] for c in CORES]
    lnl = [r.results[c]["lnn"] for c in CORES]
    maps = prep_A2(ins)
    lnfull = cat(lnl)
    for c in CORES:
        maps[c]["lnf"] = lnfull[c // 2]
    r = run(P["A2"], maps)
    po = [r.results[c]["po"] for c in CORES]
    maps = []
    for c in CORES:
        maps.append({
            "xl": xh[c],
            "p0": np.ascontiguousarray(po[2 * (c // 2)][:, :, ts(c)]),
            "p1": np.ascontiguousarray(po[2 * (c // 2) + 1][:, :, ts(c)]),
        })
    r = run(P["E1"], maps)
    xh = [r.results[c]["x1"] for c in CORES]
    lnl = [r.results[c]["ln1"] for c in CORES]
    maps = prep_F2(ins)
    for c in CORES:
        maps[c] = dict(maps[c])
        maps[c]["lnf"] = lnl[c]
        maps[c]["xl"] = xh[c]
    r = run(P["F2"], maps)
    xh = [r.results[c]["xn"] for c in CORES]
    lnl = [r.results[c]["ln3"] for c in CORES]
    maps = prep_A3(ins)
    lnfull = cat(lnl)
    for c in CORES:
        maps[c]["lnf"] = lnfull[c // 2]
    r = run(P["A3"], maps)
    po = [r.results[c]["po"] for c in CORES]
    bo = np.ascontiguousarray(
        np.asarray(ins["L3_attn_bo"], np.float32).reshape(8, 128))
    maps = []
    for c in CORES:
        maps.append({
            "xl": xh[c],
            "p0": np.ascontiguousarray(po[2 * (c // 2)][:, :, ts(c)]),
            "p1": np.ascontiguousarray(po[2 * (c // 2) + 1][:, :, ts(c)]),
            "bo": bo,
        })
    r = run(P["E3"], maps)
    xh = [r.results[c]["x1"] for c in CORES]
    lnl = [r.results[c]["ln1"] for c in CORES]
    maps = prep_F3(ins)
    for c in CORES:
        maps[c] = dict(maps[c])
        maps[c]["lnf"] = lnl[c]
        maps[c]["xl"] = xh[c]
    r = run(P["F3"], maps)
    out = np.empty((B_, S_, DM), np.float32)
    for c in CORES:
        out[c // 2, ts(c)] = _unfm(np.asarray(r.results[c]["xn"], np.float32))
    return out
